# version 03
# baseline (speedup 1.0000x reference)
import sys
sys.path.insert(0, '/opt/trn_rl_repo')
import numpy as np
import ml_dtypes
import concourse.bass as bass
import concourse.mybir as mybir
from contextlib import ExitStack

F32 = mybir.dt.float32
BF16 = mybir.dt.bfloat16
AF = mybir.ActivationFunctionType
ALU = mybir.AluOpType
AX = mybir.AxisListType

NCORES = 8
B, S, D, H, DH = 2, 4096, 2048, 16, 128
INNER = 4 * D
NT = 1024
NCH = D // 128
HCH = INNER // 128
RC = 1.5 * (2.0 ** 23)
EPS = 1e-5

bf16 = ml_dtypes.bfloat16



def host_prep(inp):
    C = np.ascontiguousarray
    sh = {}
    sh["l0_wqkv"] = C(inp["L0_attn_Wqkv"])
    sh["l0_bproj"] = C(inp["L0_attn_bproj"].reshape(D, 1))
    sh["l0_w1t"] = C(inp["L0_ffn_w1"].T)
    sh["l0_w2t"] = C(inp["L0_ffn_w2"].T)
    sh["l1_gatet"] = C(inp["L1_ffn_w_gate"].T)
    sh["l1_upt"] = C(inp["L1_ffn_w_up"].T)
    sh["l1_downt"] = C(inp["L1_ffn_w_down"].T)
    sh["ones_f"] = np.ones((128, 1), np.float32)
    sh["ones_r"] = np.ones((1, 128), np.float32)
    sh["ones_rb"] = np.ones((1, 128), bf16)
    sh["iden_b"] = np.eye(128).astype(bf16)
    pm = np.zeros((128, 128), np.float32)
    for d_ in range(128):
        pm[d_, (d_ % 16) * 8 + d_ // 16] = 1.0
    sh["perm_b"] = pm.astype(bf16)
    bands = np.zeros((128, 4 * 512), np.float32)
    q = np.arange(512)[None, :]
    k = np.arange(128)[:, None]
    for j in range(4):
        bands[:, j * 512:(j + 1) * 512] = (q >= (128 * j + k)).astype(np.float32)
    sh["bands"] = bands.astype(bf16)

    percore = []
    x0 = np.asarray(inp["L0_attn_x"])
    wqkv = np.asarray(inp["L0_attn_Wqkv"])
    bqkv = np.asarray(inp["L0_attn_bqkv"])
    wproj = np.asarray(inp["L0_attn_Wproj"])
    wq, wk, wv, wo = (np.asarray(inp["L1_attn_W" + n]) for n in "qkvo")
    w1t, w2t = sh["l0_w1t"], sh["l0_w2t"]
    gt, ut, dt_ = sh["l1_gatet"], sh["l1_upt"], sh["l1_downt"]
    for c in range(NCORES):
        s, r = c // 4, c % 4
        pc = {}
        pc["x0"] = C(x0[s, 1024 * r:1024 * (r + 1), :].T)
        cols = slice(512 * r, 512 * (r + 1))
        pc["l0_wqkv_sl"] = C(np.concatenate(
            [wqkv[:, cols], wqkv[:, 2048:][:, cols], wqkv[:, 4096:][:, cols]], axis=1))
        pc["l0_bqkv_sl"] = C(np.concatenate(
            [bqkv[cols], bqkv[2048:][cols], bqkv[4096:][cols]]).reshape(1536, 1))
        pc["l0_wproj_sl"] = C(wproj[512 * r:512 * (r + 1), :])
        pc["l1_wq_sl"] = C(wq[:, cols])
        pc["l1_wk_sl"] = C(wk[:, cols])
        pc["l1_wv_sl"] = C(wv[:, cols])
        pc["l1_wo_sl"] = C(wo[512 * r:512 * (r + 1), :])
        pc["sh_l0w1"] = C(w1t[256 * c:256 * (c + 1)])
        pc["sh_l0w2"] = C(w2t[1024 * c:1024 * (c + 1)])
        pc["sh_l1g"] = C(gt[256 * c:256 * (c + 1)])
        pc["sh_l1u"] = C(ut[256 * c:256 * (c + 1)])
        pc["sh_l1d"] = C(dt_[1024 * c:1024 * (c + 1)])
        percore.append(pc)
    return sh, percore



class _CntView:
    def __init__(self, env):
        self._env = env

    def __getitem__(self, k):
        return self._env.gcnt.get(k, 0) - self._env.base.get(k, 0)


class PDma:
    def __init__(self, env, name, nslots=2):
        self.env = env
        self.names = [f"{name}{j}" for j in range(nslots)]
        for nm in self.names:
            env.sem_h(nm)
        self.base = [env.gcnt.get(nm, 0) for nm in self.names]
        self.n = nslots

    def inc(self, instr, i):
        nm = self.names[i % self.n]
        self.env.gcnt[nm] = self.env.gcnt.get(nm, 0) + 16
        instr.then_inc(self.env._sems[nm], 16)

    def wait(self, eng, i):
        j = i % self.n
        eng.wait_ge(self.env._sems[self.names[j]],
                    self.base[j] + 16 * (i // self.n + 1))


class Env:
    def __init__(self, debug=False):
        self.nc = bass.Bass("TRN2", target_bir_lowering=False, debug=debug)
        self.st = ExitStack()
        self.gcnt = {}
        self.base = {}
        self.cnt = _CntView(self)
        self._sems = {}
        self._stage_bufs = {}
        self._cursor = 0
        self._maxcur = 0
        self._virts = []
        self.blk = None
        self.PS = None
        self._barn = 0

    def make(self):
        self.PS = self.st.enter_context(
            self.nc.psum_tensor("PS", [128, 4096], F32))
        for kb in (207, 204, 200, 196, 192):
            try:
                self._arena = self.st.enter_context(
                    self.nc.sbuf_tensor(f"arena{kb}", [128, kb * 1024],
                                        mybir.dt.uint8))
                break
            except Exception:
                continue
        self._asize = kb * 1024
        self._aoff = self.nc.lookup_mloc(self._arena).addr
        self.blk = self.st.enter_context(self.nc.Block())
        self.sem_h("bar")

    def PB(self, i):
        return self.PS[:, 512 * i:512 * (i + 1)]

    def sem_h(self, name):
        if name not in self._sems:
            self._sems[name] = self.st.enter_context(self.nc.semaphore(name))
        return self._sems[name]

    def I(self, instr, s, n=1):
        self.gcnt[s] = self.gcnt.get(s, 0) + n
        instr.then_inc(self.sem_h(s), n)
        return self.gcnt[s]

    def W(self, eng, s, v=None):
        g = self.gcnt.get(s, 0) if v is None else self.base.get(s, 0) + v
        eng.wait_ge(self.sem_h(s), g)

    def pdma(self, name, nslots=2):
        return PDma(self, name, nslots)

    def open_stage(self, tag):
        self._stage_bufs = {}
        self._cursor = 0
        self._tag = tag
        self.base = dict(self.gcnt)

    def buf(self, name, shape, dt=F32):
        if name in self._stage_bufs:
            return self._stage_bufs[name]
        per = int(np.prod(shape[1:])) * mybir.dt.size(dt)
        addr = self._cursor
        self._cursor = (addr + per + 31) // 32 * 32
        self._maxcur = max(self._maxcur, self._cursor)
        assert self._cursor <= self._asize, \
            f"arena overflow {self._tag}/{name}: {self._cursor}>{self._asize}"
        h = self.nc.virtual_sbuf_tensor(f"{self._tag}_{name}", shape, dt)
        self.nc._tensor(h.name, h.shape, h.dtype, type="SB")
        mloc = self.nc.lookup_mloc(h)
        mloc.allocated = True
        mloc.addr = self._aoff + addr
        self._stage_bufs[name] = h
        return h

    def barrier(self):
        self._barn += 1
        blk, bar = self.blk, self._sems["bar"]
        for dec in (blk.sync, blk.tensor, blk.vector, blk.scalar, blk.gpsimd):
            def _inc(e):
                e.nop().then_inc(bar, 1)
            dec(_inc)
        tgt = 5 * self._barn
        for dec in (blk.sync, blk.tensor, blk.vector, blk.scalar, blk.gpsimd):
            def _wt(e):
                e.wait_ge(bar, tgt)
            dec(_wt)

    def close(self):
        self.st.close()


G4 = [[0, 1, 2, 3], [4, 5, 6, 7]]



def emit_A(env, l, xT, wdr, shdr, xn_o, ffn_o, DR):
    nc = env.nc
    NMAT = 2 if l == 0 else 3
    SZW = D * INNER

    ones_f = DR["ones_f"]
    ones_r = DR["ones_r"]
    iden_f = DR["iden_f"]
    arin = DR["arin"]
    arout = DR["arout"]
    hscr = DR["hscr"]
    t2scr = DR["t2scr"]
    xnf = DR["xnf"]
    sxd = env.buf("sxd", [128, 2 * NT])
    sxnf = env.buf("sxnf", [128, 2 * NT])
    smx = env.buf("smx", [128, NT])
    sxn = env.buf("sxn", [128, NCH * NT], BF16)
    ssq = env.buf("ssq", [128, 4 * NT])
    srow = env.buf("srow", [1, 6 * NT])
    sbc = env.buf("sbc", [128, 2 * NT])
    smt = env.buf("smt", [128, NT])
    sacc = env.buf("sacc", [128, NT]) if l == 0 else None
    stok = env.buf("stok", [128, 8])
    sone_f = env.buf("sone_f", [128, 1])
    sone_r = env.buf("sone_r", [1, 128])
    siden = env.buf("siden", [128, 128])
    swsh = env.buf("swsh", [128, 2 * 1024])
    scol = env.buf("scol", [128, 96])
    srsc = env.buf("srsc", [1, 64])
    swsc = env.buf("swsc", [128, 2 * NMAT])
    swst = env.buf("swst", [128, 2 * 512])
    swq = env.buf("swq", [128, 2 * 512], BF16)
    swqs = env.buf("swqs", [128, 512])
    shs = env.buf("shs", [128, 8 * 512])
    sh2 = env.buf("sh2", [128, 2 * NT])
    st2r = env.buf("st2r", [128, 2 * NT], BF16)
    sti2 = env.buf("sti2", [128, 2 * NT], BF16)
    sgt = env.buf("sgt", [128, 8 * 512]) if l == 1 else None
    sfo = env.buf("sfo", [128, 8 * 512])

    psA = env.PS[:, 0:1024]
    psB = env.PS[:, 1024:2048]
    psC = env.PS[:, 2048:3072]
    psD = env.PS[:, 3072:3584]
    psE = env.PS[:, 3584:4096]
    PB8 = [psA[:, 0:512], psA[:, 512:1024], psB[:, 0:512], psB[:, 512:1024],
           psC[:, 0:512], psC[:, 512:1024], psD[:, :], psE[:, :]]

    RO = {"mean": 0, "a": 1 * NT, "b": 2 * NT, "tmp": 3 * NT, "tmp2": 4 * NT,
          "n2": 5 * NT, "rms": 0, "maxa": 1 * NT, "clp": 2 * NT, "g": 5 * NT,
          "d1": 0}

    def row(k, lo=0, hi=NT):
        return srow[0:1, RO[k] + lo: RO[k] + hi]

    BO = {"a": 0, "b": NT, "g": 0, "d": NT}

    px1 = env.pdma("px1")
    px2 = env.pdma("px2")
    pwl = env.pdma("pwl")
    psh = env.pdma("psh")
    phr = env.pdma("phr")
    phr2 = env.pdma("phr2")
    ptw = env.pdma("ptw")
    pstf = env.pdma("pstf", 4)
    pxr = env.pdma("pxr")

    blk = env.blk
    I, W, cnt = env.I, env.W, env.cnt

    @blk.sync
    def _(sync):
        I(sync.dma_start(out=sone_f[:], in_=ones_f[:, :]), "s_k", 16)
        I(sync.dma_start(out=sone_r[:], in_=ones_r[:, :]), "s_k", 16)
        I(sync.dma_start(out=siden[:], in_=iden_f[:, :]), "s_k", 16)
        for j in range(NCH):
            if j >= 2:
                W(sync, "s_ma", j - 1)
            px1.inc(sync.dma_start(out=sxd[:, (j % 2) * NT:(j % 2 + 1) * NT],
                                   in_=xT[128 * j:128 * (j + 1), :]), j)
        for j in range(NCH):
            if j < 2:
                W(sync, "s_ma", 15 + j)
            else:
                W(sync, "s_vx", j - 1)
            px2.inc(sync.dma_start(out=sxd[:, (j % 2) * NT:(j % 2 + 1) * NT],
                                   in_=xT[128 * j:128 * (j + 1), :]), j)


    wtiles = []
    if l == 0:
        for hb in range(16):
            for ic in range(16):
                wtiles.append((0, ic, hb))
        for ob in range(4):
            for ic in range(64):
                wtiles.append((1, ic, ob))
    else:
        for hb in range(16):
            for ic in range(16):
                wtiles.append((0, ic, hb))
            for ic in range(16):
                wtiles.append((1, ic, hb))
        for ob in range(4):
            for ic in range(64):
                wtiles.append((2, ic, ob))
    NWT = len(wtiles)

    @blk.sync
    def _(sync):
        for n, (m, ic, cb) in enumerate(wtiles):
            if n >= 2:
                W(sync, "s_wq", n - 1)
            pwl.inc(sync.dma_start(
                out=swst[:, (n % 2) * 512:(n % 2 + 1) * 512],
                in_=wdr[m][128 * ic:128 * (ic + 1), 512 * cb:512 * (cb + 1)]), n)


    shard_tiles = []
    for m in range(NMAT):
        rsh, csh = shdr[m].shape
        shard_tiles += [(m, ri, ci) for ri in range(rsh // 128)
                        for ci in range(csh // 1024)]
    NSH = len(shard_tiles)
    tiles_per_mat = [sum(1 for t in shard_tiles if t[0] == mm) for mm in range(NMAT)]

    @blk.gpsimd
    def _(gpsimd):
        for j in range(NCH):
            W(gpsimd, "s_vx", j + 1)
            ssem = "s_stxa" if (j % 2) == 0 else "s_stxb"
            I(gpsimd.dma_start(out=xn_o[128 * j:128 * (j + 1), :],
                               in_=sxnf[:, (j % 2) * NT:(j % 2 + 1) * NT]), ssem, 16)
            I(gpsimd.dma_start(out=xnf[128 * j:128 * (j + 1), :],
                               in_=sxnf[:, (j % 2) * NT:(j % 2 + 1) * NT]), ssem, 16)
        pass


    @blk.scalar
    def _(scalar):
        for j in range(NCH):
            px1.wait(scalar, j)
            if j >= 2:
                W(scalar, "s_ma", j - 1)
            I(scalar.activation(ssq[:, (j % 2) * NT:(j % 2 + 1) * NT],
                                sxd[:, (j % 2) * NT:(j % 2 + 1) * NT], AF.Square),
              "s_sq", 1)

    @blk.tensor
    def _(tensor):
        W(tensor, "s_k", 48)
        for j in range(NCH):
            px1.wait(tensor, j)
            for hf in range(2):
                tensor.matmul(psA[0:1, hf * 512:(hf + 1) * 512], sone_f[:],
                              sxd[:, (j % 2) * NT + hf * 512:(j % 2) * NT + (hf + 1) * 512],
                              start=(j == 0), stop=(j == NCH - 1))
            W(tensor, "s_sq", j + 1)
            for hf in range(2):
                mm = tensor.matmul(psB[0:1, hf * 512:(hf + 1) * 512], sone_f[:],
                                   ssq[:, (j % 2) * NT + hf * 512:(j % 2) * NT + (hf + 1) * 512],
                                   start=(j == 0), stop=(j == NCH - 1))
            I(mm, "s_ma", 1)

    @blk.vector
    def _(vector):
        W(vector, "s_ma", NCH)
        vector.tensor_scalar(row("mean"), psA[0:1, :], 1.0 / D, 0.0, ALU.mult, ALU.add)
        vector.drain()
        vector.tensor_tensor(row("tmp"), row("mean"), row("mean"), ALU.mult)
        vector.drain()
        vector.scalar_tensor_tensor(row("tmp"), psB[0:1, :], 1.0 / D, row("tmp"),
                                    ALU.mult, ALU.subtract)
        vector.drain()
        I(vector.tensor_scalar(row("tmp"), row("tmp"), 1e-5, 0.0, ALU.add, ALU.add),
          "s_v", 1)

    @blk.scalar
    def _(scalar):
        W(scalar, "s_v", 1)
        I(scalar.activation(row("tmp2"), row("tmp"), AF.Sqrt), "s_sc", 1)

    @blk.vector
    def _(vector):
        W(vector, "s_sc", 1)
        vector.reciprocal(row("a"), row("tmp2"))
        vector.drain()
        vector.tensor_tensor(row("b"), row("mean"), row("a"), ALU.mult)
        vector.drain()
        I(vector.tensor_scalar(row("b"), row("b"), -1.0, 0.0, ALU.mult, ALU.add),
          "s_v", 1)

    @blk.tensor
    def _(tensor):
        W(tensor, "s_v", 2)
        for hf in range(2):
            tensor.matmul(psC[:, hf * 512:(hf + 1) * 512], sone_r[:],
                          row("a", hf * 512, (hf + 1) * 512), start=True, stop=True)
        for hf in range(2):
            mm = tensor.matmul(psB[:, hf * 512:(hf + 1) * 512], sone_r[:],
                               row("b", hf * 512, (hf + 1) * 512), start=True, stop=True)
        I(mm, "s_mm", 1)

    @blk.vector
    def _(vector):
        W(vector, "s_mm", 1)
        vector.tensor_copy(sbc[:, BO["a"]:BO["a"] + NT], psC[:, :])
        vector.drain()
        I(vector.tensor_copy(sbc[:, BO["b"]:BO["b"] + NT], psB[:, :]), "s_v", 1)

    MB0x = cnt["s_mb"]

    @blk.vector
    def _(vector):
        W(vector, "s_v", 3)
        for j in range(NCH):
            px2.wait(vector, j)
            if j >= 2:
                W(vector, "s_stxa" if (j % 2) == 0 else "s_stxb", 32 * ((j - 2) // 2 + 1))
                W(vector, "s_sq2", j - 1)
            vector.tensor_tensor(smt[:, :], sxd[:, (j % 2) * NT:(j % 2 + 1) * NT],
                                 sbc[:, BO["a"]:BO["a"] + NT], ALU.mult)
            vector.drain()
            I(vector.tensor_tensor(sxnf[:, (j % 2) * NT:(j % 2 + 1) * NT], smt[:, :],
                                   sbc[:, BO["b"]:BO["b"] + NT], ALU.add), "s_vx", 1)
            vector.drain()
            if j >= 1:
                W(vector, "s_sq2", j)
                if j == 1:
                    vector.tensor_copy(smx[:, :], ssq[:, 0:NT])
                else:
                    vector.tensor_tensor(smx[:, :], smx[:, :],
                                         ssq[:, ((j - 1) % 2) * NT:((j - 1) % 2 + 1) * NT],
                                         ALU.max)
                vector.drain()
                I(vector.nop(), "s_vm", 1)
        W(vector, "s_sq2", NCH)
        vector.tensor_tensor(smx[:, :], smx[:, :],
                             ssq[:, ((NCH - 1) % 2) * NT:((NCH - 1) % 2 + 1) * NT],
                             ALU.max)
        vector.drain()
        I(vector.nop(), "s_v", 1)

    @blk.scalar
    def _(scalar):
        for j in range(NCH):
            W(scalar, "s_vx", j + 1)
            if j >= 2:
                if l == 0:
                    W(scalar, "s_mb", MB0x + j - 1)
                W(scalar, "s_vm", j - 1)
            I(scalar.activation(ssq[:, (j % 2) * NT:(j % 2 + 1) * NT],
                                sxnf[:, (j % 2) * NT:(j % 2 + 1) * NT], AF.Square),
              "s_sq2", 1)

    if l == 0:
        @blk.tensor
        def _(tensor):
            for j in range(NCH):
                W(tensor, "s_sq2", j + 1)
                for hf in range(2):
                    mm = tensor.matmul(psA[0:1, hf * 512:(hf + 1) * 512], sone_f[:],
                                       ssq[:, (j % 2) * NT + hf * 512:(j % 2) * NT + (hf + 1) * 512],
                                       start=(j == 0), stop=(j == NCH - 1))
                I(mm, "s_mb", 1)

    @blk.tensor
    def _(tensor):
        W(tensor, "s_v", cnt["s_v"])
        for bb in range(8):
            if bb >= 1:
                W(tensor, "s_vt", bb)
            I(tensor.transpose(psE[:, 0:128], smx[:, bb * 128:(bb + 1) * 128],
                               siden[:]), "s_mt", 1)

    @blk.vector
    def _(vector):
        for bb in range(8):
            W(vector, "s_mt", bb + 1)
            vector.reduce_max(stok[:, bb:bb + 1], psE[:, 0:128], axis=AX.X)
            vector.drain()
            I(vector.nop(), "s_vt", 1)
        I(vector.nop(), "s_v", 1)

    @blk.tensor
    def _(tensor):
        W(tensor, "s_v", cnt["s_v"])
        for bb in range(8):
            if bb >= 1:
                W(tensor, "s_vt", 8 + bb)
            I(tensor.transpose(psD[0:1, 0:128], stok[:, bb:bb + 1], siden[:]), "s_mt", 1)

    @blk.vector
    def _(vector):
        for bb in range(8):
            W(vector, "s_mt", 8 + bb + 1)
            vector.tensor_copy(row("maxa", bb * 128, (bb + 1) * 128), psD[0:1, 0:128])
            vector.drain()
            I(vector.nop(), "s_vt", 1)
        I(vector.nop(), "s_v", 1)
    MAXA1 = cnt["s_v"]

    @blk.scalar
    def _(scalar):
        W(scalar, "s_v", MAXA1)
        I(scalar.activation(row("maxa"), row("maxa"), AF.Sqrt), "s_sc", 1)
    SQMAXA1 = cnt["s_sc"]

    if l == 0:
        @blk.vector
        def _(vector):
            W(vector, "s_mb", MB0x + NCH)
            W(vector, "s_sc", SQMAXA1)
            I(vector.tensor_copy(row("n2"), psA[0:1, :]), "s_v", 1)
            vector.drain()

        @blk.scalar
        def _(scalar):
            W(scalar, "s_v", cnt["s_v"])
            I(scalar.activation(row("tmp2"), row("n2"), AF.Sqrt), "s_sc", 1)

        @blk.vector
        def _(vector):
            W(vector, "s_sc", cnt["s_sc"])
            vector.tensor_scalar(row("tmp2"), row("tmp2"), 1e-12, 0.0, ALU.max, ALU.add)
            vector.drain()
            vector.reciprocal(row("rms"), row("tmp2"))
            vector.drain()
            vector.tensor_scalar(row("rms"), row("rms"), float(np.sqrt(2048.0)), 0.0,
                                 ALU.mult, ALU.add)
            vector.drain()
            vector.tensor_tensor(row("tmp"), row("rms"), row("maxa"), ALU.mult)
            vector.drain()
            vector.tensor_scalar(row("clp"), row("tmp"), 1e-5, 0.0, ALU.max, ALU.add)
            vector.drain()
            vector.reciprocal(row("tmp2"), row("clp"))
            vector.drain()
            vector.tensor_scalar(row("tmp2"), row("tmp2"), 127.0, 0.0, ALU.mult, ALU.add)
            vector.drain()
            I(vector.tensor_tensor(row("g"), row("rms"), row("tmp2"), ALU.mult),
              "s_v", 1)
            vector.drain()
    else:
        @blk.vector
        def _(vector):
            W(vector, "s_sc", SQMAXA1)
            vector.tensor_scalar(row("d1"), row("maxa"), 1.0 / 127.0, 1e-5,
                                 ALU.mult, ALU.add)
            vector.drain()
            I(vector.reciprocal(row("g"), row("d1")), "s_v", 1)
            vector.drain()

    GROW = cnt["s_v"]

    @blk.tensor
    def _(tensor):
        W(tensor, "s_v", GROW)
        for hf in range(2):
            mm = tensor.matmul(psC[:, hf * 512:(hf + 1) * 512], sone_r[:],
                               row("g", hf * 512, (hf + 1) * 512), start=True, stop=True)
        I(mm, "s_mm", 1)

    @blk.vector
    def _(vector):
        W(vector, "s_mm", 2)
        I(vector.tensor_copy(sbc[:, BO["g"]:BO["g"] + NT], psC[:, :]), "s_v", 1)
        vector.drain()
    GBC = cnt["s_v"]

    @blk.gpsimd
    def _(gpsimd):
        for j in range(NCH):
            ssem = "s_stxa" if (j % 2) == 0 else "s_stxb"
            W(gpsimd, ssem, 32 * (j // 2 + 1))
            if j >= 2:
                W(gpsimd, "s_vx", NCH + j - 1)
            pxr.inc(gpsimd.dma_start(out=sh2[:, (j % 2) * NT:(j % 2 + 1) * NT],
                                     in_=xnf[128 * j:128 * (j + 1), :]), j)

    @blk.vector
    def _(vector):
        W(vector, "s_v", GBC)
        for j in range(NCH):
            pxr.wait(vector, j)
            vector.tensor_tensor(smt[:, :], sh2[:, (j % 2) * NT:(j % 2 + 1) * NT],
                                 sbc[:, BO["g"]:BO["g"] + NT], ALU.mult)
            vector.drain()
            I(vector.tensor_scalar(sxn[:, j * NT:(j + 1) * NT], smt[:, :], RC, -RC,
                                   ALU.add, ALU.add), "s_vx", 1)
            vector.drain()
    TI_DONE = cnt["s_vx"]

    @blk.gpsimd
    def _(gpsimd):
        for n, (m, ri, ci) in enumerate(shard_tiles):
            if n >= 2:
                W(gpsimd, "s_pr", n - 1)
            psh.inc(gpsimd.dma_start(
                out=swsh[:, (n % 2) * 1024:(n % 2 + 1) * 1024],
                in_=shdr[m][128 * ri:128 * (ri + 1), 1024 * ci:1024 * (ci + 1)]), n)

    @blk.vector
    def _(vector):
        for n in range(NSH):
            psh.wait(vector, n)
            I(vector.reduce_sum(scol[:, n:n + 1],
                                swsh[:, (n % 2) * 1024:(n % 2 + 1) * 1024],
                                axis=AX.X, apply_absolute_value=True), "s_pr", 1)
            vector.drain()
        base = 0
        for m in range(NMAT):
            vector.reduce_sum(stok[:, m:m + 1], scol[:, base:base + tiles_per_mat[m]],
                              axis=AX.X)
            vector.drain()
            base += tiles_per_mat[m]
        I(vector.nop(), "s_v", 1)
    FOLD1 = cnt["s_v"]

    @blk.tensor
    def _(tensor):
        W(tensor, "s_v", FOLD1)
        for m in range(NMAT):
            if m >= 1:
                W(tensor, "s_vt", 16 + m)
            I(tensor.transpose(psE[0:1, 0:128], stok[:, m:m + 1], siden[:]), "s_mt", 1)

    @blk.vector
    def _(vector):
        for m in range(NMAT):
            W(vector, "s_mt", 16 + m + 1)
            vector.reduce_sum(srsc[0:1, m:m + 1], psE[0:1, 0:128], axis=AX.X)
            vector.drain()
            I(vector.nop(), "s_vt", 1)
        I(vector.nop(), "s_v", 1)
    PARTS = cnt["s_v"]

    @blk.gpsimd
    def _(gpsimd):
        W(gpsimd, "s_v", PARTS)
        I(gpsimd.dma_start(out=arin[0:1, 0:NMAT], in_=srsc[0:1, 0:NMAT]), "s_a", 16)
        W(gpsimd, "s_a", 16)
        I(gpsimd.collective_compute(
            "AllReduce", ALU.add, replica_groups=[list(range(NCORES))],
            ins=[arin.ap().opt()], outs=[arout.ap().opt()]), "s_cc", 1)
        W(gpsimd, "s_cc", 1)
        I(gpsimd.dma_start(out=srsc[0:1, 8:8 + NMAT], in_=arout[0:1, 0:NMAT]),
          "s_ar", 16)

    @blk.vector
    def _(vector):
        W(vector, "s_ar", 16)
        if l == 0:
            vector.tensor_scalar(srsc[0:1, 16:16 + NMAT], srsc[0:1, 8:8 + NMAT],
                                 1.0 / SZW, 1e-5, ALU.mult, ALU.max)
        else:
            vector.tensor_scalar(srsc[0:1, 16:16 + NMAT], srsc[0:1, 8:8 + NMAT],
                                 1.0 / SZW, 1e-5, ALU.mult, ALU.add)
        vector.drain()
        vector.reciprocal(srsc[0:1, 24:24 + NMAT], srsc[0:1, 16:16 + NMAT])
        vector.drain()
        if l == 0:
            vector.tensor_scalar(row("d1"), row("clp"), srsc[0:1, 16:17], 1.0 / 127.0,
                                 ALU.mult, ALU.mult)
            vector.drain()
        I(vector.nop(), "s_v", 1)
    SCROW = cnt["s_v"]

    @blk.tensor
    def _(tensor):
        W(tensor, "s_v", SCROW)
        nb = 0
        for m in range(NMAT):
            tensor.matmul(psD[:, m:m + 1], sone_r[:], srsc[0:1, 24 + m:25 + m],
                          start=(nb == 0), stop=False)
            nb += 1
            mm = tensor.matmul(psD[:, 8 + m:9 + m], sone_r[:], srsc[0:1, 16 + m:17 + m],
                               start=False, stop=(m == NMAT - 1))
            nb += 1
        I(mm, "s_mm", 1)

    @blk.vector
    def _(vector):
        W(vector, "s_mm", 3)
        vector.tensor_copy(swsc[:, 0:NMAT], psD[:, 0:NMAT])
        vector.drain()
        I(vector.tensor_copy(swsc[:, NMAT:2 * NMAT], psD[:, 8:8 + NMAT]), "s_v", 1)
        vector.drain()
    SCBC = cnt["s_v"]

    @blk.tensor
    def _(tensor):
        W(tensor, "s_v", SCBC)
        for hf in range(2):
            mm = tensor.matmul(psA[:, hf * 512:(hf + 1) * 512], sone_r[:],
                               row("d1", hf * 512, (hf + 1) * 512), start=True, stop=True)
        I(mm, "s_mm", 1)

    @blk.vector
    def _(vector):
        W(vector, "s_mm", 4)
        I(vector.tensor_copy(sbc[:, BO["d"]:BO["d"] + NT], psA[:, :]), "s_v", 1)
        vector.drain()
    DBC = cnt["s_v"]

    NB1 = 16 * 16 if l == 0 else 16 * 32
    EV0 = cnt["s_ev"]
    GL0 = cnt["s_gl"]

    def emit_quant(vector, n):
        m = wtiles[n][0]
        pwl.wait(vector, n)
        if n >= 2:
            W(vector, "s_up", n - 1)
        sl = (n % 2) * 512
        I(vector.tensor_scalar(swqs[:, :], swst[:, sl:sl + 512],
                               swsc[:, m:m + 1], -1.0, ALU.mult, ALU.max), "s_wq", 1)
        vector.drain()
        vector.tensor_scalar(swqs[:, :], swqs[:, :], 1.0, RC, ALU.min, ALU.add)
        vector.drain()
        I(vector.tensor_scalar(swq[:, sl:sl + 512], swqs[:, :], -RC, 0.0,
                               ALU.add, ALU.add), "s_wu", 1)
        vector.drain()

    if l == 0:
        def _stats_piece(vector, p):
            hb, b8 = p // 8, p % 8
            t4, hf = b8 % 4, b8 // 4
            W(vector, "s_hsq", p + 1)
            q4 = (p % 8) * 512
            if hb == 0 and t4 == 0:
                I(vector.tensor_copy(smt[:, hf * 512:(hf + 1) * 512],
                                     ssq[:, q4:q4 + 512]), "s_mh", 1)
            else:
                I(vector.tensor_tensor(smt[:, hf * 512:(hf + 1) * 512],
                                       smt[:, hf * 512:(hf + 1) * 512],
                                       ssq[:, q4:q4 + 512], ALU.max), "s_mh", 1)
            vector.drain()
            if hb == 0 and t4 == 0:
                I(vector.tensor_copy(sacc[:, hf * 512:(hf + 1) * 512],
                                     ssq[:, q4:q4 + 512]), "s_va", 1)
            else:
                I(vector.tensor_tensor(sacc[:, hf * 512:(hf + 1) * 512],
                                       sacc[:, hf * 512:(hf + 1) * 512],
                                       ssq[:, q4:q4 + 512], ALU.add), "s_va", 1)
            vector.drain()

        def emit_evac(vector, hb):
            W(vector, "s_up", (hb + 1) * 16)
            for b8 in range(8):
                t4, hf = b8 % 4, b8 // 4
                if hb >= 1:
                    W(vector, "s_gl", GL0 + (hb - 1) * 8 + b8 + 1)
                I(vector.tensor_tensor(sfo[:, b8 * 512:(b8 + 1) * 512], PB8[b8],
                                       sbc[:, BO["d"] + hf * 512:BO["d"] + (hf + 1) * 512],
                                       ALU.mult), "s_ev", 1)
                vector.drain()
            if hb >= 1:
                for b8 in range(8):
                    _stats_piece(vector, (hb - 1) * 8 + b8)

        @blk.vector
        def _(vector):
            for hb in range(16):
                if hb >= 1:
                    emit_evac(vector, hb - 1)
                for n in range(hb * 16, hb * 16 + 16):
                    emit_quant(vector, n)
            emit_evac(vector, 15)
            for b8 in range(8):
                _stats_piece(vector, 15 * 8 + b8)

        @blk.scalar
        def _(scalar):
            for hb in range(16):
                for b8 in range(8):
                    p = hb * 8 + b8
                    t4, hf = b8 % 4, b8 // 4
                    W(scalar, "s_ev", EV0 + p + 1)
                    scalar.activation(sfo[:, b8 * 512:(b8 + 1) * 512],
                                      sfo[:, b8 * 512:(b8 + 1) * 512], AF.Gelu)
                    scalar.drain()
                    sl8 = t4 * 2 + hf
                    if hb >= 1:
                        W(scalar, "s_sth", 16 * hb)
                    I(scalar.activation(shs[:, sl8 * 512:(sl8 + 1) * 512],
                                        sfo[:, b8 * 512:(b8 + 1) * 512], AF.Gelu),
                      "s_gl", 1)
                    scalar.drain()
                    if p >= 8:
                        W(scalar, "s_va", p - 7)
                        W(scalar, "s_mh", p - 7)
                    I(scalar.activation(ssq[:, (p % 8) * 512:(p % 8) * 512 + 512],
                                        shs[:, sl8 * 512:(sl8 + 1) * 512], AF.Square),
                      "s_hsq", 1)
                    scalar.drain()
    else:
        def _tree_piece(vector, p):
            hb, b8 = p // 8, p % 8
            t4, hf = b8 % 4, b8 // 4
            W(vector, "s_hsq", p + 1)
            q4 = (p % 8) * 512
            if hb == 0 and t4 == 0:
                I(vector.tensor_copy(smt[:, hf * 512:(hf + 1) * 512],
                                     ssq[:, q4:q4 + 512]), "s_mh", 1)
            else:
                I(vector.tensor_tensor(smt[:, hf * 512:(hf + 1) * 512],
                                       smt[:, hf * 512:(hf + 1) * 512],
                                       ssq[:, q4:q4 + 512], ALU.max), "s_mh", 1)
            vector.drain()

        def emit_evac_gate(vector, hb):
            W(vector, "s_up", hb * 32 + 16)
            for b8 in range(8):
                if hb >= 1:
                    W(vector, "s_gl", GL0 + (hb - 1) * 8 + b8 + 1)
                I(vector.scalar_tensor_tensor(
                    sfo[:, b8 * 512:(b8 + 1) * 512], PB8[b8],
                    swsc[:, NMAT + 0:NMAT + 1],
                    sbc[:, BO["d"] + (b8 // 4) * 512:BO["d"] + (b8 // 4 + 1) * 512],
                    ALU.mult, ALU.mult), "s_ev", 1)
                vector.drain()

        def emit_evac_up(vector, hb):
            W(vector, "s_up", hb * 32 + 32)
            for b8 in range(8):
                p = hb * 8 + b8
                t4, hf = b8 % 4, b8 // 4
                W(vector, "s_gl", GL0 + hb * 8 + b8 + 1)
                vector.scalar_tensor_tensor(
                    swqs[:, :], PB8[b8], swsc[:, NMAT + 1:NMAT + 2],
                    sbc[:, BO["d"] + hf * 512:BO["d"] + (hf + 1) * 512],
                    ALU.mult, ALU.mult)
                vector.drain()
                I(vector.nop(), "s_ev", 1)
                if hb >= 1:
                    W(vector, "s_sth", 16 * hb)
                if p >= 8:
                    W(vector, "s_hsq", p - 7)
                sl8 = t4 * 2 + hf
                I(vector.tensor_tensor(shs[:, sl8 * 512:(sl8 + 1) * 512],
                                       swqs[:, :], sgt[:, b8 * 512:(b8 + 1) * 512],
                                       ALU.mult), "s_pc", 1)
                vector.drain()
                if p >= 1:
                    _tree_piece(vector, p - 1)

        @blk.vector
        def _(vector):
            for hb in range(16):
                if hb >= 1:
                    emit_evac_up(vector, hb - 1)
                for n in range(hb * 32, hb * 32 + 16):
                    emit_quant(vector, n)
                emit_evac_gate(vector, hb)
                for n in range(hb * 32 + 16, hb * 32 + 32):
                    emit_quant(vector, n)
            emit_evac_up(vector, 15)
            _tree_piece(vector, 127)

        @blk.scalar
        def _(scalar):
            for hb in range(16):
                for b8 in range(8):
                    p = hb * 8 + b8
                    W(scalar, "s_ev", EV0 + hb * 16 + b8 + 1)
                    if hb >= 1:
                        W(scalar, "s_pc", (hb - 1) * 8 + b8 + 1)
                    I(scalar.activation(sgt[:, b8 * 512:(b8 + 1) * 512],
                                        sfo[:, b8 * 512:(b8 + 1) * 512], AF.Silu),
                      "s_gl", 1)
                    scalar.drain()
                    W(scalar, "s_pc", p + 1)
                    if p >= 8:
                        W(scalar, "s_mh", p - 7)
                    sl8 = (b8 % 4) * 2 + b8 // 4
                    I(scalar.activation(ssq[:, (p % 8) * 512:(p % 8) * 512 + 512],
                                        shs[:, sl8 * 512:(sl8 + 1) * 512], AF.Square),
                      "s_hsq", 1)
                    scalar.drain()

    @blk.tensor
    def _(tensor):
        W(tensor, "s_vx", TI_DONE)
        if l == 0:
            for hb in range(16):
                for ic in range(16):
                    n = hb * 16 + ic
                    W(tensor, "s_wu", n + 1)
                    sl = (n % 2) * 512
                    for b8 in range(8):
                        t4, hf = b8 % 4, b8 // 4
                        if ic == 0 and hb >= 1:
                            W(tensor, "s_ev", EV0 + (hb - 1) * 8 + b8 + 1)
                        mm = tensor.matmul(
                            PB8[b8], swq[:, sl + t4 * 128:sl + (t4 + 1) * 128],
                            sxn[:, ic * NT + hf * 512:ic * NT + (hf + 1) * 512],
                            start=(ic == 0), stop=(ic == 15))
                    I(mm, "s_up", 1)
        else:
            for hb in range(16):
                for g_or_u in range(2):
                    for ic in range(16):
                        n = hb * 32 + g_or_u * 16 + ic
                        W(tensor, "s_wu", n + 1)
                        sl = (n % 2) * 512
                        for b8 in range(8):
                            t4, hf = b8 % 4, b8 // 4
                            if ic == 0:
                                prev = hb * 16 if g_or_u == 1 else (hb - 1) * 16 + 8
                                if hb >= 1 or g_or_u == 1:
                                    W(tensor, "s_ev", EV0 + prev + b8 + 1)
                            mm = tensor.matmul(
                                PB8[b8], swq[:, sl + t4 * 128:sl + (t4 + 1) * 128],
                                sxn[:, ic * NT + hf * 512:ic * NT + (hf + 1) * 512],
                                start=(ic == 0), stop=(ic == 15))
                        I(mm, "s_up", 1)

    PC_SEM = "s_gl" if l == 0 else "s_pc"
    PC0 = GL0 if l == 0 else 0

    @blk.gpsimd
    def _(gpsimd):
        for hb in range(16):
            W(gpsimd, PC_SEM, PC0 + (hb + 1) * 8)
            dst = hscr.ap().rearrange("(t p) e -> p t e", p=128)[:, 4 * hb:4 * hb + 4, :]
            srcap = shs.ap().rearrange("p (t e) -> p t e", t=4)
            I(gpsimd.dma_start(out=dst, in_=srcap), "s_sth", 16)

    MT0 = cnt["s_mt"]
    VT0 = cnt["s_vt"]
    TREE_DONE = cnt["s_mh"]

    if l == 0:
        @blk.tensor
        def _(tensor):
            W(tensor, "s_va", cnt["s_va"])
            W(tensor, "s_ev", cnt["s_ev"])
            for hf in range(2):
                mm = tensor.matmul(psA[0:1, hf * 512:(hf + 1) * 512], sone_f[:],
                                   sacc[:, hf * 512:(hf + 1) * 512],
                                   start=True, stop=True)
            I(mm, "s_mb", 1)
        MB1 = cnt["s_mb"]

    @blk.tensor
    def _(tensor):
        W(tensor, "s_mh", TREE_DONE)
        for bb in range(8):
            if bb >= 1:
                W(tensor, "s_vt", VT0 + bb)
            I(tensor.transpose(psE[:, 0:128], smt[:, bb * 128:(bb + 1) * 128],
                               siden[:]), "s_mt", 1)

    @blk.vector
    def _(vector):
        for bb in range(8):
            W(vector, "s_mt", MT0 + bb + 1)
            vector.reduce_max(stok[:, bb:bb + 1], psE[:, 0:128], axis=AX.X)
            vector.drain()
            I(vector.nop(), "s_vt", 1)
        I(vector.nop(), "s_v", 1)

    @blk.tensor
    def _(tensor):
        W(tensor, "s_v", cnt["s_v"])
        for bb in range(8):
            if bb >= 1:
                W(tensor, "s_vt", VT0 + 8 + bb)
            I(tensor.transpose(psD[0:1, 0:128], stok[:, bb:bb + 1], siden[:]), "s_mt", 1)

    @blk.vector
    def _(vector):
        for bb in range(8):
            W(vector, "s_mt", MT0 + 8 + bb + 1)
            vector.tensor_copy(row("maxa", bb * 128, (bb + 1) * 128), psD[0:1, 0:128])
            vector.drain()
            I(vector.nop(), "s_vt", 1)
        I(vector.nop(), "s_v", 1)
    MAXH = cnt["s_v"]

    @blk.scalar
    def _(scalar):
        W(scalar, "s_v", MAXH)
        I(scalar.activation(row("maxa"), row("maxa"), AF.Sqrt), "s_sc", 1)
    SQMAXH = cnt["s_sc"]

    if l == 0:
        @blk.vector
        def _(vector):
            W(vector, "s_mb", MB1)
            I(vector.tensor_copy(row("n2"), psA[0:1, :]), "s_v", 1)
            vector.drain()

        @blk.scalar
        def _(scalar):
            W(scalar, "s_v", cnt["s_v"])
            I(scalar.activation(row("tmp2"), row("n2"), AF.Sqrt), "s_sc", 1)
        SC2 = cnt["s_sc"]

        @blk.vector
        def _(vector):
            W(vector, "s_sc", SC2)
            W(vector, "s_v", MAXH)
            W(vector, "s_sc", SC2)
            vector.tensor_scalar(row("tmp2"), row("tmp2"), 1e-12, 0.0, ALU.max, ALU.add)
            vector.drain()
            vector.reciprocal(row("rms"), row("tmp2"))
            vector.drain()
            vector.tensor_scalar(row("rms"), row("rms"), float(np.sqrt(8192.0)), 0.0,
                                 ALU.mult, ALU.add)
            vector.drain()
            vector.tensor_tensor(row("tmp"), row("rms"), row("maxa"), ALU.mult)
            vector.drain()
            vector.tensor_scalar(row("clp"), row("tmp"), 1e-5, 0.0, ALU.max, ALU.add)
            vector.drain()
            vector.reciprocal(row("tmp2"), row("clp"))
            vector.drain()
            vector.tensor_scalar(row("tmp2"), row("tmp2"), 127.0, 0.0, ALU.mult, ALU.add)
            vector.drain()
            vector.tensor_tensor(row("g"), row("rms"), row("tmp2"), ALU.mult)
            vector.drain()
            I(vector.tensor_scalar(row("d1"), row("clp"), srsc[0:1, 17:18], 1.0 / 127.0,
                                   ALU.mult, ALU.mult), "s_v", 1)
            vector.drain()
    else:
        @blk.vector
        def _(vector):
            W(vector, "s_v", MAXH)
            W(vector, "s_sc", SQMAXH)
            vector.tensor_scalar(row("tmp"), row("maxa"), 1.0 / 127.0, 1e-5,
                                 ALU.mult, ALU.add)
            vector.drain()
            vector.reciprocal(row("g"), row("tmp"))
            vector.drain()
            I(vector.tensor_scalar(row("d1"), row("tmp"), srsc[0:1, 18:19], 0.0,
                                   ALU.mult, ALU.add), "s_v", 1)
            vector.drain()
    G2ROW = cnt["s_v"]

    @blk.tensor
    def _(tensor):
        W(tensor, "s_v", G2ROW)
        for hf in range(2):
            tensor.matmul(psC[:, hf * 512:(hf + 1) * 512], sone_r[:],
                          row("g", hf * 512, (hf + 1) * 512), start=True, stop=True)
        for hf in range(2):
            mm = tensor.matmul(psB[:, hf * 512:(hf + 1) * 512], sone_r[:],
                               row("d1", hf * 512, (hf + 1) * 512), start=True, stop=True)
        I(mm, "s_mm", 1)

    @blk.vector
    def _(vector):
        W(vector, "s_mm", cnt["s_mm"])
        vector.tensor_copy(sbc[:, BO["g"]:BO["g"] + NT], psC[:, :])
        vector.drain()
        I(vector.tensor_copy(sbc[:, BO["d"]:BO["d"] + NT], psB[:, :]), "s_v", 1)
        vector.drain()
    G2BC = cnt["s_v"]

    VX2_pre = cnt["s_vx"]
    U0 = cnt["s_up"]
    NOFF = NB1
    EV2 = cnt["s_ev"]

    @blk.gpsimd
    def _(gpsimd):
        for ic in range(HCH + 2):
            if ic < HCH:
                if ic >= 2:
                    W(gpsimd, "s_vx", VX2_pre + ic - 1)
                W(gpsimd, "s_sth", 16 * (ic // 4 + 1))
                phr.inc(gpsimd.dma_start(out=sh2[:, (ic % 2) * NT:(ic % 2 + 1) * NT],
                                         in_=hscr[128 * ic:128 * (ic + 1), :]), ic)
            if ic >= 2:
                jc = ic - 2
                W(gpsimd, "s_vx", VX2_pre + jc + 1)
                ptw.inc(gpsimd.dma_start(out=t2scr[128 * jc:128 * (jc + 1), :],
                                         in_=sti2[:, (jc % 2) * NT:(jc % 2 + 1) * NT]), jc)
        for ob in range(1, 4):
            for ic in range(HCH):
                m = (ob - 1) * HCH + ic
                ptw.wait(gpsimd, ic)
                if m >= 2:
                    W(gpsimd, "s_up", U0 + HCH + m - 1)
                phr2.inc(gpsimd.dma_start(out=st2r[:, (m % 2) * NT:(m % 2 + 1) * NT],
                                          in_=t2scr[128 * ic:128 * (ic + 1), :]), m)

    def emit_round2(vector, ic):
        phr.wait(vector, ic)
        vector.tensor_tensor(smt[:, :], sh2[:, (ic % 2) * NT:(ic % 2 + 1) * NT],
                             sbc[:, BO["g"]:BO["g"] + NT], ALU.mult)
        vector.drain()
        if ic >= 2:
            W(vector, "s_up", U0 + ic - 1)
            ptw.wait(vector, ic - 2)
        I(vector.tensor_scalar(sti2[:, (ic % 2) * NT:(ic % 2 + 1) * NT],
                               smt[:, :], RC, -RC, ALU.add, ALU.add), "s_vx", 1)
        vector.drain()

    def emit_evac2(vector, ob):
        W(vector, "s_up", U0 + (ob + 1) * HCH)
        for t4 in range(4):
            ot = ob * 4 + t4
            if ob >= 1:
                pstf.wait(vector, (ob - 1) * 4 + t4)
            for hf in range(2):
                b8 = hf * 4 + t4
                I(vector.tensor_tensor(
                    sfo[:, t4 * 1024 + hf * 512:t4 * 1024 + (hf + 1) * 512],
                    PB8[b8],
                    sbc[:, BO["d"] + hf * 512:BO["d"] + (hf + 1) * 512], ALU.mult),
                  "s_ev", 1)
                vector.drain()
            I(vector.nop(), "s_vf", 1)

    @blk.vector
    def _(vector):
        W(vector, "s_v", G2BC)
        for ob in range(4):
            if ob >= 1:
                emit_evac2(vector, ob - 1)
            for ic in range(HCH):
                emit_quant(vector, NOFF + ob * HCH + ic)
                if ob == 0:
                    emit_round2(vector, ic)
        emit_evac2(vector, 3)

    @blk.tensor
    def _(tensor):
        for ob in range(4):
            for ic in range(HCH):
                n = NOFF + ob * HCH + ic
                W(tensor, "s_wu", n + 1)
                if ob == 0:
                    W(tensor, "s_vx", VX2_pre + ic + 1)
                    rhs_base = (ic % 2) * NT
                    rhs_t = sti2
                else:
                    m = (ob - 1) * HCH + ic
                    phr2.wait(tensor, m)
                    rhs_base = (m % 2) * NT
                    rhs_t = st2r
                sl = (n % 2) * 512
                for b8 in range(8):
                    t4, hf = b8 % 4, b8 // 4
                    if ic == 0 and ob >= 1:
                        W(tensor, "s_ev", EV2 + (ob - 1) * 8 + (b8 % 4) * 2 + b8 // 4 + 1)
                    mm = tensor.matmul(
                        PB8[b8], swq[:, sl + t4 * 128:sl + (t4 + 1) * 128],
                        rhs_t[:, rhs_base + hf * 512:rhs_base + (hf + 1) * 512],
                        start=(ic == 0), stop=(ic == HCH - 1))
                I(mm, "s_up", 1)

    @blk.scalar
    def _(scalar):
        for ot in range(16):
            W(scalar, "s_vf", ot + 1)
            pstf.inc(scalar.dma_start(out=ffn_o[128 * ot:128 * (ot + 1), :],
                                      in_=sfo[:, (ot % 4) * 1024:(ot % 4) * 1024 + NT]), ot)
        for t4 in range(4):
            pstf.wait(scalar, 12 + t4)

    @blk.sync
    def _(sync):
        W(sync, "s_stxa")
        W(sync, "s_stxb")


def emit_B(env, l, xst, wq_, wk_, wv_, wo, bands, Tpart, DR):
    nc = env.nc
    SEQ = 4096
    SC = 1.0 / np.sqrt(128.0) if l == 0 else 1.0 / 64.0

    wsl = bsl = None
    ones_f = DR["ones_f"]
    ones_r = DR["ones_r"]
    ones_fb = DR["ones_fb"]
    ones_rb = DR["ones_rb"]
    iden_b = DR["iden_b"]
    iden_f = DR["iden_f"]

    sxn = env.buf("sxn", [128, 16 * 512], BF16)
    swb = env.buf("swb", [128, 16 * 1536], BF16)
    swo = env.buf("swo", [128, 4 * 2048], BF16)
    sq = env.buf("sq", [128, 2 * SEQ], BF16)
    sk = env.buf("sk", [128, 2 * SEQ], BF16)
    sv = env.buf("sv", [128, 2 * 32 * 130], BF16)
    sexp = env.buf("sexp", [128, 2 * 512], BF16)
    sctx = env.buf("sctx", [128, 32 * 128], BF16)
    sctxT = env.buf("sctxT", [128, 4 * SEQ], BF16)
    sfoB = env.buf("sfoB", [128, 8 * 512])
    ssc = sfoB
    sbnd = env.buf("sbnd", [128, 4 * 512], BF16) if l == 1 else None
    srw = env.buf("srw", [1, 64])
    sbias = env.buf("sbias", [128, 2])
    szq = env.buf("szq", [128, 2])
    sbq = env.buf("sbq", [128, 12]) if l == 0 else None
    sbvr = env.buf("sbvr", [1, 512], BF16) if l == 0 else None
    ssuf = env.buf("ssuf", [1, 9 * 129]) if l == 1 else None
    ssufb = env.buf("ssufb", [1, 8 * 129], BF16) if l == 1 else None
    sone_f = env.buf("sone_f", [128, 1])
    sone_r = env.buf("sone_r", [1, 128])
    sone_fb = env.buf("sone_fb", [128, 1], BF16)
    sone_rb = env.buf("sone_rb", [1, 128], BF16)
    siden = env.buf("siden", [128, 128], BF16)
    siden_f = env.buf("siden_f", [128, 128])

    PB = [env.PB(i) for i in range(8)]

    pxn = env.pdma("pxn", 1)
    pst = env.pdma("pst", 8)

    blk = env.blk
    I, W, cnt = env.I, env.W, env.cnt

    @blk.gpsimd
    def _(gpsimd):
        if l == 0:
            for j in range(NCH):
                I(gpsimd.dma_start(out=swb[:, j * 1536:(j + 1) * 1536],
                                   in_=wsl[128 * j:128 * (j + 1), :]), "s_w", 16)
        else:
            for j in range(NCH):
                I(gpsimd.dma_start(out=swb[:, j * 1536:j * 1536 + 512],
                                   in_=wq_[128 * j:128 * (j + 1), :]), "s_w", 16)
                I(gpsimd.dma_start(out=swb[:, j * 1536 + 512:j * 1536 + 1024],
                                   in_=wk_[128 * j:128 * (j + 1), :]), "s_w", 16)
                I(gpsimd.dma_start(out=swb[:, j * 1536 + 1024:j * 1536 + 1536],
                                   in_=wv_[128 * j:128 * (j + 1), :]), "s_w", 16)
        for j in range(4):
            I(gpsimd.dma_start(out=swo[:, j * 2048:(j + 1) * 2048],
                               in_=wo[128 * j:128 * (j + 1), :]), "s_w", 16)
        if l == 0:
            for t in range(12):
                I(gpsimd.dma_start(out=sbq[:, t:t + 1],
                                   in_=bsl[128 * t:128 * (t + 1), :]), "s_w", 16)
        gpsimd.memset(sv.ap().rearrange("p (t w) -> p t w", w=130)[:, :, 128:129], 1.0)
        gpsimd.drain()
        I(gpsimd.nop(), "s_bs", 1)

    @blk.sync
    def _(sync):
        I(sync.dma_start(out=sone_f[:], in_=ones_f[:, :]), "s_k", 16)
        I(sync.dma_start(out=sone_r[:], in_=ones_r[:, :]), "s_k", 16)
        I(sync.dma_start(out=sone_fb[:], in_=ones_fb[:, :]), "s_k", 16)
        I(sync.dma_start(out=sone_rb[:], in_=ones_rb[:, :]), "s_k", 16)
        I(sync.dma_start(out=siden[:], in_=iden_b[:, :]), "s_k", 16)
        I(sync.dma_start(out=siden_f[:], in_=iden_f[:, :]), "s_k", 16)
        if l == 1:
            I(sync.dma_start(out=sbnd[:, :], in_=bands[:, :]), "s_k", 16)
        K_ALL = cnt["s_k"]
        for hp in range(2):
            for tb in range(8):
                i = hp * 8 + tb
                if i >= 1:
                    W(sync, "s_mq", i * 8)
                src = xst.ap().rearrange("(k j p) t -> p k j t", k=4, p=128)[
                    :, tb // 2, :, 512 * (tb % 2):512 * (tb % 2) + 512]
                pxn.inc(sync.dma_start(out=sxn.ap().rearrange("p (j t) -> p j t", t=512),
                                       in_=src), i)

    def qkv_phase(hp):
        @blk.tensor
        def _(tensor):
            if hp == 0:
                W(tensor, "s_k", 96 if l == 0 else 112)
                W(tensor, "s_w", cnt["s_w"])
                W(tensor, "s_bs", 1)
                if l == 0:
                    W(tensor, "s_vb", cnt["s_vb"])
            else:
                W(tensor, "s_vt", cnt["s_vt"])
                W(tensor, "s_cv", cnt["s_cv"])
                for bk in range(8):
                    head_state["ctxT_free"][bk % 4] = None
                    head_state["av_free"][bk % 4] = None
            for tb in range(8):
                i = hp * 8 + tb
                pxn.wait(tensor, i)
                sb = 0
                if i >= 1:
                    W(tensor, "s_vq", i * 8)
                for h2 in range(2):
                    hh = hp * 2 + h2
                    for j in range(NCH):
                        mm = tensor.matmul(PB[h2][:, :],
                                           swb[:, j * 1536 + hh * 128:j * 1536 + (hh + 1) * 128],
                                           sxn[:, sb + j * 512:sb + (j + 1) * 512],
                                           start=(j == 0), stop=(j == NCH - 1))
                    I(mm, "s_mq", 1)
                    for j in range(NCH):
                        mm = tensor.matmul(PB[2 + h2][:, :],
                                           swb[:, j * 1536 + 512 + hh * 128:j * 1536 + 512 + (hh + 1) * 128],
                                           sxn[:, sb + j * 512:sb + (j + 1) * 512],
                                           start=(j == 0), stop=(j == NCH - 1))
                    I(mm, "s_mq", 1)
                for tt in range(4):
                    for j in range(NCH):
                        last = (j == NCH - 1) and (l == 1)
                        mm = tensor.matmul(PB[4 + tt][:, 0:256],
                                           sxn[:, sb + j * 512 + tt * 128:sb + j * 512 + (tt + 1) * 128],
                                           swb[:, j * 1536 + 1024 + hp * 256:j * 1536 + 1024 + (hp + 1) * 256],
                                           start=(j == 0), stop=last)
                    if l == 0:
                        mm = tensor.matmul(PB[4 + tt][:, 0:256], sone_rb[:],
                                           sbvr[0:1, hp * 256:(hp + 1) * 256],
                                           start=False, stop=True)
                    I(mm, "s_mq", 1)

        @blk.vector
        def _(vector):
            for tb in range(8):
                i = hp * 8 + tb
                base = i * 8
                for h2 in range(2):
                    W(vector, "s_mq", base + 2 * h2 + 1)
                    if l == 0:
                        vector.tensor_scalar(sq[:, h2 * SEQ + tb * 512:h2 * SEQ + (tb + 1) * 512],
                                             PB[h2][:, :], sbq[:, (hp * 2 + h2):(hp * 2 + h2) + 1],
                                             0.0, ALU.add, ALU.add)
                    else:
                        vector.tensor_copy(sq[:, h2 * SEQ + tb * 512:h2 * SEQ + (tb + 1) * 512],
                                           PB[h2][:, :])
                    vector.drain()
                    I(vector.nop(), "s_vq", 1)
                    W(vector, "s_mq", base + 2 * h2 + 2)
                    if l == 0:
                        vector.tensor_scalar(sk[:, h2 * SEQ + tb * 512:h2 * SEQ + (tb + 1) * 512],
                                             PB[2 + h2][:, :], sbq[:, 4 + hp * 2 + h2:5 + hp * 2 + h2],
                                             0.0, ALU.add, ALU.add)
                    else:
                        vector.tensor_copy(sk[:, h2 * SEQ + tb * 512:h2 * SEQ + (tb + 1) * 512],
                                           PB[2 + h2][:, :])
                    vector.drain()
                    I(vector.nop(), "s_vq", 1)
                for tt in range(4):
                    W(vector, "s_mq", base + 5 + tt)
                    for h2 in range(2):
                        vector.tensor_copy(
                            sv.ap().rearrange("p (s t w) -> p s t w", s=2, w=130)[:, h2, tb * 4 + tt, 0:128],
                            PB[4 + tt][:, h2 * 128:(h2 + 1) * 128])
                        vector.drain()
                    I(vector.nop(), "s_vq", 1)

    if l == 0:
        @blk.vector
        def _(vector):
            W(vector, "s_w", cnt["s_w"])
            I(vector.nop(), "s_vb", 1)

        @blk.tensor
        def _(tensor):
            W(tensor, "s_vb", 1)
            W(tensor, "s_k", 96)
            for t in range(4):
                mm = tensor.matmul(PB[t][0:1, 0:128],
                                   sbq[:, 8 + t:9 + t], siden_f[:],
                                   start=True, stop=True)
                I(mm, "s_mb", 1)

        @blk.vector
        def _(vector):
            W(vector, "s_mb", 4)
            for t in range(4):
                vector.tensor_copy(sbvr[0:1, t * 128:(t + 1) * 128], PB[t][0:1, 0:128])
                vector.drain()
            I(vector.nop(), "s_vb", 1)

    def bounds_head(hp, h2):
        hh = hp * 2 + h2
        MB_b = cnt["s_mb"]
        VB0 = cnt["s_vb"]

        @blk.scalar
        def _(scalar):
            W(scalar, "s_vq", 64 * (hp + 1))
            for side in range(2):
                for pc in range(8):
                    i = side * 8 + pc
                    W(scalar, "s_mb", MB_b + i)
                    srcbuf = sq if side == 0 else sk
                    I(scalar.activation(ssc[:, 3584:4096], srcbuf[:, h2 * SEQ + pc * 512:h2 * SEQ + (pc + 1) * 512],
                                        AF.Square), "s_sb", 1)

        SB0 = cnt["s_sb"] - 16
        MB0 = cnt["s_mb"]

        @blk.tensor
        def _(tensor):
            for bk in range(2):
                if head_state["ctxT_free"][bk] is not None:
                    W(tensor, "s_vt", head_state["ctxT_free"][bk])
                    head_state["ctxT_free"][bk] = None
            for i in range(16):
                W(tensor, "s_sb", SB0 + i + 1)
                if i >= 1:
                    W(tensor, "s_vb", VB0 + i)
                I(tensor.matmul(PB[0][0:1, 0:512], sone_f[:], ssc[:, 3584:4096],
                                start=True, stop=True), "s_mb", 1)

        @blk.vector
        def _(vector):
            for i in range(16):
                W(vector, "s_mb", MB0 + i + 1)
                vector.reduce_max(srw[0:1, 8 + i % 2:9 + i % 2] if False else srw[0:1, 8:9],
                                  PB[0][0:1, 0:512], axis=AX.X)
                vector.drain()
                side = i // 8
                if i % 8 == 0:
                    vector.tensor_copy(srw[0:1, side:side + 1], srw[0:1, 8:9])
                else:
                    vector.tensor_tensor(srw[0:1, side:side + 1], srw[0:1, side:side + 1],
                                         srw[0:1, 8:9], ALU.max)
                vector.drain()
                I(vector.nop(), "s_vb", 1)
            I(vector.tensor_tensor(srw[0:1, 2:3], srw[0:1, 0:1], srw[0:1, 1:2],
                                   ALU.mult), "s_vb", 1)
            vector.drain()

        @blk.scalar
        def _(scalar):
            W(scalar, "s_vb", cnt["s_vb"])
            I(scalar.activation(srw[0:1, 2:3], srw[0:1, 2:3], AF.Sqrt), "s_sb", 1)
            scalar.drain()
            if l == 1:
                I(scalar.activation(srw[0:1, 4:5], srw[0:1, 2:3], AF.Exp,
                                    scale=-SC), "s_sb", 1)
                scalar.drain()

        @blk.vector
        def _(vector):
            W(vector, "s_sb", cnt["s_sb"])
            I(vector.tensor_scalar(srw[0:1, 3:4], srw[0:1, 2:3], -SC, 0.0,
                                   ALU.mult, ALU.add), "s_vb", 1)
            vector.drain()

        @blk.tensor
        def _(tensor):
            W(tensor, "s_vb", cnt["s_vb"])
            I(tensor.matmul(PB[2 + h2][:, 0:1], sone_r[:], srw[0:1, 3:4],
                            start=True, stop=True), "s_mb", 1)

        @blk.vector
        def _(vector):
            W(vector, "s_mb", cnt["s_mb"])
            I(vector.tensor_copy(sbias[:, h2:h2 + 1], PB[2 + h2][:, 0:1]), "s_vb", 1)
            vector.drain()

    def suffix_head(hp, h2):
        VB_b = cnt["s_vb"]

        @blk.tensor
        def _(tensor):
            for b in range(8):
                if b >= 1:
                    W(tensor, "s_vb", VB_b + b)
                for t in range(4):
                    mm = tensor.matmul(
                        PB[1][0:1, 0:129], sone_fb[:],
                        sv.ap().rearrange("p (s t w) -> p s t w", s=2, w=130)[:, h2, b * 4 + t, 0:129],
                        start=(t == 0), stop=(t == 3))
                I(mm, "s_mb", 1)

        MB0 = cnt["s_mb"] - 8

        @blk.vector
        def _(vector):
            for b in range(8):
                W(vector, "s_mb", MB0 + b + 1)
                vector.tensor_copy(ssuf[0:1, b * 129:(b + 1) * 129], PB[1][0:1, 0:129])
                vector.drain()
                I(vector.nop(), "s_vb", 1)
            vector.tensor_copy(ssuf[0:1, 8 * 129:9 * 129], ssuf[0:1, 7 * 129:8 * 129])
            vector.drain()
            I(vector.tensor_scalar(ssufb[0:1, 6 * 129:7 * 129], ssuf[0:1, 8 * 129:9 * 129],
                                   srw[0:1, 4:5], 0.0, ALU.mult, ALU.add), "s_vb", 1)
            vector.drain()
            for Q in range(5, -1, -1):
                vector.tensor_tensor(ssuf[0:1, 8 * 129:9 * 129], ssuf[0:1, 8 * 129:9 * 129],
                                     ssuf[0:1, (Q + 1) * 129:(Q + 2) * 129], ALU.add)
                vector.drain()
                I(vector.tensor_scalar(ssufb[0:1, Q * 129:(Q + 1) * 129],
                                       ssuf[0:1, 8 * 129:9 * 129],
                                       srw[0:1, 4:5], 0.0, ALU.mult, ALU.add), "s_vb", 1)
                vector.drain()

    head_state = {"av_free": [None] * 4, "bank_free": [None] * 4, "ek": 0,
                  "ctxT_free": [None] * 4}

    def attn_head(hp, h2):
        hh = hp * 2 + h2
        KMs = [32] * 8 if l == 0 else [4 * (Q + 1) for Q in range(8)]
        pre = [0]
        for Q in range(8):
            pre.append(pre[-1] + KMs[Q])
        KMtot = pre[-1]
        ek0 = head_state["ek"]
        ak0 = cnt["s_ak"]
        cv0 = cnt["s_cv"]
        mk0 = cnt["s_mk"]
        vk0 = cnt["s_vk"]
        svbase = h2 * 32 * 130

        def av_inc(Q, kc):
            sfx = Q if l == 1 else 0
            return ak0 + pre[Q] + kc + 1 + sfx

        def emit_av(tensor, Q, kc, KM):
            W(tensor, "s_ek", ek0 + pre[Q] + kc + 1)
            for qt in range(4):
                if kc == 0 and head_state["av_free"][qt] is not None:
                    W(tensor, "s_cv", head_state["av_free"][qt])
                last = (kc == KM - 1) and (l == 0 or Q == 7)
                mm = tensor.matmul(
                    PB[4 + qt][:, 0:129],
                    sexp[:, (kc % 2) * 512 + qt * 128:(kc % 2) * 512 + (qt + 1) * 128],
                    sv[:, (svbase + kc * 130):(svbase + kc * 130 + 129)],
                    start=(kc == 0), stop=last)
            I(mm, "s_ak", 1)

        @blk.tensor
        def _(tensor):
            W(tensor, "s_vb", cnt["s_vb"])
            for Q in range(8):
                KM = KMs[Q]
                for kc in range(KM):
                    bk = kc % 4
                    g = pre[Q] + kc
                    if head_state["bank_free"][bk] is not None:
                        W(tensor, "s_ek", head_state["bank_free"][bk])
                    if head_state["ctxT_free"][bk] is not None:
                        W(tensor, "s_vt", head_state["ctxT_free"][bk])
                        head_state["ctxT_free"][bk] = None
                    head_state["bank_free"][bk] = ek0 + g + 1
                    I(tensor.matmul(PB[bk][:, :],
                                    sk[:, h2 * SEQ + kc * 128:h2 * SEQ + (kc + 1) * 128],
                                    sq[:, h2 * SEQ + Q * 512:h2 * SEQ + (Q + 1) * 512],
                                    start=True, stop=True), "s_mk", 1)
                    if kc >= 1:
                        emit_av(tensor, Q, kc - 1, KM)
                emit_av(tensor, KM and Q, KM - 1, KM)
                if l == 1 and Q < 7:
                    for qt in range(4):
                        mm = tensor.matmul(PB[4 + qt][:, 0:129], sone_rb[:],
                                           ssufb[0:1, Q * 129:(Q + 1) * 129],
                                           start=False, stop=True)
                    I(mm, "s_ak", 1)
                for qt in range(4):
                    head_state["av_free"][qt] = cv0 + Q * 4 + qt + 1

        @blk.scalar
        def _(scalar):
            for Q in range(8):
                KM = KMs[Q]
                for kc in range(KM):
                    g = pre[Q] + kc
                    if g >= 2:
                        gq = g - 2
                        Q2 = 0
                        while pre[Q2 + 1] <= gq:
                            Q2 += 1
                        W(scalar, "s_ak", av_inc(Q2, gq - pre[Q2]))
                    if l == 1 and kc >= 4 * Q:
                        W(scalar, "s_vk", vk0 + 4 * Q + (kc - 4 * Q) + 1)
                    else:
                        W(scalar, "s_mk", mk0 + g + 1)
                    I(scalar.activation(sexp[:, (kc % 2) * 512:(kc % 2 + 1) * 512],
                                        PB[kc % 4][:, :], AF.Exp,
                                        bias=sbias[:, h2:h2 + 1], scale=SC),
                      "s_ek", 1)
        head_state["ek"] = ek0 + KMtot

        def _evac_block(vector, Q):
            KM = KMs[Q]
            for qt in range(4):
                W(vector, "s_ak", av_inc(Q, KM - 1) + (1 if (l == 1 and Q < 7) else 0))
                vector.tensor_copy(szq[:, 0:1], PB[4 + qt][:, 128:129])
                vector.drain()
                vector.reciprocal(szq[:, 1:2], szq[:, 0:1])
                vector.drain()
                I(vector.tensor_scalar(sctx[:, (Q * 4 + qt) * 128:(Q * 4 + qt + 1) * 128],
                                       PB[4 + qt][:, 0:128], szq[:, 1:2], 0.0,
                                       ALU.mult, ALU.add), "s_cv", 1)
                vector.drain()

        @blk.vector
        def _(vector):
            for Q in range(8):
                if Q >= 1:
                    _evac_block(vector, Q - 1)
                if l == 1:
                    for j in range(4):
                        kc = 4 * Q + j
                        g = pre[Q] + kc
                        W(vector, "s_mk", mk0 + g + 1)
                        I(vector.tensor_tensor(PB[kc % 4][:, :], PB[kc % 4][:, :],
                                               sbnd[:, j * 512:(j + 1) * 512],
                                               ALU.mult), "s_vk", 1)
                        vector.drain()
            _evac_block(vector, 7)

        MT0 = cnt["s_mt"]
        VT0 = cnt["s_vt"]

        @blk.tensor
        def _(tensor):
            for i32 in range(32):
                bk = i32 % 4
                W(tensor, "s_cv", cv0 + i32 + 1)
                if head_state["bank_free"][bk] is not None:
                    W(tensor, "s_ek", head_state["bank_free"][bk])
                    head_state["bank_free"][bk] = None
                if i32 >= 4:
                    W(tensor, "s_vt", VT0 + i32 - 3)
                I(tensor.matmul(PB[bk][:, 0:128], sctx[:, i32 * 128:(i32 + 1) * 128],
                                siden[:], start=True, stop=True), "s_mt", 1)

        @blk.vector
        def _(vector):
            for i32 in range(32):
                bk = i32 % 4
                W(vector, "s_mt", MT0 + i32 + 1)
                vector.tensor_copy(sctxT[:, hh * SEQ + i32 * 128:hh * SEQ + (i32 + 1) * 128],
                                   PB[bk][:, 0:128])
                vector.drain()
                I(vector.nop(), "s_vt", 1)
        for bk in range(4):
            head_state["ctxT_free"][bk] = VT0 + 29 + bk

    def proj_phase():
        MP0 = cnt["s_mp"]
        VP0 = cnt["s_vp"]

        @blk.tensor
        def _(tensor):
            g = 0
            for ot in range(16):
                for tb in range(8):
                    bk = g % 8
                    if g >= 8:
                        W(tensor, "s_vp", VP0 + g - 7)
                    elif bk < 4:
                        if head_state["ctxT_free"][bk] is not None:
                            W(tensor, "s_vt", head_state["ctxT_free"][bk])
                            head_state["ctxT_free"][bk] = None
                    else:
                        if head_state["av_free"][bk - 4] is not None:
                            W(tensor, "s_cv", head_state["av_free"][bk - 4])
                            head_state["av_free"][bk - 4] = None
                    for fc in range(4):
                        mm = tensor.matmul(PB[bk][:, :],
                                           swo[:, fc * 2048 + ot * 128:fc * 2048 + (ot + 1) * 128],
                                           sctxT[:, fc * SEQ + tb * 512:fc * SEQ + (tb + 1) * 512],
                                           start=(fc == 0), stop=(fc == 3))
                    I(mm, "s_mp", 1)
                    g += 1

        @blk.vector
        def _(vector):
            for g in range(128):
                bk = g % 8
                W(vector, "s_mp", MP0 + g + 1)
                if g >= 8:
                    pst.wait(vector, g - 8)
                vector.tensor_copy(sfoB[:, bk * 512:(bk + 1) * 512], PB[bk][:, :])
                vector.drain()
                I(vector.nop(), "s_vp", 1)

        @blk.scalar
        def _(scalar):
            for g in range(128):
                ot, tb = g // 8, g % 8
                bk = g % 8
                W(scalar, "s_vp", VP0 + g + 1)
                ro = 2048 * (tb // 2) + 128 * ot
                co = 512 * (tb % 2)
                pst.inc(scalar.dma_start(out=Tpart[ro:ro + 128, co:co + 512],
                                         in_=sfoB[:, bk * 512:(bk + 1) * 512]), g)
            for i in range(8):
                pst.wait(scalar, 120 + i)

    qkv_phase(0)
    for h2 in range(2):
        bounds_head(0, h2)
        if l == 1:
            suffix_head(0, h2)
        attn_head(0, h2)
    qkv_phase(1)
    for h2 in range(2):
        bounds_head(1, h2)
        if l == 1:
            suffix_head(1, h2)
        attn_head(1, h2)
    proj_phase()


def emit_C1(env, xT, ffn, psl, xo):
    sx = env.buf("sx", [128, 2 * NT])
    sf = env.buf("sf", [128, 2 * NT])
    sp = env.buf("sp", [128, 2 * NT])
    so = env.buf("so", [128, 2 * NT])
    px = env.pdma("px")
    pf = env.pdma("pf")
    pp2 = env.pdma("psc", 4)
    po = env.pdma("po")
    blk = env.blk
    I, W = env.I, env.W

    @blk.sync
    def _(sync):
        for j in range(NCH):
            if j >= 2:
                W(sync, "s_v", j - 1)
            sl = (j % 2)
            px.inc(sync.dma_start(out=sx[:, sl * NT:(sl + 1) * NT],
                                  in_=xT[128 * j:128 * (j + 1), :]), j)
            pf.inc(sync.dma_start(out=sf[:, sl * NT:(sl + 1) * NT],
                                  in_=ffn[128 * j:128 * (j + 1), :]), j)
            pp2.inc(sync.dma_start(out=sp[:, sl * NT:(sl + 1) * NT],
                                   in_=psl[128 * j:128 * (j + 1), :]), j)

    @blk.vector
    def _(vector):
        for j in range(NCH):
            sl = (j % 2)
            px.wait(vector, j)
            pf.wait(vector, j)
            pp2.wait(vector, j)
            if j >= 2:
                po.wait(vector, j - 2)
            vector.tensor_tensor(so[:, sl * NT:(sl + 1) * NT],
                                 sp[:, sl * NT:(sl + 1) * NT],
                                 sx[:, sl * NT:(sl + 1) * NT], ALU.add)
            vector.drain()
            I(vector.tensor_tensor(so[:, sl * NT:(sl + 1) * NT],
                                   so[:, sl * NT:(sl + 1) * NT],
                                   sf[:, sl * NT:(sl + 1) * NT], ALU.add), "s_v", 1)
            vector.drain()

    @blk.scalar
    def _(scalar):
        for j in range(NCH):
            W(scalar, "s_v", j + 1)
            po.inc(scalar.dma_start(out=xo[128 * j:128 * (j + 1), :],
                                    in_=so[:, (j % 2) * NT:(j % 2 + 1) * NT]), j)
        po.wait(scalar, NCH - 2)
        po.wait(scalar, NCH - 1)


def emit_B0(env, xn, wqkv, bqkv, Tp, DR):
    nc = env.nc
    SC = 1.0 / np.sqrt(128.0)

    ones_rb = DR["ones_rb"]

    sxnb = env.buf("sxnb", [128, NCH * NT], BF16)
    sqkv = env.buf("sqkv", [128, 8 * 6144], BF16)
    swt = env.buf("swt", [128, 2 * 512], BF16)
    sbrf = env.buf("sbrf", [1, 3 * D])
    sbrb = env.buf("sbrb", [1, 3 * D], BF16)
    sone_rb = env.buf("sone_rb", [1, 128], BF16)
    sP = env.buf("sP", [128, 2048], BF16)
    sS = env.buf("sS", [128, 256])
    sE = env.buf("sE", [128, 256])
    sm = env.buf("sm", [128, 1])
    sZ = env.buf("sZ", [128, 32])
    sacc = env.buf("sacc", [128, 2048])
    sco = env.buf("sco", [128, 2 * 2048], BF16)

    PB = [env.PB(i) for i in range(8)]

    pwl = env.pdma("pwg")
    pco = env.pdma("px")

    blk = env.blk
    I, W, cnt = env.I, env.W, env.cnt

    @blk.sync
    def _(sync):
        I(sync.dma_start(out=sbrf[:, :], in_=bqkv[:, :]), "s_k", 16)
        I(sync.dma_start(out=sone_rb[:], in_=ones_rb[:, :]), "s_k", 16)
        for j in range(NCH):
            I(sync.dma_start(out=sxnb[:, j * NT:(j + 1) * NT],
                             in_=xn[128 * j:128 * (j + 1), :]), "s_xa", 16)

    @blk.vector
    def _(vector):
        W(vector, "s_k", 32)
        vector.tensor_copy(sbrb[:, :], sbrf[:, :])
        vector.drain()
        I(vector.nop(), "s_vx", 1)

    def pe_done(n):
        cb, fc = n // 16, n % 16
        return cb * 136 + (fc + 1) * 8

    @blk.gpsimd
    def _(gpsimd):
        for n in range(12 * NCH):
            if n >= 2:
                W(gpsimd, "s_mq", pe_done(n - 2))
            cb, fc = n // 16, n % 16
            pwl.inc(gpsimd.dma_start(
                out=swt[:, (n % 2) * 512:(n % 2 + 1) * 512],
                in_=wqkv[128 * fc:128 * (fc + 1), 512 * cb:512 * (cb + 1)]), n)

    @blk.tensor
    def _(tensor):
        W(tensor, "s_vx", 1)
        W(tensor, "s_xa", 16 * NCH)
        for cb in range(12):
            if cb >= 1:
                W(tensor, "s_vq", cb * 8)
            for fc in range(NCH):
                n = cb * NCH + fc
                pwl.wait(tensor, n)
                sl = (n % 2) * 512
                for tt in range(8):
                    mm = tensor.matmul(PB[tt][:, :],
                                       sxnb[:, fc * NT + tt * 128:fc * NT + (tt + 1) * 128],
                                       swt[:, sl:sl + 512],
                                       start=(fc == 0), stop=False)
                I(mm, "s_mq", 8)
            for tt in range(8):
                mm = tensor.matmul(PB[tt][:, :], sone_rb[:],
                                   sbrb[0:1, 512 * cb:512 * (cb + 1)],
                                   start=False, stop=True)
            I(mm, "s_mq", 8)

    @blk.vector
    def _(vector):
        for cb in range(12):
            W(vector, "s_mq", (cb + 1) * 136)
            for tt in range(8):
                dst = sqkv[:, tt * 6144 + cb * 512:tt * 6144 + (cb + 1) * 512]
                if cb < 4:
                    vector.tensor_scalar(dst, PB[tt][:, :], SC, 0.0, ALU.mult, ALU.add)
                else:
                    vector.tensor_copy(dst, PB[tt][:, :])
                vector.drain()
            I(vector.nop(), "s_vq", 8)

    VQ_ALL = 12 * 8

    @blk.vector
    def _(vector):
        W(vector, "s_vq", VQ_ALL)
        for tt in range(8):
            qb, kb, vb = tt * 6144, tt * 6144 + 2048, tt * 6144 + 4096
            if tt >= 1:
                W(vector, "s_ex", tt)
            for o in range(16):
                lim = (16 - o) * 128
                vector.tensor_tensor(sP[:, 0:lim], sqkv[:, qb:qb + lim],
                                     sqkv[:, kb + o * 128:kb + 2048], ALU.mult)
                vector.drain()
                if o > 0:
                    vector.tensor_tensor(sP[:, lim:2048], sqkv[:, qb + lim:qb + 2048],
                                         sqkv[:, kb:kb + o * 128], ALU.mult)
                    vector.drain()
                vector.reduce_sum(sS[:, o * 16:(o + 1) * 16],
                                  sP.ap().rearrange("p (h e) -> p h e", e=128),
                                  axis=AX.X)
                vector.drain()
            I(vector.reduce_max(sm[:, :], sS[:, :], axis=AX.X, negate=True), "s_s0", 1)
            vector.drain()
            W(vector, "s_ex", tt + 1)
            if tt >= 1:
                W(vector, "s_co", 16 * tt)
            vector.reduce_sum(sZ[:, 0:16],
                              sE.ap().rearrange("p (o h) -> p h o", h=16),
                              axis=AX.X)
            vector.drain()
            vector.reciprocal(sZ[:, 16:32], sZ[:, 0:16])
            vector.drain()
            I(vector.nop(), "s_zv", 1)
            for h in range(16):
                for t in range(16):
                    col = ((t - h) % 16) * 16 + h
                    vsl = sqkv[:, vb + t * 128:vb + (t + 1) * 128]
                    if t == 0:
                        vector.tensor_scalar(sacc[:, h * 128:(h + 1) * 128], vsl,
                                             sE[:, col:col + 1], 0.0, ALU.mult, ALU.add)
                    else:
                        vector.scalar_tensor_tensor(sacc[:, h * 128:(h + 1) * 128], vsl,
                                                    sE[:, col:col + 1],
                                                    sacc[:, h * 128:(h + 1) * 128],
                                                    ALU.mult, ALU.add)
                    vector.drain()
                if tt >= 2:
                    pco.wait(vector, tt - 2)
                I(vector.tensor_scalar(sco[:, (tt % 2) * 2048 + h * 128:(tt % 2) * 2048 + (h + 1) * 128],
                                       sacc[:, h * 128:(h + 1) * 128],
                                       sZ[:, 16 + h:17 + h], 0.0, ALU.mult, ALU.add),
                  "s_co", 1)
                vector.drain()

    @blk.scalar
    def _(scalar):
        for tt in range(8):
            W(scalar, "s_s0", tt + 1)
            if tt >= 1:
                W(scalar, "s_co", 16 * tt)
            I(scalar.activation(sE[:, :], sS[:, :], AF.Exp, bias=sm[:, :]), "s_ex", 1)
            scalar.drain()

    @blk.sync
    def _(sync):
        TpV = Tp.ap().rearrange("(g t) d -> t g d", g=4)
        for tt in range(8):
            W(sync, "s_co", 16 * (tt + 1))
            pco.inc(sync.dma_start(
                out=TpV[128 * tt:128 * (tt + 1)],
                in_=sco.ap().rearrange("p (s g d) -> p s g d", s=2, g=4)[:, tt % 2]),
                tt)
        pco.wait(sync, 6)
        pco.wait(sync, 7)


def emit_SC(env, Tp, Gc, Tq, ctxR, Yt, DR):
    blk = env.blk
    I, W, cnt = env.I, env.W, env.cnt
    PB = [env.PB(i) for i in range(8)]
    sld = env.buf("sld", [128, 4 * 128], BF16)
    sct = env.buf("sct", [128, 4 * 128], BF16)
    sperm = env.buf("sperm", [128, 128], BF16)
    psc = env.pdma("psc", 4)
    pys = env.pdma("pstf", 4)

    @blk.gpsimd
    def _(g):
        I(g.collective_compute("AllGather", ALU.bypass, replica_groups=G4,
                               ins=[Tp.ap().opt()], outs=[Gc.ap().opt()]),
          "s_cc", 1)
        W(g, "s_cc")
        for k in range(4):
            for gi in range(4):
                I(g.dma_start(
                    out=Tq[4096 * gi + 1024 * k:4096 * gi + 1024 * (k + 1), :],
                    in_=Gc[4096 * k + 1024 * gi:4096 * k + 1024 * (gi + 1), :]),
                  "s_pm", 16)
        W(g, "s_pm")
        I(g.collective_compute("ReduceScatter", ALU.max, replica_groups=G4,
                               ins=[Tq.ap().opt()], outs=[ctxR.ap().opt()]),
          "s_cc", 1)

    @blk.sync
    def _(sync):
        I(sync.dma_start(out=sperm[:, :], in_=DR["perm_b"][:, :]), "s_k", 16)
        W(sync, "s_cc")
        for n in range(128):
            a, h = n // 4, n % 4
            if n >= 4:
                W(sync, "s_mt", n - 3)
            psc.inc(sync.dma_start(
                out=sld[:, (n % 4) * 128:(n % 4 + 1) * 128],
                in_=ctxR[128 * a:128 * (a + 1), 128 * h:128 * (h + 1)]), n)

    @blk.tensor
    def _(t):
        W(t, "s_k", 16)
        for n in range(128):
            psc.wait(t, n)
            if n >= 4:
                W(t, "s_vt", n - 3)
            I(t.matmul(PB[n % 4][:, 0:128],
                       sld[:, (n % 4) * 128:(n % 4 + 1) * 128],
                       sperm[:, :], start=True, stop=True), "s_mt", 1)

    @blk.vector
    def _(v):
        for n in range(128):
            W(v, "s_mt", n + 1)
            if n >= 4:
                pys.wait(v, n - 4)
            v.tensor_copy(sct[:, (n % 4) * 128:(n % 4 + 1) * 128],
                          PB[n % 4][:, 0:128])
            v.drain()
            I(v.nop(), "s_vt", 1)

    @blk.scalar
    def _(s):
        YV = Yt.ap().rearrange("(u dd) (h a j) -> dd u h a j", u=16, h=4, a=32)
        sctV = sct.ap().rearrange("p (sl u j) -> p sl u j", sl=4, j=8)
        for n in range(128):
            a, h = n // 4, n % 4
            W(s, "s_vt", n + 1)
            pys.inc(s.dma_start(out=YV[:, :, h, a, :], in_=sctV[:, n % 4]), n)
        for i in range(4):
            pys.wait(s, 124 + i)


def emit_C0(env, Yt, wproj, bias, xT, ffn, xo, DR):
    nc = env.nc

    sy = env.buf("sy", [128, NCH * NT], BF16)
    swt = env.buf("swt", [128, 4 * 128], BF16)
    sb = env.buf("sb", [128, 16])
    sx = env.buf("sx", [128, 2 * NT])
    sf = env.buf("sf", [128, 2 * NT])
    so = env.buf("so", [128, 2 * NT])
    PB = [env.PB(i) for i in range(8)]

    pwl = env.pdma("pwl4", 4)
    px = env.pdma("px")
    pf = env.pdma("pf")
    po = env.pdma("po")

    blk = env.blk
    I, W, cnt = env.I, env.W, env.cnt

    @blk.sync
    def _(sync):
        for j in range(NCH):
            I(sync.dma_start(out=sb[:, j:j + 1], in_=bias[128 * j:128 * (j + 1), :]),
              "s_k", 16)
        for j in range(NCH):
            I(sync.dma_start(out=sy[:, j * NT:(j + 1) * NT],
                             in_=Yt[128 * j:128 * (j + 1), :]), "s_ya", 16)
        for ot in range(16):
            if ot >= 2:
                W(sync, "s_vp", 2 * (ot - 1))
            sl = (ot % 2)
            px.inc(sync.dma_start(out=sx[:, sl * NT:(sl + 1) * NT],
                                  in_=xT[128 * ot:128 * (ot + 1), :]), ot)
            pf.inc(sync.dma_start(out=sf[:, sl * NT:(sl + 1) * NT],
                                  in_=ffn[128 * ot:128 * (ot + 1), :]), ot)

    @blk.gpsimd
    def _(gpsimd):
        for n in range(16 * 16):
            ot, fc = n // 16, n % 16
            if n >= 4:
                W(gpsimd, "s_mp", n - 3)
            pwl.inc(gpsimd.dma_start(
                out=swt[:, (n % 4) * 128:(n % 4 + 1) * 128],
                in_=wproj[128 * fc:128 * (fc + 1), 128 * ot:128 * (ot + 1)]), n)

    @blk.tensor
    def _(tensor):
        W(tensor, "s_ya", 16 * NCH)
        for ot in range(16):
            b0 = (ot % 4) * 2
            if ot >= 4:
                W(tensor, "s_vp", 2 * (ot - 3))
            for fc in range(NCH):
                n = ot * 16 + fc
                pwl.wait(tensor, n)
                sl = (n % 4) * 128
                for hf in range(2):
                    mm = tensor.matmul(PB[b0 + hf][:, :], swt[:, sl:sl + 128],
                                       sy[:, fc * NT + hf * 512:fc * NT + (hf + 1) * 512],
                                       start=(fc == 0), stop=(fc == NCH - 1))
                I(mm, "s_mp", 1)

    @blk.vector
    def _(vector):
        W(vector, "s_k", 16 * NCH)
        for ot in range(16):
            b0 = (ot % 4) * 2
            W(vector, "s_mp", (ot + 1) * NCH)
            px.wait(vector, ot)
            pf.wait(vector, ot)
            if ot >= 2:
                po.wait(vector, ot - 2)
            sl = (ot % 2)
            for hf in range(2):
                rng = slice(sl * NT + hf * 512, sl * NT + (hf + 1) * 512)
                vector.scalar_tensor_tensor(so[:, rng], PB[b0 + hf][:, :],
                                            sb[:, ot:ot + 1], sx[:, rng],
                                            ALU.add, ALU.add)
                vector.drain()
                vector.tensor_tensor(so[:, rng], so[:, rng], sf[:, rng], ALU.add)
                vector.drain()
            I(vector.nop(), "s_vp", 2)

    @blk.scalar
    def _(scalar):
        for ot in range(16):
            W(scalar, "s_vp", 2 * (ot + 1))
            po.inc(scalar.dma_start(out=xo[128 * ot:128 * (ot + 1), :],
                                    in_=so[:, (ot % 2) * NT:(ot % 2 + 1) * NT]), ot)
        po.wait(scalar, 14)
        po.wait(scalar, 15)


def build_fused(debug=False):
    env = Env(debug=debug)
    nc = env.nc

    def din_(name, shape, dt=F32):
        return nc.dram_tensor(name, shape, dt, kind="ExternalInput")

    xT0 = din_("xT0", [D, NT])
    w1t = din_("w1t", [D, INNER])
    w2t = din_("w2t", [INNER, D])
    gatet = din_("gatet", [D, INNER])
    upt = din_("upt", [D, INNER])
    downt = din_("downt", [INNER, D])
    sh_w1 = din_("sh_w1", [D // 8, INNER])
    sh_w2 = din_("sh_w2", [INNER // 8, D])
    sh_g = din_("sh_g", [D // 8, INNER])
    sh_u = din_("sh_u", [D // 8, INNER])
    sh_d = din_("sh_d", [INNER // 8, D])
    wqkv = din_("wqkv", [D, 3 * D])
    bqkv = din_("bqkv", [1, 3 * D])
    wproj = din_("wproj", [D, D])
    bproj = din_("bproj", [D, 1])
    wq_sl = din_("wq_sl", [D, 512])
    wk_sl = din_("wk_sl", [D, 512])
    wv_sl = din_("wv_sl", [D, 512])
    wo_sl = din_("wo_sl", [512, D])
    bands = din_("bands", [128, 4 * 512], BF16)
    DR = {
        "ones_f": din_("ones_f", [128, 1]),
        "ones_r": din_("ones_r", [1, 128]),
        "ones_fb": din_("ones_fb", [128, 1], BF16),
        "ones_rb": din_("ones_rb", [1, 128], BF16),
        "iden_f": din_("iden_f", [128, 128]),
        "iden_b": din_("iden_b", [128, 128], BF16),
        "perm_b": din_("perm_b", [128, 128], BF16),
    }
    xo = nc.dram_tensor("xo", [D, NT], F32, kind="ExternalOutput")

    DR["arin"] = nc.dram_tensor("arin", [1, 3], F32)
    DR["arout"] = nc.dram_tensor("arout", [1, 3], F32)
    DR["hscr"] = nc.dram_tensor("hscr", [INNER, NT], F32)
    DR["t2scr"] = nc.dram_tensor("t2scr", [INNER, NT], BF16)
    DR["xnf"] = nc.dram_tensor("xnf", [D, NT], F32)
    xn0 = nc.dram_tensor("xn0", [D, NT], BF16)
    ffn0 = nc.dram_tensor("ffn0", [D, NT], F32)
    Tp = nc.dram_tensor("Tp", [S, 512], BF16)
    Gc = nc.dram_tensor("Gc", [4 * S, 512], BF16)
    Tq = nc.dram_tensor("Tq", [4 * S, 512], BF16)
    ctxR = nc.dram_tensor("ctxR", [S, 512], BF16)
    Yt = nc.dram_tensor("Yt", [D, NT], BF16)
    x1 = nc.dram_tensor("x1", [D, NT], F32)
    xn1 = nc.dram_tensor("xn1", [D, NT], BF16)
    ffn1 = nc.dram_tensor("ffn1", [D, NT], F32)
    xst = nc.dram_tensor("xst", [4 * D, NT], BF16)
    Tpart = nc.dram_tensor("Tpart", [4 * D, NT], F32)
    psl = nc.dram_tensor("psl", [D, NT], F32)

    env.make()
    blk, I, W = env.blk, env.I, env.W

    env.open_stage("a0")
    emit_A(env, 0, xT0, [w1t, w2t], [sh_w1, sh_w2], xn0, ffn0, DR)
    env.barrier()
    env.open_stage("b0")
    emit_B0(env, xn0, wqkv, bqkv, Tp, DR)
    env.barrier()
    env.open_stage("sc")
    emit_SC(env, Tp, Gc, Tq, ctxR, Yt, DR)
    env.barrier()
    env.open_stage("c0")
    emit_C0(env, Yt, wproj, bproj, xT0, ffn0, x1, DR)
    env.barrier()
    env.open_stage("a1")
    emit_A(env, 1, x1, [gatet, upt, downt], [sh_g, sh_u, sh_d], xn1, ffn1, DR)
    env.barrier()
    env.open_stage("gx")

    @blk.gpsimd
    def _(g):
        I(g.collective_compute("AllGather", ALU.bypass, replica_groups=G4,
                               ins=[xn1.ap().opt()], outs=[xst.ap().opt()]),
          "s_cc", 1)
        W(g, "s_cc")

    env.barrier()
    env.open_stage("b1")
    emit_B(env, 1, xst, wq_sl, wk_sl, wv_sl, wo_sl, bands, Tpart, DR)
    env.barrier()
    env.open_stage("pr")

    @blk.gpsimd
    def _(g):
        I(g.collective_compute("ReduceScatter", ALU.add, replica_groups=G4,
                               ins=[Tpart.ap().opt()], outs=[psl.ap().opt()]),
          "s_cc", 1)
        W(g, "s_cc")

    env.barrier()
    env.open_stage("c1")
    emit_C1(env, x1, ffn1, psl, xo)

    env.close()
    return env


from concourse.bass_utils import run_bass_kernel_spmd

_CACHE = {}


def kernel(**inputs):
    inputs = {k: np.asarray(v) for k, v in inputs.items()}
    sh, percore = host_prep(inputs)
    if "env" not in _CACHE:
        _CACHE["env"] = build_fused()
    nc = _CACHE["env"].nc
    bq = np.ascontiguousarray(
        np.asarray(inputs["L0_attn_bqkv"], np.float32).reshape(1, -1))
    wproj = np.ascontiguousarray(np.asarray(inputs["L0_attn_Wproj"], np.float32))
    ims = []
    for c in range(NCORES):
        pc = percore[c]
        ims.append({
            "xT0": pc["x0"].astype(np.float32),
            "w1t": sh["l0_w1t"], "w2t": sh["l0_w2t"],
            "gatet": sh["l1_gatet"], "upt": sh["l1_upt"],
            "downt": sh["l1_downt"],
            "sh_w1": pc["sh_l0w1"], "sh_w2": pc["sh_l0w2"],
            "sh_g": pc["sh_l1g"], "sh_u": pc["sh_l1u"], "sh_d": pc["sh_l1d"],
            "wqkv": sh["l0_wqkv"], "bqkv": bq,
            "wproj": wproj, "bproj": sh["l0_bproj"],
            "wq_sl": pc["l1_wq_sl"], "wk_sl": pc["l1_wk_sl"],
            "wv_sl": pc["l1_wv_sl"], "wo_sl": pc["l1_wo_sl"],
            "bands": sh["bands"],
            "ones_f": sh["ones_f"], "ones_r": sh["ones_r"],
            "ones_fb": np.ones((128, 1), bf16), "ones_rb": sh["ones_rb"],
            "iden_f": np.eye(128, dtype=np.float32), "iden_b": sh["iden_b"],
            "perm_b": sh["perm_b"],
        })
    r = run_bass_kernel_spmd(nc, ims, core_ids=list(range(NCORES)))
    out = np.empty((B, S, D), np.float32)
    for c in range(NCORES):
        s_, r_ = c // 4, c % 4
        out[s_, 1024 * r_:1024 * (r_ + 1), :] = np.asarray(r.results[c]["xo"]).T
    return out


# version 04
# speedup vs baseline: 1.0405x; 1.0405x over previous
import sys
sys.path.insert(0, '/opt/trn_rl_repo')
import numpy as np
import ml_dtypes
import concourse.bass as bass
import concourse.mybir as mybir
from contextlib import ExitStack

F32 = mybir.dt.float32
BF16 = mybir.dt.bfloat16
AF = mybir.ActivationFunctionType
ALU = mybir.AluOpType
AX = mybir.AxisListType

NCORES = 8
B, S, D, H, DH = 2, 4096, 2048, 16, 128
INNER = 4 * D
NT = 1024
NCH = D // 128
HCH = INNER // 128
RC = 1.5 * (2.0 ** 23)
EPS = 1e-5

bf16 = ml_dtypes.bfloat16



def host_prep(inp):
    C = np.ascontiguousarray
    sh = {}
    sh["l0_wqkv"] = C(inp["L0_attn_Wqkv"])
    sh["l0_bproj"] = C(inp["L0_attn_bproj"].reshape(D, 1))
    sh["l0_w1t"] = C(inp["L0_ffn_w1"].T)
    sh["l0_w2t"] = C(inp["L0_ffn_w2"].T)
    sh["l1_gatet"] = C(inp["L1_ffn_w_gate"].T)
    sh["l1_upt"] = C(inp["L1_ffn_w_up"].T)
    sh["l1_downt"] = C(inp["L1_ffn_w_down"].T)
    sh["ones_f"] = np.ones((128, 1), np.float32)
    sh["ones_r"] = np.ones((1, 128), np.float32)
    sh["ones_rb"] = np.ones((1, 128), bf16)
    sh["iden_b"] = np.eye(128).astype(bf16)
    pm = np.zeros((128, 128), np.float32)
    for d_ in range(128):
        pm[d_, (d_ % 16) * 8 + d_ // 16] = 1.0
    sh["perm_b"] = pm.astype(bf16)
    bands = np.zeros((128, 4 * 512), np.float32)
    q = np.arange(512)[None, :]
    k = np.arange(128)[:, None]
    for j in range(4):
        bands[:, j * 512:(j + 1) * 512] = (q >= (128 * j + k)).astype(np.float32)
    sh["bands"] = bands.astype(bf16)

    percore = []
    x0 = np.asarray(inp["L0_attn_x"])
    wqkv = np.asarray(inp["L0_attn_Wqkv"])
    bqkv = np.asarray(inp["L0_attn_bqkv"])
    wproj = np.asarray(inp["L0_attn_Wproj"])
    wq, wk, wv, wo = (np.asarray(inp["L1_attn_W" + n]) for n in "qkvo")
    w1t, w2t = sh["l0_w1t"], sh["l0_w2t"]
    gt, ut, dt_ = sh["l1_gatet"], sh["l1_upt"], sh["l1_downt"]
    for c in range(NCORES):
        s, r = c // 4, c % 4
        pc = {}
        pc["x0"] = C(x0[s, 1024 * r:1024 * (r + 1), :].T)
        cols = slice(512 * r, 512 * (r + 1))
        pc["l0_wqkv_sl"] = C(np.concatenate(
            [wqkv[:, cols], wqkv[:, 2048:][:, cols], wqkv[:, 4096:][:, cols]], axis=1))
        pc["l0_bqkv_sl"] = C(np.concatenate(
            [bqkv[cols], bqkv[2048:][cols], bqkv[4096:][cols]]).reshape(1536, 1))
        pc["l0_wproj_sl"] = C(wproj[512 * r:512 * (r + 1), :])
        pc["l1_wq_sl"] = C(wq[:, cols])
        pc["l1_wk_sl"] = C(wk[:, cols])
        pc["l1_wv_sl"] = C(wv[:, cols])
        pc["l1_wo_sl"] = C(wo[512 * r:512 * (r + 1), :])
        pc["sh_l0w1"] = C(w1t[256 * c:256 * (c + 1)])
        pc["sh_l0w2"] = C(w2t[1024 * c:1024 * (c + 1)])
        pc["sh_l1g"] = C(gt[256 * c:256 * (c + 1)])
        pc["sh_l1u"] = C(ut[256 * c:256 * (c + 1)])
        pc["sh_l1d"] = C(dt_[1024 * c:1024 * (c + 1)])
        percore.append(pc)
    return sh, percore



class _CntView:
    def __init__(self, env):
        self._env = env

    def __getitem__(self, k):
        return self._env.gcnt.get(k, 0) - self._env.base.get(k, 0)


class PDma:
    def __init__(self, env, name, nslots=2):
        self.env = env
        self.names = [f"{name}{j}" for j in range(nslots)]
        for nm in self.names:
            env.sem_h(nm)
        self.base = [env.gcnt.get(nm, 0) for nm in self.names]
        self.n = nslots

    def inc(self, instr, i):
        nm = self.names[i % self.n]
        self.env.gcnt[nm] = self.env.gcnt.get(nm, 0) + 16
        instr.then_inc(self.env._sems[nm], 16)

    def wait(self, eng, i):
        j = i % self.n
        eng.wait_ge(self.env._sems[self.names[j]],
                    self.base[j] + 16 * (i // self.n + 1))


class Env:
    def __init__(self, debug=False):
        self.nc = bass.Bass("TRN2", target_bir_lowering=False, debug=debug)
        self.st = ExitStack()
        self.gcnt = {}
        self.base = {}
        self.cnt = _CntView(self)
        self._sems = {}
        self._stage_bufs = {}
        self._cursor = 0
        self._maxcur = 0
        self._virts = []
        self.blk = None
        self.PS = None
        self._barn = 0
        self.stage_marks = []

    def make(self):
        self.PS = self.st.enter_context(
            self.nc.psum_tensor("PS", [128, 4096], F32))
        for kb in (207, 204, 200, 196, 192):
            try:
                self._arena = self.st.enter_context(
                    self.nc.sbuf_tensor(f"arena{kb}", [128, kb * 1024],
                                        mybir.dt.uint8))
                break
            except Exception:
                continue
        self._asize = kb * 1024
        self._aoff = self.nc.lookup_mloc(self._arena).addr
        self.blk = self.st.enter_context(self.nc.Block())
        self.sem_h("bar")

    def PB(self, i):
        return self.PS[:, 512 * i:512 * (i + 1)]

    def sem_h(self, name):
        if name not in self._sems:
            self._sems[name] = self.st.enter_context(self.nc.semaphore(name))
        return self._sems[name]

    def I(self, instr, s, n=1):
        self.gcnt[s] = self.gcnt.get(s, 0) + n
        instr.then_inc(self.sem_h(s), n)
        return self.gcnt[s]

    def W(self, eng, s, v=None):
        g = self.gcnt.get(s, 0) if v is None else self.base.get(s, 0) + v
        eng.wait_ge(self.sem_h(s), g)

    def pdma(self, name, nslots=2):
        return PDma(self, name, nslots)

    def open_stage(self, tag):
        self._stage_bufs = {}
        self._cursor = 0
        self._tag = tag
        self.base = dict(self.gcnt)
        self.stage_marks.append((tag, self.nc.n_instructions()))

    def buf(self, name, shape, dt=F32):
        if name in self._stage_bufs:
            return self._stage_bufs[name]
        per = int(np.prod(shape[1:])) * mybir.dt.size(dt)
        addr = self._cursor
        self._cursor = (addr + per + 31) // 32 * 32
        self._maxcur = max(self._maxcur, self._cursor)
        assert self._cursor <= self._asize, \
            f"arena overflow {self._tag}/{name}: {self._cursor}>{self._asize}"
        h = self.nc.virtual_sbuf_tensor(f"{self._tag}_{name}", shape, dt)
        self.nc._tensor(h.name, h.shape, h.dtype, type="SB")
        mloc = self.nc.lookup_mloc(h)
        mloc.allocated = True
        mloc.addr = self._aoff + addr
        self._stage_bufs[name] = h
        return h

    def barrier(self):
        self._barn += 1
        blk, bar = self.blk, self._sems["bar"]
        for dec in (blk.sync, blk.tensor, blk.vector, blk.scalar, blk.gpsimd):
            def _inc(e):
                e.nop().then_inc(bar, 1)
            dec(_inc)
        tgt = 5 * self._barn
        for dec in (blk.sync, blk.tensor, blk.vector, blk.scalar, blk.gpsimd):
            def _wt(e):
                e.wait_ge(bar, tgt)
            dec(_wt)

    def close(self):
        self.st.close()


G4 = [[0, 1, 2, 3], [4, 5, 6, 7]]



def emit_A(env, l, xT, wdr, shdr, xn_o, ffn_o, DR):
    nc = env.nc
    NMAT = 2 if l == 0 else 3
    SZW = D * INNER

    ones_f = DR["ones_f"]
    ones_r = DR["ones_r"]
    iden_f = DR["iden_f"]
    arin = DR["arin"]
    arout = DR["arout"]
    hscr = DR["hscr"]
    t2scr = DR["t2scr"]
    xnf = DR["xnf"]
    sxd = env.buf("sxd", [128, 2 * NT])
    sxnf = env.buf("sxnf", [128, 2 * NT])
    smx = env.buf("smx", [128, NT])
    sxn = env.buf("sxn", [128, NCH * NT], BF16)
    ssq = env.buf("ssq", [128, 4 * NT])
    srow = env.buf("srow", [1, 6 * NT])
    sbc = env.buf("sbc", [128, 2 * NT])
    smt = env.buf("smt", [128, NT])
    sacc = env.buf("sacc", [128, NT]) if l == 0 else None
    stok = env.buf("stok", [128, 8])
    sone_f = env.buf("sone_f", [128, 1])
    sone_r = env.buf("sone_r", [1, 128])
    siden = env.buf("siden", [128, 128])
    swsh = env.buf("swsh", [128, 2 * 1024])
    scol = env.buf("scol", [128, 96])
    srsc = env.buf("srsc", [1, 64])
    swsc = env.buf("swsc", [128, 2 * NMAT])
    swst = env.buf("swst", [128, 2 * 512])
    swq = env.buf("swq", [128, 2 * 512], BF16)
    swqs = env.buf("swqs", [128, 2 * 512])
    sups = env.buf("sups", [128, 512]) if l == 1 else None
    shs = env.buf("shs", [128, 8 * 512])
    sh2 = env.buf("sh2", [128, 2 * NT])
    st2r = env.buf("st2r", [128, 2 * NT], BF16)
    sti2 = env.buf("sti2", [128, 2 * NT], BF16)
    sgt = env.buf("sgt", [128, 8 * 512]) if l == 1 else None
    sfo = env.buf("sfo", [128, 8 * 512])

    psA = env.PS[:, 0:1024]
    psB = env.PS[:, 1024:2048]
    psC = env.PS[:, 2048:3072]
    psD = env.PS[:, 3072:3584]
    psE = env.PS[:, 3584:4096]
    PB8 = [psA[:, 0:512], psA[:, 512:1024], psB[:, 0:512], psB[:, 512:1024],
           psC[:, 0:512], psC[:, 512:1024], psD[:, :], psE[:, :]]

    RO = {"mean": 0, "a": 1 * NT, "b": 2 * NT, "tmp": 3 * NT, "tmp2": 4 * NT,
          "n2": 5 * NT, "rms": 0, "maxa": 1 * NT, "clp": 2 * NT, "g": 5 * NT,
          "d1": 0}

    def row(k, lo=0, hi=NT):
        return srow[0:1, RO[k] + lo: RO[k] + hi]

    BO = {"a": 0, "b": NT, "g": 0, "d": NT}

    px1 = env.pdma("px1")
    px2 = env.pdma("px2")
    pwl = env.pdma("pwl")
    psh = env.pdma("psh")
    phr = env.pdma("phr")
    phr2 = env.pdma("phr2")
    ptw = env.pdma("ptw")
    pstf = env.pdma("pstf", 4)
    pxr = env.pdma("pxr")

    blk = env.blk
    I, W, cnt = env.I, env.W, env.cnt

    @blk.sync
    def _(sync):
        I(sync.dma_start(out=sone_f[:], in_=ones_f[:, :]), "s_k", 16)
        I(sync.dma_start(out=sone_r[:], in_=ones_r[:, :]), "s_k", 16)
        I(sync.dma_start(out=siden[:], in_=iden_f[:, :]), "s_k", 16)
        for j in range(NCH):
            if j >= 2:
                W(sync, "s_ma", j - 1)
            px1.inc(sync.dma_start(out=sxd[:, (j % 2) * NT:(j % 2 + 1) * NT],
                                   in_=xT[128 * j:128 * (j + 1), :]), j)
        for j in range(NCH):
            if j < 2:
                W(sync, "s_ma", 15 + j)
            else:
                W(sync, "s_vx", j - 1)
            px2.inc(sync.dma_start(out=sxd[:, (j % 2) * NT:(j % 2 + 1) * NT],
                                   in_=xT[128 * j:128 * (j + 1), :]), j)


    wtiles = []
    if l == 0:
        for hb in range(16):
            for ic in range(16):
                wtiles.append((0, ic, hb))
        for ob in range(4):
            for ic in range(64):
                wtiles.append((1, ic, ob))
    else:
        for hb in range(16):
            for ic in range(16):
                wtiles.append((0, ic, hb))
            for ic in range(16):
                wtiles.append((1, ic, hb))
        for ob in range(4):
            for ic in range(64):
                wtiles.append((2, ic, ob))
    NWT = len(wtiles)

    @blk.sync
    def _(sync):
        for n, (m, ic, cb) in enumerate(wtiles):
            if n >= 2:
                W(sync, "s_wq", n - 1)
            pwl.inc(sync.dma_start(
                out=swst[:, (n % 2) * 512:(n % 2 + 1) * 512],
                in_=wdr[m][128 * ic:128 * (ic + 1), 512 * cb:512 * (cb + 1)]), n)


    shard_tiles = []
    for m in range(NMAT):
        rsh, csh = shdr[m].shape
        shard_tiles += [(m, ri, ci) for ri in range(rsh // 128)
                        for ci in range(csh // 1024)]
    NSH = len(shard_tiles)
    tiles_per_mat = [sum(1 for t in shard_tiles if t[0] == mm) for mm in range(NMAT)]

    @blk.gpsimd
    def _(gpsimd):
        for j in range(NCH):
            W(gpsimd, "s_vx", j + 1)
            ssem = "s_stxa" if (j % 2) == 0 else "s_stxb"
            I(gpsimd.dma_start(out=xn_o[128 * j:128 * (j + 1), :],
                               in_=sxnf[:, (j % 2) * NT:(j % 2 + 1) * NT]), ssem, 16)
            I(gpsimd.dma_start(out=xnf[128 * j:128 * (j + 1), :],
                               in_=sxnf[:, (j % 2) * NT:(j % 2 + 1) * NT]), ssem, 16)
        pass


    @blk.scalar
    def _(scalar):
        for j in range(NCH):
            px1.wait(scalar, j)
            if j >= 2:
                W(scalar, "s_ma", j - 1)
            I(scalar.activation(ssq[:, (j % 2) * NT:(j % 2 + 1) * NT],
                                sxd[:, (j % 2) * NT:(j % 2 + 1) * NT], AF.Square),
              "s_sq", 1)

    @blk.tensor
    def _(tensor):
        W(tensor, "s_k", 48)
        for j in range(NCH):
            px1.wait(tensor, j)
            for hf in range(2):
                tensor.matmul(psA[0:1, hf * 512:(hf + 1) * 512], sone_f[:],
                              sxd[:, (j % 2) * NT + hf * 512:(j % 2) * NT + (hf + 1) * 512],
                              start=(j == 0), stop=(j == NCH - 1))
            W(tensor, "s_sq", j + 1)
            for hf in range(2):
                mm = tensor.matmul(psB[0:1, hf * 512:(hf + 1) * 512], sone_f[:],
                                   ssq[:, (j % 2) * NT + hf * 512:(j % 2) * NT + (hf + 1) * 512],
                                   start=(j == 0), stop=(j == NCH - 1))
            I(mm, "s_ma", 1)

    @blk.vector
    def _(vector):
        W(vector, "s_ma", NCH)
        vector.tensor_scalar(row("mean"), psA[0:1, :], 1.0 / D, 0.0, ALU.mult, ALU.add)
        vector.drain()
        vector.tensor_tensor(row("tmp"), row("mean"), row("mean"), ALU.mult)
        vector.drain()
        vector.scalar_tensor_tensor(row("tmp"), psB[0:1, :], 1.0 / D, row("tmp"),
                                    ALU.mult, ALU.subtract)
        vector.drain()
        I(vector.tensor_scalar(row("tmp"), row("tmp"), 1e-5, 0.0, ALU.add, ALU.add),
          "s_v", 1)

    @blk.scalar
    def _(scalar):
        W(scalar, "s_v", 1)
        I(scalar.activation(row("tmp2"), row("tmp"), AF.Sqrt), "s_sc", 1)

    @blk.vector
    def _(vector):
        W(vector, "s_sc", 1)
        vector.reciprocal(row("a"), row("tmp2"))
        vector.drain()
        vector.tensor_tensor(row("b"), row("mean"), row("a"), ALU.mult)
        vector.drain()
        I(vector.tensor_scalar(row("b"), row("b"), -1.0, 0.0, ALU.mult, ALU.add),
          "s_v", 1)

    @blk.tensor
    def _(tensor):
        W(tensor, "s_v", 2)
        for hf in range(2):
            tensor.matmul(psC[:, hf * 512:(hf + 1) * 512], sone_r[:],
                          row("a", hf * 512, (hf + 1) * 512), start=True, stop=True)
        for hf in range(2):
            mm = tensor.matmul(psB[:, hf * 512:(hf + 1) * 512], sone_r[:],
                               row("b", hf * 512, (hf + 1) * 512), start=True, stop=True)
        I(mm, "s_mm", 1)

    @blk.vector
    def _(vector):
        W(vector, "s_mm", 1)
        vector.tensor_copy(sbc[:, BO["a"]:BO["a"] + NT], psC[:, :])
        vector.drain()
        I(vector.tensor_copy(sbc[:, BO["b"]:BO["b"] + NT], psB[:, :]), "s_v", 1)

    MB0x = cnt["s_mb"]

    @blk.vector
    def _(vector):
        W(vector, "s_v", 3)
        for j in range(NCH):
            px2.wait(vector, j)
            if j >= 2:
                W(vector, "s_stxa" if (j % 2) == 0 else "s_stxb", 32 * ((j - 2) // 2 + 1))
                W(vector, "s_sq2", j - 1)
            vector.tensor_tensor(smt[:, :], sxd[:, (j % 2) * NT:(j % 2 + 1) * NT],
                                 sbc[:, BO["a"]:BO["a"] + NT], ALU.mult)
            vector.drain()
            I(vector.tensor_tensor(sxnf[:, (j % 2) * NT:(j % 2 + 1) * NT], smt[:, :],
                                   sbc[:, BO["b"]:BO["b"] + NT], ALU.add), "s_vx", 1)
            vector.drain()
            if j >= 1:
                W(vector, "s_sq2", j)
                if j == 1:
                    vector.tensor_copy(smx[:, :], ssq[:, 0:NT])
                else:
                    vector.tensor_tensor(smx[:, :], smx[:, :],
                                         ssq[:, ((j - 1) % 2) * NT:((j - 1) % 2 + 1) * NT],
                                         ALU.max)
                vector.drain()
                I(vector.nop(), "s_vm", 1)
        W(vector, "s_sq2", NCH)
        vector.tensor_tensor(smx[:, :], smx[:, :],
                             ssq[:, ((NCH - 1) % 2) * NT:((NCH - 1) % 2 + 1) * NT],
                             ALU.max)
        vector.drain()
        I(vector.nop(), "s_v", 1)

    @blk.scalar
    def _(scalar):
        for j in range(NCH):
            W(scalar, "s_vx", j + 1)
            if j >= 2:
                if l == 0:
                    W(scalar, "s_mb", MB0x + j - 1)
                W(scalar, "s_vm", j - 1)
            I(scalar.activation(ssq[:, (j % 2) * NT:(j % 2 + 1) * NT],
                                sxnf[:, (j % 2) * NT:(j % 2 + 1) * NT], AF.Square),
              "s_sq2", 1)

    if l == 0:
        @blk.tensor
        def _(tensor):
            for j in range(NCH):
                W(tensor, "s_sq2", j + 1)
                for hf in range(2):
                    mm = tensor.matmul(psA[0:1, hf * 512:(hf + 1) * 512], sone_f[:],
                                       ssq[:, (j % 2) * NT + hf * 512:(j % 2) * NT + (hf + 1) * 512],
                                       start=(j == 0), stop=(j == NCH - 1))
                I(mm, "s_mb", 1)

    @blk.tensor
    def _(tensor):
        W(tensor, "s_v", cnt["s_v"])
        for bb in range(8):
            if bb >= 1:
                W(tensor, "s_vt", bb)
            I(tensor.transpose(psE[:, 0:128], smx[:, bb * 128:(bb + 1) * 128],
                               siden[:]), "s_mt", 1)

    @blk.vector
    def _(vector):
        for bb in range(8):
            W(vector, "s_mt", bb + 1)
            vector.reduce_max(stok[:, bb:bb + 1], psE[:, 0:128], axis=AX.X)
            vector.drain()
            I(vector.nop(), "s_vt", 1)
        I(vector.nop(), "s_v", 1)

    @blk.tensor
    def _(tensor):
        W(tensor, "s_v", cnt["s_v"])
        for bb in range(8):
            if bb >= 1:
                W(tensor, "s_vt", 8 + bb)
            I(tensor.transpose(psD[0:1, 0:128], stok[:, bb:bb + 1], siden[:]), "s_mt", 1)

    @blk.vector
    def _(vector):
        for bb in range(8):
            W(vector, "s_mt", 8 + bb + 1)
            vector.tensor_copy(row("maxa", bb * 128, (bb + 1) * 128), psD[0:1, 0:128])
            vector.drain()
            I(vector.nop(), "s_vt", 1)
        I(vector.nop(), "s_v", 1)
    MAXA1 = cnt["s_v"]

    @blk.scalar
    def _(scalar):
        W(scalar, "s_v", MAXA1)
        I(scalar.activation(row("maxa"), row("maxa"), AF.Sqrt), "s_sc", 1)
    SQMAXA1 = cnt["s_sc"]

    if l == 0:
        @blk.vector
        def _(vector):
            W(vector, "s_mb", MB0x + NCH)
            W(vector, "s_sc", SQMAXA1)
            I(vector.tensor_copy(row("n2"), psA[0:1, :]), "s_v", 1)
            vector.drain()

        @blk.scalar
        def _(scalar):
            W(scalar, "s_v", cnt["s_v"])
            I(scalar.activation(row("tmp2"), row("n2"), AF.Sqrt), "s_sc", 1)

        @blk.vector
        def _(vector):
            W(vector, "s_sc", cnt["s_sc"])
            vector.tensor_scalar(row("tmp2"), row("tmp2"), 1e-12, 0.0, ALU.max, ALU.add)
            vector.drain()
            vector.reciprocal(row("rms"), row("tmp2"))
            vector.drain()
            vector.tensor_scalar(row("rms"), row("rms"), float(np.sqrt(2048.0)), 0.0,
                                 ALU.mult, ALU.add)
            vector.drain()
            vector.tensor_tensor(row("tmp"), row("rms"), row("maxa"), ALU.mult)
            vector.drain()
            vector.tensor_scalar(row("clp"), row("tmp"), 1e-5, 0.0, ALU.max, ALU.add)
            vector.drain()
            vector.reciprocal(row("tmp2"), row("clp"))
            vector.drain()
            vector.tensor_scalar(row("tmp2"), row("tmp2"), 127.0, 0.0, ALU.mult, ALU.add)
            vector.drain()
            I(vector.tensor_tensor(row("g"), row("rms"), row("tmp2"), ALU.mult),
              "s_v", 1)
            vector.drain()
    else:
        @blk.vector
        def _(vector):
            W(vector, "s_sc", SQMAXA1)
            vector.tensor_scalar(row("d1"), row("maxa"), 1.0 / 127.0, 1e-5,
                                 ALU.mult, ALU.add)
            vector.drain()
            I(vector.reciprocal(row("g"), row("d1")), "s_v", 1)
            vector.drain()

    GROW = cnt["s_v"]

    @blk.tensor
    def _(tensor):
        W(tensor, "s_v", GROW)
        for hf in range(2):
            mm = tensor.matmul(psC[:, hf * 512:(hf + 1) * 512], sone_r[:],
                               row("g", hf * 512, (hf + 1) * 512), start=True, stop=True)
        I(mm, "s_mm", 1)

    @blk.vector
    def _(vector):
        W(vector, "s_mm", 2)
        I(vector.tensor_copy(sbc[:, BO["g"]:BO["g"] + NT], psC[:, :]), "s_v", 1)
        vector.drain()
    GBC = cnt["s_v"]

    @blk.gpsimd
    def _(gpsimd):
        for j in range(NCH):
            ssem = "s_stxa" if (j % 2) == 0 else "s_stxb"
            W(gpsimd, ssem, 32 * (j // 2 + 1))
            if j >= 2:
                W(gpsimd, "s_vx", NCH + j - 1)
            pxr.inc(gpsimd.dma_start(out=sh2[:, (j % 2) * NT:(j % 2 + 1) * NT],
                                     in_=xnf[128 * j:128 * (j + 1), :]), j)

    @blk.vector
    def _(vector):
        W(vector, "s_v", GBC)
        for j in range(NCH):
            pxr.wait(vector, j)
            vector.tensor_tensor(smt[:, :], sh2[:, (j % 2) * NT:(j % 2 + 1) * NT],
                                 sbc[:, BO["g"]:BO["g"] + NT], ALU.mult)
            vector.drain()
            I(vector.tensor_scalar(sxn[:, j * NT:(j + 1) * NT], smt[:, :], RC, -RC,
                                   ALU.add, ALU.add), "s_vx", 1)
            vector.drain()
    TI_DONE = cnt["s_vx"]

    @blk.gpsimd
    def _(gpsimd):
        for n, (m, ri, ci) in enumerate(shard_tiles):
            if n >= 2:
                W(gpsimd, "s_pr", n - 1)
            psh.inc(gpsimd.dma_start(
                out=swsh[:, (n % 2) * 1024:(n % 2 + 1) * 1024],
                in_=shdr[m][128 * ri:128 * (ri + 1), 1024 * ci:1024 * (ci + 1)]), n)

    @blk.vector
    def _(vector):
        for n in range(NSH):
            psh.wait(vector, n)
            I(vector.reduce_sum(scol[:, n:n + 1],
                                swsh[:, (n % 2) * 1024:(n % 2 + 1) * 1024],
                                axis=AX.X, apply_absolute_value=True), "s_pr", 1)
            vector.drain()
        base = 0
        for m in range(NMAT):
            vector.reduce_sum(stok[:, m:m + 1], scol[:, base:base + tiles_per_mat[m]],
                              axis=AX.X)
            vector.drain()
            base += tiles_per_mat[m]
        I(vector.nop(), "s_v", 1)
    FOLD1 = cnt["s_v"]

    @blk.tensor
    def _(tensor):
        W(tensor, "s_v", FOLD1)
        for m in range(NMAT):
            if m >= 1:
                W(tensor, "s_vt", 16 + m)
            I(tensor.transpose(psE[0:1, 0:128], stok[:, m:m + 1], siden[:]), "s_mt", 1)

    @blk.vector
    def _(vector):
        for m in range(NMAT):
            W(vector, "s_mt", 16 + m + 1)
            vector.reduce_sum(srsc[0:1, m:m + 1], psE[0:1, 0:128], axis=AX.X)
            vector.drain()
            I(vector.nop(), "s_vt", 1)
        I(vector.nop(), "s_v", 1)
    PARTS = cnt["s_v"]

    @blk.gpsimd
    def _(gpsimd):
        W(gpsimd, "s_v", PARTS)
        I(gpsimd.dma_start(out=arin[0:1, 0:NMAT], in_=srsc[0:1, 0:NMAT]), "s_a", 16)
        W(gpsimd, "s_a", 16)
        I(gpsimd.collective_compute(
            "AllReduce", ALU.add, replica_groups=[list(range(NCORES))],
            ins=[arin.ap().opt()], outs=[arout.ap().opt()]), "s_cc", 1)
        W(gpsimd, "s_cc", 1)
        I(gpsimd.dma_start(out=srsc[0:1, 8:8 + NMAT], in_=arout[0:1, 0:NMAT]),
          "s_ar", 16)

    @blk.vector
    def _(vector):
        W(vector, "s_ar", 16)
        if l == 0:
            vector.tensor_scalar(srsc[0:1, 16:16 + NMAT], srsc[0:1, 8:8 + NMAT],
                                 1.0 / SZW, 1e-5, ALU.mult, ALU.max)
        else:
            vector.tensor_scalar(srsc[0:1, 16:16 + NMAT], srsc[0:1, 8:8 + NMAT],
                                 1.0 / SZW, 1e-5, ALU.mult, ALU.add)
        vector.drain()
        vector.reciprocal(srsc[0:1, 24:24 + NMAT], srsc[0:1, 16:16 + NMAT])
        vector.drain()
        if l == 0:
            vector.tensor_scalar(row("d1"), row("clp"), srsc[0:1, 16:17], 1.0 / 127.0,
                                 ALU.mult, ALU.mult)
            vector.drain()
        I(vector.nop(), "s_v", 1)
    SCROW = cnt["s_v"]

    @blk.tensor
    def _(tensor):
        W(tensor, "s_v", SCROW)
        nb = 0
        for m in range(NMAT):
            tensor.matmul(psD[:, m:m + 1], sone_r[:], srsc[0:1, 24 + m:25 + m],
                          start=(nb == 0), stop=False)
            nb += 1
            mm = tensor.matmul(psD[:, 8 + m:9 + m], sone_r[:], srsc[0:1, 16 + m:17 + m],
                               start=False, stop=(m == NMAT - 1))
            nb += 1
        I(mm, "s_mm", 1)

    @blk.vector
    def _(vector):
        W(vector, "s_mm", 3)
        vector.tensor_copy(swsc[:, 0:NMAT], psD[:, 0:NMAT])
        vector.drain()
        I(vector.tensor_copy(swsc[:, NMAT:2 * NMAT], psD[:, 8:8 + NMAT]), "s_v", 1)
        vector.drain()
    SCBC = cnt["s_v"]

    @blk.tensor
    def _(tensor):
        W(tensor, "s_v", SCBC)
        for hf in range(2):
            mm = tensor.matmul(psA[:, hf * 512:(hf + 1) * 512], sone_r[:],
                               row("d1", hf * 512, (hf + 1) * 512), start=True, stop=True)
        I(mm, "s_mm", 1)

    @blk.vector
    def _(vector):
        W(vector, "s_mm", 4)
        I(vector.tensor_copy(sbc[:, BO["d"]:BO["d"] + NT], psA[:, :]), "s_v", 1)
        vector.drain()
    DBC = cnt["s_v"]

    NB1 = 16 * 16 if l == 0 else 16 * 32
    EV0 = cnt["s_ev"]
    GL0 = cnt["s_gl"]

    def emit_quant(vector, n):
        m = wtiles[n][0]
        pwl.wait(vector, n)
        if n >= 2:
            W(vector, "s_up", n - 1)
        sl = (n % 2) * 512
        I(vector.tensor_scalar(swqs[:, 0:512], swst[:, sl:sl + 512],
                               swsc[:, m:m + 1], -1.0, ALU.mult, ALU.max),
          "s_wq", 1)
        vector.drain()
        vector.tensor_scalar(swqs[:, 0:512], swqs[:, 0:512], 1.0, RC,
                             ALU.min, ALU.add)
        vector.drain()
        I(vector.tensor_scalar(swq[:, sl:sl + 512], swqs[:, 0:512], -RC, 0.0,
                               ALU.add, ALU.add), "s_wu", 1)
        vector.drain()

    if l == 0:
        def _stats_piece(vector, p):
            hb, b8 = p // 8, p % 8
            t4, hf = b8 % 4, b8 // 4
            W(vector, "s_hsq", p + 1)
            q4 = (p % 8) * 512
            if hb == 0 and t4 == 0:
                I(vector.tensor_copy(smt[:, hf * 512:(hf + 1) * 512],
                                     ssq[:, q4:q4 + 512]), "s_mh", 1)
            else:
                I(vector.tensor_tensor(smt[:, hf * 512:(hf + 1) * 512],
                                       smt[:, hf * 512:(hf + 1) * 512],
                                       ssq[:, q4:q4 + 512], ALU.max), "s_mh", 1)
            vector.drain()
            if hb == 0 and t4 == 0:
                I(vector.tensor_copy(sacc[:, hf * 512:(hf + 1) * 512],
                                     ssq[:, q4:q4 + 512]), "s_va", 1)
            else:
                I(vector.tensor_tensor(sacc[:, hf * 512:(hf + 1) * 512],
                                       sacc[:, hf * 512:(hf + 1) * 512],
                                       ssq[:, q4:q4 + 512], ALU.add), "s_va", 1)
            vector.drain()

        def emit_evac(vector, hb):
            W(vector, "s_up", (hb + 1) * 16)
            for b8 in range(8):
                t4, hf = b8 % 4, b8 // 4
                if hb >= 1:
                    W(vector, "s_gl", GL0 + (hb - 1) * 8 + b8 + 1)
                I(vector.tensor_tensor(sfo[:, b8 * 512:(b8 + 1) * 512], PB8[b8],
                                       sbc[:, BO["d"] + hf * 512:BO["d"] + (hf + 1) * 512],
                                       ALU.mult), "s_ev", 1)
                vector.drain()
            if hb >= 1:
                for b8 in range(8):
                    _stats_piece(vector, (hb - 1) * 8 + b8)

        @blk.vector
        def _(vector):
            for hb in range(16):
                if hb >= 1:
                    emit_evac(vector, hb - 1)
                for n in range(hb * 16, hb * 16 + 16):
                    emit_quant(vector, n)
            emit_evac(vector, 15)
            for b8 in range(8):
                _stats_piece(vector, 15 * 8 + b8)

        @blk.scalar
        def _(scalar):
            for hb in range(16):
                for b8 in range(8):
                    p = hb * 8 + b8
                    t4, hf = b8 % 4, b8 // 4
                    W(scalar, "s_ev", EV0 + p + 1)
                    scalar.activation(sfo[:, b8 * 512:(b8 + 1) * 512],
                                      sfo[:, b8 * 512:(b8 + 1) * 512], AF.Gelu)
                    scalar.drain()
                    sl8 = t4 * 2 + hf
                    if hb >= 1:
                        W(scalar, "s_sth", 16 * hb)
                    I(scalar.activation(shs[:, sl8 * 512:(sl8 + 1) * 512],
                                        sfo[:, b8 * 512:(b8 + 1) * 512], AF.Gelu),
                      "s_gl", 1)
                    scalar.drain()
                    if p >= 8:
                        W(scalar, "s_va", p - 7)
                        W(scalar, "s_mh", p - 7)
                    I(scalar.activation(ssq[:, (p % 8) * 512:(p % 8) * 512 + 512],
                                        shs[:, sl8 * 512:(sl8 + 1) * 512], AF.Square),
                      "s_hsq", 1)
                    scalar.drain()
    else:
        def _tree_piece(vector, p):
            hb, b8 = p // 8, p % 8
            t4, hf = b8 % 4, b8 // 4
            W(vector, "s_hsq", p + 1)
            q4 = (p % 8) * 512
            if hb == 0 and t4 == 0:
                I(vector.tensor_copy(smt[:, hf * 512:(hf + 1) * 512],
                                     ssq[:, q4:q4 + 512]), "s_mh", 1)
            else:
                I(vector.tensor_tensor(smt[:, hf * 512:(hf + 1) * 512],
                                       smt[:, hf * 512:(hf + 1) * 512],
                                       ssq[:, q4:q4 + 512], ALU.max), "s_mh", 1)
            vector.drain()

        def emit_evac_gate(vector, hb):
            W(vector, "s_up", hb * 32 + 16)
            for b8 in range(8):
                if hb >= 1:
                    W(vector, "s_gl", GL0 + (hb - 1) * 8 + b8 + 1)
                I(vector.scalar_tensor_tensor(
                    sfo[:, b8 * 512:(b8 + 1) * 512], PB8[b8],
                    swsc[:, NMAT + 0:NMAT + 1],
                    sbc[:, BO["d"] + (b8 // 4) * 512:BO["d"] + (b8 // 4 + 1) * 512],
                    ALU.mult, ALU.mult), "s_ev", 1)
                vector.drain()

        def emit_evac_up(vector, hb):
            W(vector, "s_up", hb * 32 + 32)
            for b8 in range(8):
                p = hb * 8 + b8
                t4, hf = b8 % 4, b8 // 4
                W(vector, "s_gl", GL0 + hb * 8 + b8 + 1)
                vector.scalar_tensor_tensor(
                    sups[:, :], PB8[b8], swsc[:, NMAT + 1:NMAT + 2],
                    sbc[:, BO["d"] + hf * 512:BO["d"] + (hf + 1) * 512],
                    ALU.mult, ALU.mult)
                vector.drain()
                I(vector.nop(), "s_ev", 1)
                if hb >= 1:
                    W(vector, "s_sth", 16 * hb)
                if p >= 8:
                    W(vector, "s_hsq", p - 7)
                sl8 = t4 * 2 + hf
                I(vector.tensor_tensor(shs[:, sl8 * 512:(sl8 + 1) * 512],
                                       sups[:, :], sgt[:, b8 * 512:(b8 + 1) * 512],
                                       ALU.mult), "s_pc", 1)
                vector.drain()
                if p >= 1:
                    _tree_piece(vector, p - 1)

        @blk.vector
        def _(vector):
            for hb in range(16):
                if hb >= 1:
                    emit_evac_up(vector, hb - 1)
                for n in range(hb * 32, hb * 32 + 16):
                    emit_quant(vector, n)
                emit_evac_gate(vector, hb)
                for n in range(hb * 32 + 16, hb * 32 + 32):
                    emit_quant(vector, n)
            emit_evac_up(vector, 15)
            _tree_piece(vector, 127)

        @blk.scalar
        def _(scalar):
            for hb in range(16):
                for b8 in range(8):
                    p = hb * 8 + b8
                    W(scalar, "s_ev", EV0 + hb * 16 + b8 + 1)
                    if hb >= 1:
                        W(scalar, "s_pc", (hb - 1) * 8 + b8 + 1)
                    I(scalar.activation(sgt[:, b8 * 512:(b8 + 1) * 512],
                                        sfo[:, b8 * 512:(b8 + 1) * 512], AF.Silu),
                      "s_gl", 1)
                    scalar.drain()
                    W(scalar, "s_pc", p + 1)
                    if p >= 8:
                        W(scalar, "s_mh", p - 7)
                    sl8 = (b8 % 4) * 2 + b8 // 4
                    I(scalar.activation(ssq[:, (p % 8) * 512:(p % 8) * 512 + 512],
                                        shs[:, sl8 * 512:(sl8 + 1) * 512], AF.Square),
                      "s_hsq", 1)
                    scalar.drain()

    @blk.tensor
    def _(tensor):
        W(tensor, "s_vx", TI_DONE)
        if l == 0:
            for hb in range(16):
                for ic in range(16):
                    n = hb * 16 + ic
                    W(tensor, "s_wu", n + 1)
                    sl = (n % 2) * 512
                    for b8 in range(8):
                        t4, hf = b8 % 4, b8 // 4
                        if ic == 0 and hb >= 1:
                            W(tensor, "s_ev", EV0 + (hb - 1) * 8 + b8 + 1)
                        mm = tensor.matmul(
                            PB8[b8], swq[:, sl + t4 * 128:sl + (t4 + 1) * 128],
                            sxn[:, ic * NT + hf * 512:ic * NT + (hf + 1) * 512],
                            start=(ic == 0), stop=(ic == 15))
                    I(mm, "s_up", 1)
        else:
            for hb in range(16):
                for g_or_u in range(2):
                    for ic in range(16):
                        n = hb * 32 + g_or_u * 16 + ic
                        W(tensor, "s_wu", n + 1)
                        sl = (n % 2) * 512
                        for b8 in range(8):
                            t4, hf = b8 % 4, b8 // 4
                            if ic == 0:
                                prev = hb * 16 if g_or_u == 1 else (hb - 1) * 16 + 8
                                if hb >= 1 or g_or_u == 1:
                                    W(tensor, "s_ev", EV0 + prev + b8 + 1)
                            mm = tensor.matmul(
                                PB8[b8], swq[:, sl + t4 * 128:sl + (t4 + 1) * 128],
                                sxn[:, ic * NT + hf * 512:ic * NT + (hf + 1) * 512],
                                start=(ic == 0), stop=(ic == 15))
                        I(mm, "s_up", 1)

    PC_SEM = "s_gl" if l == 0 else "s_pc"
    PC0 = GL0 if l == 0 else 0
    TPB = 16 if l == 0 else 32

    @blk.gpsimd
    def _(gpsimd):
        def h_store(hb):
            W(gpsimd, PC_SEM, PC0 + (hb + 1) * 8)
            dst = hscr.ap().rearrange("(t p) e -> p t e", p=128)[:, 4 * hb:4 * hb + 4, :]
            srcap = shs.ap().rearrange("p (t e) -> p t e", t=4)
            I(gpsimd.dma_start(out=dst, in_=srcap), "s_sth", 16)

        for hb in range(16):
            h_store(hb)

    MT0 = cnt["s_mt"]
    VT0 = cnt["s_vt"]
    TREE_DONE = cnt["s_mh"]

    if l == 0:
        @blk.tensor
        def _(tensor):
            W(tensor, "s_va", cnt["s_va"])
            W(tensor, "s_ev", cnt["s_ev"])
            for hf in range(2):
                mm = tensor.matmul(psA[0:1, hf * 512:(hf + 1) * 512], sone_f[:],
                                   sacc[:, hf * 512:(hf + 1) * 512],
                                   start=True, stop=True)
            I(mm, "s_mb", 1)
        MB1 = cnt["s_mb"]

    @blk.tensor
    def _(tensor):
        W(tensor, "s_mh", TREE_DONE)
        for bb in range(8):
            if bb >= 1:
                W(tensor, "s_vt", VT0 + bb)
            I(tensor.transpose(psE[:, 0:128], smt[:, bb * 128:(bb + 1) * 128],
                               siden[:]), "s_mt", 1)

    @blk.vector
    def _(vector):
        for bb in range(8):
            W(vector, "s_mt", MT0 + bb + 1)
            vector.reduce_max(stok[:, bb:bb + 1], psE[:, 0:128], axis=AX.X)
            vector.drain()
            I(vector.nop(), "s_vt", 1)
        I(vector.nop(), "s_v", 1)

    @blk.tensor
    def _(tensor):
        W(tensor, "s_v", cnt["s_v"])
        for bb in range(8):
            if bb >= 1:
                W(tensor, "s_vt", VT0 + 8 + bb)
            I(tensor.transpose(psD[0:1, 0:128], stok[:, bb:bb + 1], siden[:]), "s_mt", 1)

    @blk.vector
    def _(vector):
        for bb in range(8):
            W(vector, "s_mt", MT0 + 8 + bb + 1)
            vector.tensor_copy(row("maxa", bb * 128, (bb + 1) * 128), psD[0:1, 0:128])
            vector.drain()
            I(vector.nop(), "s_vt", 1)
        I(vector.nop(), "s_v", 1)
    MAXH = cnt["s_v"]

    @blk.scalar
    def _(scalar):
        W(scalar, "s_v", MAXH)
        I(scalar.activation(row("maxa"), row("maxa"), AF.Sqrt), "s_sc", 1)
    SQMAXH = cnt["s_sc"]

    if l == 0:
        @blk.vector
        def _(vector):
            W(vector, "s_mb", MB1)
            I(vector.tensor_copy(row("n2"), psA[0:1, :]), "s_v", 1)
            vector.drain()

        @blk.scalar
        def _(scalar):
            W(scalar, "s_v", cnt["s_v"])
            I(scalar.activation(row("tmp2"), row("n2"), AF.Sqrt), "s_sc", 1)
        SC2 = cnt["s_sc"]

        @blk.vector
        def _(vector):
            W(vector, "s_sc", SC2)
            W(vector, "s_v", MAXH)
            W(vector, "s_sc", SC2)
            vector.tensor_scalar(row("tmp2"), row("tmp2"), 1e-12, 0.0, ALU.max, ALU.add)
            vector.drain()
            vector.reciprocal(row("rms"), row("tmp2"))
            vector.drain()
            vector.tensor_scalar(row("rms"), row("rms"), float(np.sqrt(8192.0)), 0.0,
                                 ALU.mult, ALU.add)
            vector.drain()
            vector.tensor_tensor(row("tmp"), row("rms"), row("maxa"), ALU.mult)
            vector.drain()
            vector.tensor_scalar(row("clp"), row("tmp"), 1e-5, 0.0, ALU.max, ALU.add)
            vector.drain()
            vector.reciprocal(row("tmp2"), row("clp"))
            vector.drain()
            vector.tensor_scalar(row("tmp2"), row("tmp2"), 127.0, 0.0, ALU.mult, ALU.add)
            vector.drain()
            vector.tensor_tensor(row("g"), row("rms"), row("tmp2"), ALU.mult)
            vector.drain()
            I(vector.tensor_scalar(row("d1"), row("clp"), srsc[0:1, 17:18], 1.0 / 127.0,
                                   ALU.mult, ALU.mult), "s_v", 1)
            vector.drain()
    else:
        @blk.vector
        def _(vector):
            W(vector, "s_v", MAXH)
            W(vector, "s_sc", SQMAXH)
            vector.tensor_scalar(row("tmp"), row("maxa"), 1.0 / 127.0, 1e-5,
                                 ALU.mult, ALU.add)
            vector.drain()
            vector.reciprocal(row("g"), row("tmp"))
            vector.drain()
            I(vector.tensor_scalar(row("d1"), row("tmp"), srsc[0:1, 18:19], 0.0,
                                   ALU.mult, ALU.add), "s_v", 1)
            vector.drain()
    G2ROW = cnt["s_v"]

    @blk.tensor
    def _(tensor):
        W(tensor, "s_v", G2ROW)
        for hf in range(2):
            tensor.matmul(psC[:, hf * 512:(hf + 1) * 512], sone_r[:],
                          row("g", hf * 512, (hf + 1) * 512), start=True, stop=True)
        for hf in range(2):
            mm = tensor.matmul(psB[:, hf * 512:(hf + 1) * 512], sone_r[:],
                               row("d1", hf * 512, (hf + 1) * 512), start=True, stop=True)
        I(mm, "s_mm", 1)

    @blk.vector
    def _(vector):
        W(vector, "s_mm", cnt["s_mm"])
        vector.tensor_copy(sbc[:, BO["g"]:BO["g"] + NT], psC[:, :])
        vector.drain()
        I(vector.tensor_copy(sbc[:, BO["d"]:BO["d"] + NT], psB[:, :]), "s_v", 1)
        vector.drain()
    G2BC = cnt["s_v"]

    VX2_pre = cnt["s_vx"]
    U0 = cnt["s_up"]
    NOFF = NB1
    EV2 = cnt["s_ev"]

    @blk.gpsimd
    def _(gpsimd):
        for ic in range(HCH + 2):
            if ic < HCH:
                if ic >= 2:
                    W(gpsimd, "s_vx", VX2_pre + ic - 1)
                W(gpsimd, "s_sth", 16 * (ic // 4 + 1))
                phr.inc(gpsimd.dma_start(out=sh2[:, (ic % 2) * NT:(ic % 2 + 1) * NT],
                                         in_=hscr[128 * ic:128 * (ic + 1), :]), ic)
            if ic >= 2:
                jc = ic - 2
                W(gpsimd, "s_vx", VX2_pre + jc + 1)
                ptw.inc(gpsimd.dma_start(out=t2scr[128 * jc:128 * (jc + 1), :],
                                         in_=sti2[:, (jc % 2) * NT:(jc % 2 + 1) * NT]), jc)
        for ob in range(1, 4):
            for ic in range(HCH):
                m = (ob - 1) * HCH + ic
                ptw.wait(gpsimd, ic)
                if m >= 2:
                    W(gpsimd, "s_up", U0 + HCH + m - 1)
                phr2.inc(gpsimd.dma_start(out=st2r[:, (m % 2) * NT:(m % 2 + 1) * NT],
                                          in_=t2scr[128 * ic:128 * (ic + 1), :]), m)

    def emit_round2(vector, ic):
        phr.wait(vector, ic)
        vector.tensor_tensor(smt[:, :], sh2[:, (ic % 2) * NT:(ic % 2 + 1) * NT],
                             sbc[:, BO["g"]:BO["g"] + NT], ALU.mult)
        vector.drain()
        if ic >= 2:
            W(vector, "s_up", U0 + ic - 1)
            ptw.wait(vector, ic - 2)
        I(vector.tensor_scalar(sti2[:, (ic % 2) * NT:(ic % 2 + 1) * NT],
                               smt[:, :], RC, -RC, ALU.add, ALU.add), "s_vx", 1)
        vector.drain()

    def emit_evac2(vector, ob):
        W(vector, "s_up", U0 + (ob + 1) * HCH)
        for t4 in range(4):
            ot = ob * 4 + t4
            if ob >= 1:
                pstf.wait(vector, (ob - 1) * 4 + t4)
            for hf in range(2):
                b8 = hf * 4 + t4
                I(vector.tensor_tensor(
                    sfo[:, t4 * 1024 + hf * 512:t4 * 1024 + (hf + 1) * 512],
                    PB8[b8],
                    sbc[:, BO["d"] + hf * 512:BO["d"] + (hf + 1) * 512], ALU.mult),
                  "s_ev", 1)
                vector.drain()
            I(vector.nop(), "s_vf", 1)

    @blk.vector
    def _(vector):
        W(vector, "s_v", G2BC)
        for ob in range(4):
            if ob >= 1:
                emit_evac2(vector, ob - 1)
            for ic in range(HCH):
                emit_quant(vector, NOFF + ob * HCH + ic)
                if ob == 0:
                    emit_round2(vector, ic)
        emit_evac2(vector, 3)

    @blk.tensor
    def _(tensor):
        for ob in range(4):
            for ic in range(HCH):
                n = NOFF + ob * HCH + ic
                W(tensor, "s_wu", n + 1)
                if ob == 0:
                    W(tensor, "s_vx", VX2_pre + ic + 1)
                    rhs_base = (ic % 2) * NT
                    rhs_t = sti2
                else:
                    m = (ob - 1) * HCH + ic
                    phr2.wait(tensor, m)
                    rhs_base = (m % 2) * NT
                    rhs_t = st2r
                sl = (n % 2) * 512
                for b8 in range(8):
                    t4, hf = b8 % 4, b8 // 4
                    if ic == 0 and ob >= 1:
                        W(tensor, "s_ev", EV2 + (ob - 1) * 8 + (b8 % 4) * 2 + b8 // 4 + 1)
                    mm = tensor.matmul(
                        PB8[b8], swq[:, sl + t4 * 128:sl + (t4 + 1) * 128],
                        rhs_t[:, rhs_base + hf * 512:rhs_base + (hf + 1) * 512],
                        start=(ic == 0), stop=(ic == HCH - 1))
                I(mm, "s_up", 1)

    @blk.scalar
    def _(scalar):
        for ot in range(16):
            W(scalar, "s_vf", ot + 1)
            pstf.inc(scalar.dma_start(out=ffn_o[128 * ot:128 * (ot + 1), :],
                                      in_=sfo[:, (ot % 4) * 1024:(ot % 4) * 1024 + NT]), ot)
        for t4 in range(4):
            pstf.wait(scalar, 12 + t4)

    @blk.sync
    def _(sync):
        W(sync, "s_stxa")
        W(sync, "s_stxb")


def emit_B(env, l, xst, wq_, wk_, wv_, wo, bands, Tpart, DR):
    nc = env.nc
    SEQ = 4096
    SC = 1.0 / np.sqrt(128.0) if l == 0 else 1.0 / 64.0

    wsl = bsl = None
    ones_f = DR["ones_f"]
    ones_r = DR["ones_r"]
    ones_fb = DR["ones_fb"]
    ones_rb = DR["ones_rb"]
    iden_b = DR["iden_b"]
    iden_f = DR["iden_f"]

    sxn = env.buf("sxn", [128, 16 * 512], BF16)
    swb = env.buf("swb", [128, 16 * 1536], BF16)
    swo = env.buf("swo", [128, 4 * 2048], BF16)
    sq = env.buf("sq", [128, 2 * SEQ], BF16)
    sk = env.buf("sk", [128, 2 * SEQ], BF16)
    sv = env.buf("sv", [128, 2 * 32 * 130], BF16)
    sexp = env.buf("sexp", [128, 2 * 512], BF16)
    sctx = env.buf("sctx", [128, 32 * 128], BF16)
    sctxT = env.buf("sctxT", [128, 4 * SEQ], BF16)
    sfoB = env.buf("sfoB", [128, 8 * 512])
    ssc = sfoB
    sbnd = env.buf("sbnd", [128, 4 * 512], BF16) if l == 1 else None
    srw = env.buf("srw", [1, 64])
    sbias = env.buf("sbias", [128, 2])
    szq = env.buf("szq", [128, 2])
    sbq = env.buf("sbq", [128, 12]) if l == 0 else None
    sbvr = env.buf("sbvr", [1, 512], BF16) if l == 0 else None
    ssuf = env.buf("ssuf", [1, 9 * 129]) if l == 1 else None
    ssufb = env.buf("ssufb", [1, 8 * 129], BF16) if l == 1 else None
    sone_f = env.buf("sone_f", [128, 1])
    sone_r = env.buf("sone_r", [1, 128])
    sone_fb = env.buf("sone_fb", [128, 1], BF16)
    sone_rb = env.buf("sone_rb", [1, 128], BF16)
    siden = env.buf("siden", [128, 128], BF16)
    siden_f = env.buf("siden_f", [128, 128])

    PB = [env.PB(i) for i in range(8)]

    pxn = env.pdma("pxn", 1)
    pst = env.pdma("pst", 8)

    blk = env.blk
    I, W, cnt = env.I, env.W, env.cnt

    @blk.gpsimd
    def _(gpsimd):
        if l == 0:
            for j in range(NCH):
                I(gpsimd.dma_start(out=swb[:, j * 1536:(j + 1) * 1536],
                                   in_=wsl[128 * j:128 * (j + 1), :]), "s_w", 16)
        else:
            for j in range(NCH):
                I(gpsimd.dma_start(out=swb[:, j * 1536:j * 1536 + 512],
                                   in_=wq_[128 * j:128 * (j + 1), :]), "s_w", 16)
                I(gpsimd.dma_start(out=swb[:, j * 1536 + 512:j * 1536 + 1024],
                                   in_=wk_[128 * j:128 * (j + 1), :]), "s_w", 16)
                I(gpsimd.dma_start(out=swb[:, j * 1536 + 1024:j * 1536 + 1536],
                                   in_=wv_[128 * j:128 * (j + 1), :]), "s_w", 16)
        for j in range(4):
            I(gpsimd.dma_start(out=swo[:, j * 2048:(j + 1) * 2048],
                               in_=wo[128 * j:128 * (j + 1), :]), "s_w", 16)
        if l == 0:
            for t in range(12):
                I(gpsimd.dma_start(out=sbq[:, t:t + 1],
                                   in_=bsl[128 * t:128 * (t + 1), :]), "s_w", 16)
        gpsimd.memset(sv.ap().rearrange("p (t w) -> p t w", w=130)[:, :, 128:129], 1.0)
        gpsimd.drain()
        I(gpsimd.nop(), "s_bs", 1)

    @blk.sync
    def _(sync):
        I(sync.dma_start(out=sone_f[:], in_=ones_f[:, :]), "s_k", 16)
        I(sync.dma_start(out=sone_r[:], in_=ones_r[:, :]), "s_k", 16)
        I(sync.dma_start(out=sone_fb[:], in_=ones_fb[:, :]), "s_k", 16)
        I(sync.dma_start(out=sone_rb[:], in_=ones_rb[:, :]), "s_k", 16)
        I(sync.dma_start(out=siden[:], in_=iden_b[:, :]), "s_k", 16)
        I(sync.dma_start(out=siden_f[:], in_=iden_f[:, :]), "s_k", 16)
        if l == 1:
            I(sync.dma_start(out=sbnd[:, :], in_=bands[:, :]), "s_k", 16)
        K_ALL = cnt["s_k"]
        for hp in range(2):
            for tb in range(8):
                i = hp * 8 + tb
                if i >= 1:
                    W(sync, "s_mq", i * 8)
                src = xst.ap().rearrange("(k j p) t -> p k j t", k=4, p=128)[
                    :, tb // 2, :, 512 * (tb % 2):512 * (tb % 2) + 512]
                pxn.inc(sync.dma_start(out=sxn.ap().rearrange("p (j t) -> p j t", t=512),
                                       in_=src), i)

    def qkv_phase(hp):
        @blk.tensor
        def _(tensor):
            if hp == 0:
                W(tensor, "s_k", 96 if l == 0 else 112)
                W(tensor, "s_w", cnt["s_w"])
                W(tensor, "s_bs", 1)
                if l == 0:
                    W(tensor, "s_vb", cnt["s_vb"])
            else:
                W(tensor, "s_vt", cnt["s_vt"])
                W(tensor, "s_cv", cnt["s_cv"])
                for bk in range(8):
                    head_state["ctxT_free"][bk % 4] = None
                    head_state["av_free"][bk % 4] = None
            for tb in range(8):
                i = hp * 8 + tb
                pxn.wait(tensor, i)
                sb = 0
                if i >= 1:
                    W(tensor, "s_vq", i * 8)
                for h2 in range(2):
                    hh = hp * 2 + h2
                    for j in range(NCH):
                        mm = tensor.matmul(PB[h2][:, :],
                                           swb[:, j * 1536 + hh * 128:j * 1536 + (hh + 1) * 128],
                                           sxn[:, sb + j * 512:sb + (j + 1) * 512],
                                           start=(j == 0), stop=(j == NCH - 1))
                    I(mm, "s_mq", 1)
                    for j in range(NCH):
                        mm = tensor.matmul(PB[2 + h2][:, :],
                                           swb[:, j * 1536 + 512 + hh * 128:j * 1536 + 512 + (hh + 1) * 128],
                                           sxn[:, sb + j * 512:sb + (j + 1) * 512],
                                           start=(j == 0), stop=(j == NCH - 1))
                    I(mm, "s_mq", 1)
                for tt in range(4):
                    for j in range(NCH):
                        last = (j == NCH - 1) and (l == 1)
                        mm = tensor.matmul(PB[4 + tt][:, 0:256],
                                           sxn[:, sb + j * 512 + tt * 128:sb + j * 512 + (tt + 1) * 128],
                                           swb[:, j * 1536 + 1024 + hp * 256:j * 1536 + 1024 + (hp + 1) * 256],
                                           start=(j == 0), stop=last)
                    if l == 0:
                        mm = tensor.matmul(PB[4 + tt][:, 0:256], sone_rb[:],
                                           sbvr[0:1, hp * 256:(hp + 1) * 256],
                                           start=False, stop=True)
                    I(mm, "s_mq", 1)

        @blk.vector
        def _(vector):
            for tb in range(8):
                i = hp * 8 + tb
                base = i * 8
                for h2 in range(2):
                    W(vector, "s_mq", base + 2 * h2 + 1)
                    if l == 0:
                        vector.tensor_scalar(sq[:, h2 * SEQ + tb * 512:h2 * SEQ + (tb + 1) * 512],
                                             PB[h2][:, :], sbq[:, (hp * 2 + h2):(hp * 2 + h2) + 1],
                                             0.0, ALU.add, ALU.add)
                    else:
                        vector.tensor_copy(sq[:, h2 * SEQ + tb * 512:h2 * SEQ + (tb + 1) * 512],
                                           PB[h2][:, :])
                    vector.drain()
                    I(vector.nop(), "s_vq", 1)
                    W(vector, "s_mq", base + 2 * h2 + 2)
                    if l == 0:
                        vector.tensor_scalar(sk[:, h2 * SEQ + tb * 512:h2 * SEQ + (tb + 1) * 512],
                                             PB[2 + h2][:, :], sbq[:, 4 + hp * 2 + h2:5 + hp * 2 + h2],
                                             0.0, ALU.add, ALU.add)
                    else:
                        vector.tensor_copy(sk[:, h2 * SEQ + tb * 512:h2 * SEQ + (tb + 1) * 512],
                                           PB[2 + h2][:, :])
                    vector.drain()
                    I(vector.nop(), "s_vq", 1)
                for tt in range(4):
                    W(vector, "s_mq", base + 5 + tt)
                    for h2 in range(2):
                        vector.tensor_copy(
                            sv.ap().rearrange("p (s t w) -> p s t w", s=2, w=130)[:, h2, tb * 4 + tt, 0:128],
                            PB[4 + tt][:, h2 * 128:(h2 + 1) * 128])
                        vector.drain()
                    I(vector.nop(), "s_vq", 1)

    if l == 0:
        @blk.vector
        def _(vector):
            W(vector, "s_w", cnt["s_w"])
            I(vector.nop(), "s_vb", 1)

        @blk.tensor
        def _(tensor):
            W(tensor, "s_vb", 1)
            W(tensor, "s_k", 96)
            for t in range(4):
                mm = tensor.matmul(PB[t][0:1, 0:128],
                                   sbq[:, 8 + t:9 + t], siden_f[:],
                                   start=True, stop=True)
                I(mm, "s_mb", 1)

        @blk.vector
        def _(vector):
            W(vector, "s_mb", 4)
            for t in range(4):
                vector.tensor_copy(sbvr[0:1, t * 128:(t + 1) * 128], PB[t][0:1, 0:128])
                vector.drain()
            I(vector.nop(), "s_vb", 1)

    def bounds_head(hp, h2):
        hh = hp * 2 + h2
        MB_b = cnt["s_mb"]
        VB0 = cnt["s_vb"]

        @blk.scalar
        def _(scalar):
            W(scalar, "s_vq", 64 * (hp + 1))
            for side in range(2):
                for pc in range(8):
                    i = side * 8 + pc
                    W(scalar, "s_mb", MB_b + i)
                    srcbuf = sq if side == 0 else sk
                    I(scalar.activation(ssc[:, 3584:4096], srcbuf[:, h2 * SEQ + pc * 512:h2 * SEQ + (pc + 1) * 512],
                                        AF.Square), "s_sb", 1)

        SB0 = cnt["s_sb"] - 16
        MB0 = cnt["s_mb"]

        @blk.tensor
        def _(tensor):
            for bk in range(2):
                if head_state["ctxT_free"][bk] is not None:
                    W(tensor, "s_vt", head_state["ctxT_free"][bk])
                    head_state["ctxT_free"][bk] = None
            for i in range(16):
                W(tensor, "s_sb", SB0 + i + 1)
                if i >= 1:
                    W(tensor, "s_vb", VB0 + i)
                I(tensor.matmul(PB[0][0:1, 0:512], sone_f[:], ssc[:, 3584:4096],
                                start=True, stop=True), "s_mb", 1)

        @blk.vector
        def _(vector):
            for i in range(16):
                W(vector, "s_mb", MB0 + i + 1)
                vector.reduce_max(srw[0:1, 8 + i % 2:9 + i % 2] if False else srw[0:1, 8:9],
                                  PB[0][0:1, 0:512], axis=AX.X)
                vector.drain()
                side = i // 8
                if i % 8 == 0:
                    vector.tensor_copy(srw[0:1, side:side + 1], srw[0:1, 8:9])
                else:
                    vector.tensor_tensor(srw[0:1, side:side + 1], srw[0:1, side:side + 1],
                                         srw[0:1, 8:9], ALU.max)
                vector.drain()
                I(vector.nop(), "s_vb", 1)
            I(vector.tensor_tensor(srw[0:1, 2:3], srw[0:1, 0:1], srw[0:1, 1:2],
                                   ALU.mult), "s_vb", 1)
            vector.drain()

        @blk.scalar
        def _(scalar):
            W(scalar, "s_vb", cnt["s_vb"])
            I(scalar.activation(srw[0:1, 2:3], srw[0:1, 2:3], AF.Sqrt), "s_sb", 1)
            scalar.drain()
            if l == 1:
                I(scalar.activation(srw[0:1, 4:5], srw[0:1, 2:3], AF.Exp,
                                    scale=-SC), "s_sb", 1)
                scalar.drain()

        @blk.vector
        def _(vector):
            W(vector, "s_sb", cnt["s_sb"])
            I(vector.tensor_scalar(srw[0:1, 3:4], srw[0:1, 2:3], -SC, 0.0,
                                   ALU.mult, ALU.add), "s_vb", 1)
            vector.drain()

        @blk.tensor
        def _(tensor):
            W(tensor, "s_vb", cnt["s_vb"])
            I(tensor.matmul(PB[2 + h2][:, 0:1], sone_r[:], srw[0:1, 3:4],
                            start=True, stop=True), "s_mb", 1)

        @blk.vector
        def _(vector):
            W(vector, "s_mb", cnt["s_mb"])
            I(vector.tensor_copy(sbias[:, h2:h2 + 1], PB[2 + h2][:, 0:1]), "s_vb", 1)
            vector.drain()

    def suffix_head(hp, h2):
        VB_b = cnt["s_vb"]

        @blk.tensor
        def _(tensor):
            for b in range(8):
                if b >= 1:
                    W(tensor, "s_vb", VB_b + b)
                for t in range(4):
                    mm = tensor.matmul(
                        PB[1][0:1, 0:129], sone_fb[:],
                        sv.ap().rearrange("p (s t w) -> p s t w", s=2, w=130)[:, h2, b * 4 + t, 0:129],
                        start=(t == 0), stop=(t == 3))
                I(mm, "s_mb", 1)

        MB0 = cnt["s_mb"] - 8

        @blk.vector
        def _(vector):
            for b in range(8):
                W(vector, "s_mb", MB0 + b + 1)
                vector.tensor_copy(ssuf[0:1, b * 129:(b + 1) * 129], PB[1][0:1, 0:129])
                vector.drain()
                I(vector.nop(), "s_vb", 1)
            vector.tensor_copy(ssuf[0:1, 8 * 129:9 * 129], ssuf[0:1, 7 * 129:8 * 129])
            vector.drain()
            I(vector.tensor_scalar(ssufb[0:1, 6 * 129:7 * 129], ssuf[0:1, 8 * 129:9 * 129],
                                   srw[0:1, 4:5], 0.0, ALU.mult, ALU.add), "s_vb", 1)
            vector.drain()
            for Q in range(5, -1, -1):
                vector.tensor_tensor(ssuf[0:1, 8 * 129:9 * 129], ssuf[0:1, 8 * 129:9 * 129],
                                     ssuf[0:1, (Q + 1) * 129:(Q + 2) * 129], ALU.add)
                vector.drain()
                I(vector.tensor_scalar(ssufb[0:1, Q * 129:(Q + 1) * 129],
                                       ssuf[0:1, 8 * 129:9 * 129],
                                       srw[0:1, 4:5], 0.0, ALU.mult, ALU.add), "s_vb", 1)
                vector.drain()

    head_state = {"av_free": [None] * 4, "bank_free": [None] * 4, "ek": 0,
                  "ctxT_free": [None] * 4}

    def attn_head(hp, h2):
        hh = hp * 2 + h2
        KMs = [32] * 8 if l == 0 else [4 * (Q + 1) for Q in range(8)]
        pre = [0]
        for Q in range(8):
            pre.append(pre[-1] + KMs[Q])
        KMtot = pre[-1]
        ek0 = head_state["ek"]
        ak0 = cnt["s_ak"]
        cv0 = cnt["s_cv"]
        mk0 = cnt["s_mk"]
        vk0 = cnt["s_vk"]
        svbase = h2 * 32 * 130

        def av_inc(Q, kc):
            sfx = Q if l == 1 else 0
            return ak0 + pre[Q] + kc + 1 + sfx

        def emit_av(tensor, Q, kc, KM):
            W(tensor, "s_ek", ek0 + pre[Q] + kc + 1)
            for qt in range(4):
                if kc == 0 and head_state["av_free"][qt] is not None:
                    W(tensor, "s_cv", head_state["av_free"][qt])
                last = (kc == KM - 1) and (l == 0 or Q == 7)
                mm = tensor.matmul(
                    PB[4 + qt][:, 0:129],
                    sexp[:, (kc % 2) * 512 + qt * 128:(kc % 2) * 512 + (qt + 1) * 128],
                    sv[:, (svbase + kc * 130):(svbase + kc * 130 + 129)],
                    start=(kc == 0), stop=last)
            I(mm, "s_ak", 1)

        @blk.tensor
        def _(tensor):
            W(tensor, "s_vb", cnt["s_vb"])
            for Q in range(8):
                KM = KMs[Q]
                for kc in range(KM):
                    bk = kc % 4
                    g = pre[Q] + kc
                    if head_state["bank_free"][bk] is not None:
                        W(tensor, "s_ek", head_state["bank_free"][bk])
                    if head_state["ctxT_free"][bk] is not None:
                        W(tensor, "s_vt", head_state["ctxT_free"][bk])
                        head_state["ctxT_free"][bk] = None
                    head_state["bank_free"][bk] = ek0 + g + 1
                    I(tensor.matmul(PB[bk][:, :],
                                    sk[:, h2 * SEQ + kc * 128:h2 * SEQ + (kc + 1) * 128],
                                    sq[:, h2 * SEQ + Q * 512:h2 * SEQ + (Q + 1) * 512],
                                    start=True, stop=True), "s_mk", 1)
                    if kc >= 1:
                        emit_av(tensor, Q, kc - 1, KM)
                emit_av(tensor, KM and Q, KM - 1, KM)
                if l == 1 and Q < 7:
                    for qt in range(4):
                        mm = tensor.matmul(PB[4 + qt][:, 0:129], sone_rb[:],
                                           ssufb[0:1, Q * 129:(Q + 1) * 129],
                                           start=False, stop=True)
                    I(mm, "s_ak", 1)
                for qt in range(4):
                    head_state["av_free"][qt] = cv0 + Q * 4 + qt + 1

        @blk.scalar
        def _(scalar):
            for Q in range(8):
                KM = KMs[Q]
                for kc in range(KM):
                    g = pre[Q] + kc
                    if g >= 2:
                        gq = g - 2
                        Q2 = 0
                        while pre[Q2 + 1] <= gq:
                            Q2 += 1
                        W(scalar, "s_ak", av_inc(Q2, gq - pre[Q2]))
                    if l == 1 and kc >= 4 * Q:
                        W(scalar, "s_vk", vk0 + 4 * Q + (kc - 4 * Q) + 1)
                    else:
                        W(scalar, "s_mk", mk0 + g + 1)
                    I(scalar.activation(sexp[:, (kc % 2) * 512:(kc % 2 + 1) * 512],
                                        PB[kc % 4][:, :], AF.Exp,
                                        bias=sbias[:, h2:h2 + 1], scale=SC),
                      "s_ek", 1)
        head_state["ek"] = ek0 + KMtot

        def _evac_block(vector, Q):
            KM = KMs[Q]
            for qt in range(4):
                W(vector, "s_ak", av_inc(Q, KM - 1) + (1 if (l == 1 and Q < 7) else 0))
                vector.tensor_copy(szq[:, 0:1], PB[4 + qt][:, 128:129])
                vector.drain()
                vector.reciprocal(szq[:, 1:2], szq[:, 0:1])
                vector.drain()
                I(vector.tensor_scalar(sctx[:, (Q * 4 + qt) * 128:(Q * 4 + qt + 1) * 128],
                                       PB[4 + qt][:, 0:128], szq[:, 1:2], 0.0,
                                       ALU.mult, ALU.add), "s_cv", 1)
                vector.drain()

        @blk.vector
        def _(vector):
            for Q in range(8):
                if Q >= 1:
                    _evac_block(vector, Q - 1)
                if l == 1:
                    for j in range(4):
                        kc = 4 * Q + j
                        g = pre[Q] + kc
                        W(vector, "s_mk", mk0 + g + 1)
                        I(vector.tensor_tensor(PB[kc % 4][:, :], PB[kc % 4][:, :],
                                               sbnd[:, j * 512:(j + 1) * 512],
                                               ALU.mult), "s_vk", 1)
                        vector.drain()
            _evac_block(vector, 7)

        MT0 = cnt["s_mt"]
        VT0 = cnt["s_vt"]

        @blk.tensor
        def _(tensor):
            for i32 in range(32):
                bk = i32 % 4
                W(tensor, "s_cv", cv0 + i32 + 1)
                if head_state["bank_free"][bk] is not None:
                    W(tensor, "s_ek", head_state["bank_free"][bk])
                    head_state["bank_free"][bk] = None
                if i32 >= 4:
                    W(tensor, "s_vt", VT0 + i32 - 3)
                I(tensor.matmul(PB[bk][:, 0:128], sctx[:, i32 * 128:(i32 + 1) * 128],
                                siden[:], start=True, stop=True), "s_mt", 1)

        @blk.vector
        def _(vector):
            for i32 in range(32):
                bk = i32 % 4
                W(vector, "s_mt", MT0 + i32 + 1)
                vector.tensor_copy(sctxT[:, hh * SEQ + i32 * 128:hh * SEQ + (i32 + 1) * 128],
                                   PB[bk][:, 0:128])
                vector.drain()
                I(vector.nop(), "s_vt", 1)
        for bk in range(4):
            head_state["ctxT_free"][bk] = VT0 + 29 + bk

    def proj_phase():
        MP0 = cnt["s_mp"]
        VP0 = cnt["s_vp"]

        @blk.tensor
        def _(tensor):
            g = 0
            for ot in range(16):
                for tb in range(8):
                    bk = g % 8
                    if g >= 8:
                        W(tensor, "s_vp", VP0 + g - 7)
                    elif bk < 4:
                        if head_state["ctxT_free"][bk] is not None:
                            W(tensor, "s_vt", head_state["ctxT_free"][bk])
                            head_state["ctxT_free"][bk] = None
                    else:
                        if head_state["av_free"][bk - 4] is not None:
                            W(tensor, "s_cv", head_state["av_free"][bk - 4])
                            head_state["av_free"][bk - 4] = None
                    for fc in range(4):
                        mm = tensor.matmul(PB[bk][:, :],
                                           swo[:, fc * 2048 + ot * 128:fc * 2048 + (ot + 1) * 128],
                                           sctxT[:, fc * SEQ + tb * 512:fc * SEQ + (tb + 1) * 512],
                                           start=(fc == 0), stop=(fc == 3))
                    I(mm, "s_mp", 1)
                    g += 1

        @blk.vector
        def _(vector):
            for g in range(128):
                bk = g % 8
                W(vector, "s_mp", MP0 + g + 1)
                if g >= 8:
                    pst.wait(vector, g - 8)
                vector.tensor_copy(sfoB[:, bk * 512:(bk + 1) * 512], PB[bk][:, :])
                vector.drain()
                I(vector.nop(), "s_vp", 1)

        @blk.scalar
        def _(scalar):
            for g in range(128):
                ot, tb = g // 8, g % 8
                bk = g % 8
                W(scalar, "s_vp", VP0 + g + 1)
                ro = 2048 * (tb // 2) + 128 * ot
                co = 512 * (tb % 2)
                pst.inc(scalar.dma_start(out=Tpart[ro:ro + 128, co:co + 512],
                                         in_=sfoB[:, bk * 512:(bk + 1) * 512]), g)
            for i in range(8):
                pst.wait(scalar, 120 + i)

    qkv_phase(0)
    for h2 in range(2):
        bounds_head(0, h2)
        if l == 1:
            suffix_head(0, h2)
        attn_head(0, h2)
    qkv_phase(1)
    for h2 in range(2):
        bounds_head(1, h2)
        if l == 1:
            suffix_head(1, h2)
        attn_head(1, h2)
    proj_phase()


def emit_C1(env, xT, ffn, psl, xo):
    sx = env.buf("sx", [128, 2 * NT])
    sf = env.buf("sf", [128, 2 * NT])
    sp = env.buf("sp", [128, 2 * NT])
    so = env.buf("so", [128, 2 * NT])
    px = env.pdma("px")
    pf = env.pdma("pf")
    pp2 = env.pdma("psc", 4)
    po = env.pdma("po")
    blk = env.blk
    I, W = env.I, env.W

    @blk.sync
    def _(sync):
        for j in range(NCH):
            if j >= 2:
                W(sync, "s_v", j - 1)
            sl = (j % 2)
            px.inc(sync.dma_start(out=sx[:, sl * NT:(sl + 1) * NT],
                                  in_=xT[128 * j:128 * (j + 1), :]), j)
            pf.inc(sync.dma_start(out=sf[:, sl * NT:(sl + 1) * NT],
                                  in_=ffn[128 * j:128 * (j + 1), :]), j)
            pp2.inc(sync.dma_start(out=sp[:, sl * NT:(sl + 1) * NT],
                                   in_=psl[128 * j:128 * (j + 1), :]), j)

    @blk.vector
    def _(vector):
        for j in range(NCH):
            sl = (j % 2)
            px.wait(vector, j)
            pf.wait(vector, j)
            pp2.wait(vector, j)
            if j >= 2:
                po.wait(vector, j - 2)
            vector.tensor_tensor(so[:, sl * NT:(sl + 1) * NT],
                                 sp[:, sl * NT:(sl + 1) * NT],
                                 sx[:, sl * NT:(sl + 1) * NT], ALU.add)
            vector.drain()
            I(vector.tensor_tensor(so[:, sl * NT:(sl + 1) * NT],
                                   so[:, sl * NT:(sl + 1) * NT],
                                   sf[:, sl * NT:(sl + 1) * NT], ALU.add), "s_v", 1)
            vector.drain()

    @blk.scalar
    def _(scalar):
        for j in range(NCH):
            W(scalar, "s_v", j + 1)
            po.inc(scalar.dma_start(out=xo[128 * j:128 * (j + 1), :],
                                    in_=so[:, (j % 2) * NT:(j % 2 + 1) * NT]), j)
        po.wait(scalar, NCH - 2)
        po.wait(scalar, NCH - 1)


def emit_B0(env, xn, wqkv, bqkv, Tp, DR):
    nc = env.nc
    SC = 1.0 / np.sqrt(128.0)

    ones_rb = DR["ones_rb"]

    sxnb = env.buf("sxnb", [128, NCH * NT], BF16)
    sqkv = env.buf("sqkv", [128, 8 * 6144], BF16)
    swt = env.buf("swt", [128, 2 * 512], BF16)
    sbrf = env.buf("sbrf", [1, 3 * D])
    sbrb = env.buf("sbrb", [1, 3 * D], BF16)
    sone_rb = env.buf("sone_rb", [1, 128], BF16)
    sP = env.buf("sP", [128, 2048], BF16)
    sS = env.buf("sS", [128, 256])
    sE = env.buf("sE", [128, 256])
    sm = env.buf("sm", [128, 1])
    sZ = env.buf("sZ", [128, 32])
    sacc = env.buf("sacc", [128, 2048])
    sco = env.buf("sco", [128, 2 * 2048], BF16)

    PB = [env.PB(i) for i in range(8)]

    pwl = env.pdma("pwg")
    pco = env.pdma("px")

    blk = env.blk
    I, W, cnt = env.I, env.W, env.cnt

    @blk.sync
    def _(sync):
        I(sync.dma_start(out=sbrf[:, :], in_=bqkv[:, :]), "s_k", 16)
        I(sync.dma_start(out=sone_rb[:], in_=ones_rb[:, :]), "s_k", 16)
        for j in range(NCH):
            I(sync.dma_start(out=sxnb[:, j * NT:(j + 1) * NT],
                             in_=xn[128 * j:128 * (j + 1), :]), "s_xa", 16)

    @blk.vector
    def _(vector):
        W(vector, "s_k", 32)
        vector.tensor_copy(sbrb[:, :], sbrf[:, :])
        vector.drain()
        I(vector.nop(), "s_vx", 1)

    def pe_done(n):
        cb, fc = n // 16, n % 16
        return cb * 136 + (fc + 1) * 8

    @blk.gpsimd
    def _(gpsimd):
        for n in range(12 * NCH):
            if n >= 2:
                W(gpsimd, "s_mq", pe_done(n - 2))
            cb, fc = n // 16, n % 16
            pwl.inc(gpsimd.dma_start(
                out=swt[:, (n % 2) * 512:(n % 2 + 1) * 512],
                in_=wqkv[128 * fc:128 * (fc + 1), 512 * cb:512 * (cb + 1)]), n)

    @blk.tensor
    def _(tensor):
        W(tensor, "s_vx", 1)
        W(tensor, "s_xa", 16 * NCH)
        for cb in range(12):
            if cb >= 1:
                W(tensor, "s_vq", cb * 8)
            for fc in range(NCH):
                n = cb * NCH + fc
                pwl.wait(tensor, n)
                sl = (n % 2) * 512
                for tt in range(8):
                    mm = tensor.matmul(PB[tt][:, :],
                                       sxnb[:, fc * NT + tt * 128:fc * NT + (tt + 1) * 128],
                                       swt[:, sl:sl + 512],
                                       start=(fc == 0), stop=False)
                I(mm, "s_mq", 8)
            for tt in range(8):
                mm = tensor.matmul(PB[tt][:, :], sone_rb[:],
                                   sbrb[0:1, 512 * cb:512 * (cb + 1)],
                                   start=False, stop=True)
            I(mm, "s_mq", 8)

    @blk.vector
    def _(vector):
        for cb in range(12):
            W(vector, "s_mq", (cb + 1) * 136)
            for tt in range(8):
                dst = sqkv[:, tt * 6144 + cb * 512:tt * 6144 + (cb + 1) * 512]
                if cb < 4:
                    vector.tensor_scalar(dst, PB[tt][:, :], SC, 0.0, ALU.mult, ALU.add)
                else:
                    vector.tensor_copy(dst, PB[tt][:, :])
                vector.drain()
            I(vector.nop(), "s_vq", 8)

    VQ_ALL = 12 * 8

    @blk.vector
    def _(vector):
        W(vector, "s_vq", VQ_ALL)
        for tt in range(8):
            qb, kb, vb = tt * 6144, tt * 6144 + 2048, tt * 6144 + 4096
            if tt >= 1:
                W(vector, "s_ex", tt)
            for o in range(16):
                lim = (16 - o) * 128
                vector.tensor_tensor(sP[:, 0:lim], sqkv[:, qb:qb + lim],
                                     sqkv[:, kb + o * 128:kb + 2048], ALU.mult)
                if o > 0:
                    vector.tensor_tensor(sP[:, lim:2048], sqkv[:, qb + lim:qb + 2048],
                                         sqkv[:, kb:kb + o * 128], ALU.mult)
                vector.drain()
                vector.reduce_sum(sS[:, o * 16:(o + 1) * 16],
                                  sP.ap().rearrange("p (h e) -> p h e", e=128),
                                  axis=AX.X)
                vector.drain()
            I(vector.reduce_max(sm[:, :], sS[:, :], axis=AX.X, negate=True), "s_s0", 1)
            vector.drain()
            W(vector, "s_ex", tt + 1)
            if tt >= 1:
                W(vector, "s_co", 16 * tt)
            vector.reduce_sum(sZ[:, 0:16],
                              sE.ap().rearrange("p (o h) -> p h o", h=16),
                              axis=AX.X)
            vector.drain()
            vector.reciprocal(sZ[:, 16:32], sZ[:, 0:16])
            vector.drain()
            I(vector.nop(), "s_zv", 1)
            for t in range(16):
                vsl = sqkv[:, vb + t * 128:vb + (t + 1) * 128]
                for h in range(16):
                    col = ((t - h) % 16) * 16 + h
                    if t == 0:
                        vector.tensor_scalar(sacc[:, h * 128:(h + 1) * 128], vsl,
                                             sE[:, col:col + 1], 0.0, ALU.mult, ALU.add)
                    else:
                        vector.scalar_tensor_tensor(sacc[:, h * 128:(h + 1) * 128], vsl,
                                                    sE[:, col:col + 1],
                                                    sacc[:, h * 128:(h + 1) * 128],
                                                    ALU.mult, ALU.add)
                vector.drain()
            if tt >= 2:
                pco.wait(vector, tt - 2)
            for h in range(16):
                I(vector.tensor_scalar(sco[:, (tt % 2) * 2048 + h * 128:(tt % 2) * 2048 + (h + 1) * 128],
                                       sacc[:, h * 128:(h + 1) * 128],
                                       sZ[:, 16 + h:17 + h], 0.0, ALU.mult, ALU.add),
                  "s_co", 1)
            vector.drain()

    @blk.scalar
    def _(scalar):
        for tt in range(8):
            W(scalar, "s_s0", tt + 1)
            if tt >= 1:
                W(scalar, "s_co", 16 * tt)
            I(scalar.activation(sE[:, :], sS[:, :], AF.Exp, bias=sm[:, :]), "s_ex", 1)
            scalar.drain()

    @blk.sync
    def _(sync):
        TpV = Tp.ap().rearrange("(g t) d -> t g d", g=4)
        for tt in range(8):
            W(sync, "s_co", 16 * (tt + 1))
            pco.inc(sync.dma_start(
                out=TpV[128 * tt:128 * (tt + 1)],
                in_=sco.ap().rearrange("p (s g d) -> p s g d", s=2, g=4)[:, tt % 2]),
                tt)
        pco.wait(sync, 6)
        pco.wait(sync, 7)


def emit_SC(env, Tp, Gc, Tq, ctxR, Yt, DR):
    blk = env.blk
    I, W, cnt = env.I, env.W, env.cnt
    PB = [env.PB(i) for i in range(8)]
    sld = env.buf("sld", [128, 4 * 128], BF16)
    sct = env.buf("sct", [128, 4 * 128], BF16)
    sperm = env.buf("sperm", [128, 128], BF16)
    psc = env.pdma("psc", 4)
    pys = env.pdma("pstf", 4)

    @blk.gpsimd
    def _(g):
        I(g.collective_compute("AllGather", ALU.bypass, replica_groups=G4,
                               ins=[Tp.ap().opt()], outs=[Gc.ap().opt()]),
          "s_cc", 1)
        W(g, "s_cc")
        for k in range(4):
            for gi in range(4):
                I(g.dma_start(
                    out=Tq[4096 * gi + 1024 * k:4096 * gi + 1024 * (k + 1), :],
                    in_=Gc[4096 * k + 1024 * gi:4096 * k + 1024 * (gi + 1), :]),
                  "s_pm", 16)
        W(g, "s_pm")
        I(g.collective_compute("ReduceScatter", ALU.max, replica_groups=G4,
                               ins=[Tq.ap().opt()], outs=[ctxR.ap().opt()]),
          "s_cc", 1)

    @blk.sync
    def _(sync):
        I(sync.dma_start(out=sperm[:, :], in_=DR["perm_b"][:, :]), "s_k", 16)
        W(sync, "s_cc")
        for n in range(128):
            a, h = n // 4, n % 4
            if n >= 4:
                W(sync, "s_mt", n - 3)
            psc.inc(sync.dma_start(
                out=sld[:, (n % 4) * 128:(n % 4 + 1) * 128],
                in_=ctxR[128 * a:128 * (a + 1), 128 * h:128 * (h + 1)]), n)

    @blk.tensor
    def _(t):
        W(t, "s_k", 16)
        for n in range(128):
            psc.wait(t, n)
            if n >= 4:
                W(t, "s_vt", n - 3)
            I(t.matmul(PB[n % 4][:, 0:128],
                       sld[:, (n % 4) * 128:(n % 4 + 1) * 128],
                       sperm[:, :], start=True, stop=True), "s_mt", 1)

    @blk.vector
    def _(v):
        for n in range(128):
            W(v, "s_mt", n + 1)
            if n >= 4:
                pys.wait(v, n - 4)
            v.tensor_copy(sct[:, (n % 4) * 128:(n % 4 + 1) * 128],
                          PB[n % 4][:, 0:128])
            v.drain()
            I(v.nop(), "s_vt", 1)

    @blk.scalar
    def _(s):
        YV = Yt.ap().rearrange("(u dd) (h a j) -> dd u h a j", u=16, h=4, a=32)
        sctV = sct.ap().rearrange("p (sl u j) -> p sl u j", sl=4, j=8)
        for n in range(128):
            a, h = n // 4, n % 4
            W(s, "s_vt", n + 1)
            pys.inc(s.dma_start(out=YV[:, :, h, a, :], in_=sctV[:, n % 4]), n)
        for i in range(4):
            pys.wait(s, 124 + i)


def emit_C0(env, Yt, wproj, bias, xT, ffn, xo, DR):
    nc = env.nc

    sy = env.buf("sy", [128, NCH * NT], BF16)
    swt = env.buf("swt", [128, 4 * 128], BF16)
    sb = env.buf("sb", [128, 16])
    sx = env.buf("sx", [128, 2 * NT])
    sf = env.buf("sf", [128, 2 * NT])
    so = env.buf("so", [128, 2 * NT])
    PB = [env.PB(i) for i in range(8)]

    pwl = env.pdma("pwl4", 4)
    px = env.pdma("px")
    pf = env.pdma("pf")
    po = env.pdma("po")

    blk = env.blk
    I, W, cnt = env.I, env.W, env.cnt

    @blk.sync
    def _(sync):
        for j in range(NCH):
            I(sync.dma_start(out=sb[:, j:j + 1], in_=bias[128 * j:128 * (j + 1), :]),
              "s_k", 16)
        for j in range(NCH):
            I(sync.dma_start(out=sy[:, j * NT:(j + 1) * NT],
                             in_=Yt[128 * j:128 * (j + 1), :]), "s_ya", 16)
        for ot in range(16):
            if ot >= 2:
                W(sync, "s_vp", 2 * (ot - 1))
            sl = (ot % 2)
            px.inc(sync.dma_start(out=sx[:, sl * NT:(sl + 1) * NT],
                                  in_=xT[128 * ot:128 * (ot + 1), :]), ot)
            pf.inc(sync.dma_start(out=sf[:, sl * NT:(sl + 1) * NT],
                                  in_=ffn[128 * ot:128 * (ot + 1), :]), ot)

    @blk.gpsimd
    def _(gpsimd):
        for n in range(16 * 16):
            ot, fc = n // 16, n % 16
            if n >= 4:
                W(gpsimd, "s_mp", n - 3)
            pwl.inc(gpsimd.dma_start(
                out=swt[:, (n % 4) * 128:(n % 4 + 1) * 128],
                in_=wproj[128 * fc:128 * (fc + 1), 128 * ot:128 * (ot + 1)]), n)

    @blk.tensor
    def _(tensor):
        W(tensor, "s_ya", 16 * NCH)
        for ot in range(16):
            b0 = (ot % 4) * 2
            if ot >= 4:
                W(tensor, "s_vp", 2 * (ot - 3))
            for fc in range(NCH):
                n = ot * 16 + fc
                pwl.wait(tensor, n)
                sl = (n % 4) * 128
                for hf in range(2):
                    mm = tensor.matmul(PB[b0 + hf][:, :], swt[:, sl:sl + 128],
                                       sy[:, fc * NT + hf * 512:fc * NT + (hf + 1) * 512],
                                       start=(fc == 0), stop=(fc == NCH - 1))
                I(mm, "s_mp", 1)

    @blk.vector
    def _(vector):
        W(vector, "s_k", 16 * NCH)
        for ot in range(16):
            b0 = (ot % 4) * 2
            W(vector, "s_mp", (ot + 1) * NCH)
            px.wait(vector, ot)
            pf.wait(vector, ot)
            if ot >= 2:
                po.wait(vector, ot - 2)
            sl = (ot % 2)
            for hf in range(2):
                rng = slice(sl * NT + hf * 512, sl * NT + (hf + 1) * 512)
                vector.scalar_tensor_tensor(so[:, rng], PB[b0 + hf][:, :],
                                            sb[:, ot:ot + 1], sx[:, rng],
                                            ALU.add, ALU.add)
                vector.drain()
                vector.tensor_tensor(so[:, rng], so[:, rng], sf[:, rng], ALU.add)
                vector.drain()
            I(vector.nop(), "s_vp", 2)

    @blk.scalar
    def _(scalar):
        for ot in range(16):
            W(scalar, "s_vp", 2 * (ot + 1))
            po.inc(scalar.dma_start(out=xo[128 * ot:128 * (ot + 1), :],
                                    in_=so[:, (ot % 2) * NT:(ot % 2 + 1) * NT]), ot)
        po.wait(scalar, 14)
        po.wait(scalar, 15)


def build_fused(debug=False):
    env = Env(debug=debug)
    nc = env.nc

    def din_(name, shape, dt=F32):
        return nc.dram_tensor(name, shape, dt, kind="ExternalInput")

    xT0 = din_("xT0", [D, NT])
    w1t = din_("w1t", [D, INNER])
    w2t = din_("w2t", [INNER, D])
    gatet = din_("gatet", [D, INNER])
    upt = din_("upt", [D, INNER])
    downt = din_("downt", [INNER, D])
    sh_w1 = din_("sh_w1", [D // 8, INNER])
    sh_w2 = din_("sh_w2", [INNER // 8, D])
    sh_g = din_("sh_g", [D // 8, INNER])
    sh_u = din_("sh_u", [D // 8, INNER])
    sh_d = din_("sh_d", [INNER // 8, D])
    wqkv = din_("wqkv", [D, 3 * D])
    bqkv = din_("bqkv", [1, 3 * D])
    wproj = din_("wproj", [D, D])
    bproj = din_("bproj", [D, 1])
    wq_sl = din_("wq_sl", [D, 512])
    wk_sl = din_("wk_sl", [D, 512])
    wv_sl = din_("wv_sl", [D, 512])
    wo_sl = din_("wo_sl", [512, D])
    bands = din_("bands", [128, 4 * 512], BF16)
    DR = {
        "ones_f": din_("ones_f", [128, 1]),
        "ones_r": din_("ones_r", [1, 128]),
        "ones_fb": din_("ones_fb", [128, 1], BF16),
        "ones_rb": din_("ones_rb", [1, 128], BF16),
        "iden_f": din_("iden_f", [128, 128]),
        "iden_b": din_("iden_b", [128, 128], BF16),
        "perm_b": din_("perm_b", [128, 128], BF16),
    }
    xo = nc.dram_tensor("xo", [D, NT], F32, kind="ExternalOutput")

    DR["arin"] = nc.dram_tensor("arin", [1, 3], F32)
    DR["arout"] = nc.dram_tensor("arout", [1, 3], F32)
    DR["hscr"] = nc.dram_tensor("hscr", [INNER, NT], F32)
    DR["t2scr"] = nc.dram_tensor("t2scr", [INNER, NT], BF16)
    DR["xnf"] = nc.dram_tensor("xnf", [D, NT], F32)
    xn0 = nc.dram_tensor("xn0", [D, NT], BF16)
    ffn0 = nc.dram_tensor("ffn0", [D, NT], F32)
    Tp = nc.dram_tensor("Tp", [S, 512], BF16)
    Gc = nc.dram_tensor("Gc", [4 * S, 512], BF16)
    Tq = nc.dram_tensor("Tq", [4 * S, 512], BF16)
    ctxR = nc.dram_tensor("ctxR", [S, 512], BF16)
    Yt = nc.dram_tensor("Yt", [D, NT], BF16)
    x1 = nc.dram_tensor("x1", [D, NT], F32)
    xn1 = nc.dram_tensor("xn1", [D, NT], BF16)
    ffn1 = nc.dram_tensor("ffn1", [D, NT], F32)
    xst = nc.dram_tensor("xst", [4 * D, NT], BF16)
    Tpart = nc.dram_tensor("Tpart", [4 * D, NT], F32)
    psl = nc.dram_tensor("psl", [D, NT], F32)

    env.make()
    blk, I, W = env.blk, env.I, env.W

    env.open_stage("a0")
    emit_A(env, 0, xT0, [w1t, w2t], [sh_w1, sh_w2], xn0, ffn0, DR)
    env.barrier()
    env.open_stage("b0")
    emit_B0(env, xn0, wqkv, bqkv, Tp, DR)
    env.barrier()
    env.open_stage("sc")
    emit_SC(env, Tp, Gc, Tq, ctxR, Yt, DR)
    env.barrier()
    env.open_stage("c0")
    emit_C0(env, Yt, wproj, bproj, xT0, ffn0, x1, DR)
    env.barrier()
    env.open_stage("a1")
    emit_A(env, 1, x1, [gatet, upt, downt], [sh_g, sh_u, sh_d], xn1, ffn1, DR)
    env.barrier()
    env.open_stage("gx")

    @blk.gpsimd
    def _(g):
        I(g.collective_compute("AllGather", ALU.bypass, replica_groups=G4,
                               ins=[xn1.ap().opt()], outs=[xst.ap().opt()]),
          "s_cc", 1)
        W(g, "s_cc")

    env.barrier()
    env.open_stage("b1")
    emit_B(env, 1, xst, wq_sl, wk_sl, wv_sl, wo_sl, bands, Tpart, DR)
    env.barrier()
    env.open_stage("pr")

    @blk.gpsimd
    def _(g):
        I(g.collective_compute("ReduceScatter", ALU.add, replica_groups=G4,
                               ins=[Tpart.ap().opt()], outs=[psl.ap().opt()]),
          "s_cc", 1)
        W(g, "s_cc")

    env.barrier()
    env.open_stage("c1")
    emit_C1(env, x1, ffn1, psl, xo)

    env.close()
    return env


from concourse.bass_utils import run_bass_kernel_spmd

_CACHE = {}


def kernel(**inputs):
    inputs = {k: np.asarray(v) for k, v in inputs.items()}
    sh, percore = host_prep(inputs)
    if "env" not in _CACHE:
        _CACHE["env"] = build_fused()
    nc = _CACHE["env"].nc
    bq = np.ascontiguousarray(
        np.asarray(inputs["L0_attn_bqkv"], np.float32).reshape(1, -1))
    wproj = np.ascontiguousarray(np.asarray(inputs["L0_attn_Wproj"], np.float32))
    ims = []
    for c in range(NCORES):
        pc = percore[c]
        ims.append({
            "xT0": pc["x0"].astype(np.float32),
            "w1t": sh["l0_w1t"], "w2t": sh["l0_w2t"],
            "gatet": sh["l1_gatet"], "upt": sh["l1_upt"],
            "downt": sh["l1_downt"],
            "sh_w1": pc["sh_l0w1"], "sh_w2": pc["sh_l0w2"],
            "sh_g": pc["sh_l1g"], "sh_u": pc["sh_l1u"], "sh_d": pc["sh_l1d"],
            "wqkv": sh["l0_wqkv"], "bqkv": bq,
            "wproj": wproj, "bproj": sh["l0_bproj"],
            "wq_sl": pc["l1_wq_sl"], "wk_sl": pc["l1_wk_sl"],
            "wv_sl": pc["l1_wv_sl"], "wo_sl": pc["l1_wo_sl"],
            "bands": sh["bands"],
            "ones_f": sh["ones_f"], "ones_r": sh["ones_r"],
            "ones_fb": np.ones((128, 1), bf16), "ones_rb": sh["ones_rb"],
            "iden_f": np.eye(128, dtype=np.float32), "iden_b": sh["iden_b"],
            "perm_b": sh["perm_b"],
        })
    r = run_bass_kernel_spmd(nc, ims, core_ids=list(range(NCORES)))
    out = np.empty((B, S, D), np.float32)
    for c in range(NCORES):
        s_, r_ = c // 4, c % 4
        out[s_, 1024 * r_:1024 * (r_ + 1), :] = np.asarray(r.results[c]["xo"]).T
    return out


# version 05 (final)
# speedup vs baseline: 1.0568x; 1.0157x over previous
import sys
sys.path.insert(0, '/opt/trn_rl_repo')
import numpy as np
import ml_dtypes
import concourse.bass as bass
import concourse.mybir as mybir
from contextlib import ExitStack

F32 = mybir.dt.float32
BF16 = mybir.dt.bfloat16
AF = mybir.ActivationFunctionType
ALU = mybir.AluOpType
AX = mybir.AxisListType

NCORES = 8
B, S, D, H, DH = 2, 4096, 2048, 16, 128
INNER = 4 * D
NT = 1024
NCH = D // 128
HCH = INNER // 128
RC = 1.5 * (2.0 ** 23)
EPS = 1e-5

bf16 = ml_dtypes.bfloat16



def host_prep(inp):
    C = np.ascontiguousarray
    sh = {}
    sh["l0_wqkv"] = C(inp["L0_attn_Wqkv"])
    sh["l0_bproj"] = C(inp["L0_attn_bproj"].reshape(D, 1))
    sh["l0_w1t"] = C(inp["L0_ffn_w1"].T)
    sh["l0_w2t"] = C(inp["L0_ffn_w2"].T)
    sh["l1_gatet"] = C(inp["L1_ffn_w_gate"].T)
    sh["l1_upt"] = C(inp["L1_ffn_w_up"].T)
    sh["l1_downt"] = C(inp["L1_ffn_w_down"].T)
    sh["ones_f"] = np.ones((128, 1), np.float32)
    sh["ones_r"] = np.ones((1, 128), np.float32)
    sh["ones_rb"] = np.ones((1, 128), bf16)
    sh["iden_b"] = np.eye(128).astype(bf16)
    pm = np.zeros((128, 128), np.float32)
    for d_ in range(128):
        pm[d_, (d_ % 16) * 8 + d_ // 16] = 1.0
    sh["perm_b"] = pm.astype(bf16)
    bands = np.zeros((128, 4 * 512), np.float32)
    q = np.arange(512)[None, :]
    k = np.arange(128)[:, None]
    for j in range(4):
        bands[:, j * 512:(j + 1) * 512] = (q >= (128 * j + k)).astype(np.float32)
    sh["bands"] = bands.astype(bf16)

    percore = []
    x0 = np.asarray(inp["L0_attn_x"])
    wqkv = np.asarray(inp["L0_attn_Wqkv"])
    bqkv = np.asarray(inp["L0_attn_bqkv"])
    wproj = np.asarray(inp["L0_attn_Wproj"])
    wq, wk, wv, wo = (np.asarray(inp["L1_attn_W" + n]) for n in "qkvo")
    w1t, w2t = sh["l0_w1t"], sh["l0_w2t"]
    gt, ut, dt_ = sh["l1_gatet"], sh["l1_upt"], sh["l1_downt"]
    for c in range(NCORES):
        s, r = c // 4, c % 4
        pc = {}
        pc["x0"] = C(x0[s, 1024 * r:1024 * (r + 1), :].T)
        cols = slice(512 * r, 512 * (r + 1))
        pc["l0_wqkv_sl"] = C(np.concatenate(
            [wqkv[:, cols], wqkv[:, 2048:][:, cols], wqkv[:, 4096:][:, cols]], axis=1))
        pc["l0_bqkv_sl"] = C(np.concatenate(
            [bqkv[cols], bqkv[2048:][cols], bqkv[4096:][cols]]).reshape(1536, 1))
        pc["l0_wproj_sl"] = C(wproj[512 * r:512 * (r + 1), :])
        pc["l1_wq_sl"] = C(wq[:, cols])
        pc["l1_wk_sl"] = C(wk[:, cols])
        pc["l1_wv_sl"] = C(wv[:, cols])
        pc["l1_wo_sl"] = C(wo[512 * r:512 * (r + 1), :])
        pc["sh_l0w1"] = C(w1t[256 * c:256 * (c + 1)])
        pc["sh_l0w2"] = C(w2t[1024 * c:1024 * (c + 1)])
        pc["sh_l1g"] = C(gt[256 * c:256 * (c + 1)])
        pc["sh_l1u"] = C(ut[256 * c:256 * (c + 1)])
        pc["sh_l1d"] = C(dt_[1024 * c:1024 * (c + 1)])
        percore.append(pc)
    return sh, percore



class _CntView:
    def __init__(self, env):
        self._env = env

    def __getitem__(self, k):
        return self._env.gcnt.get(k, 0) - self._env.base.get(k, 0)


class PDma:
    def __init__(self, env, name, nslots=2):
        self.env = env
        self.names = [f"{name}{j}" for j in range(nslots)]
        for nm in self.names:
            env.sem_h(nm)
        self.base = [env.gcnt.get(nm, 0) for nm in self.names]
        self.n = nslots

    def inc(self, instr, i):
        nm = self.names[i % self.n]
        self.env.gcnt[nm] = self.env.gcnt.get(nm, 0) + 16
        instr.then_inc(self.env._sems[nm], 16)

    def wait(self, eng, i):
        j = i % self.n
        eng.wait_ge(self.env._sems[self.names[j]],
                    self.base[j] + 16 * (i // self.n + 1))


class Env:
    def __init__(self, debug=False):
        self.nc = bass.Bass("TRN2", target_bir_lowering=False, debug=debug)
        self.st = ExitStack()
        self.gcnt = {}
        self.base = {}
        self.cnt = _CntView(self)
        self._sems = {}
        self._stage_bufs = {}
        self._cursor = 0
        self._maxcur = 0
        self._virts = []
        self.blk = None
        self.PS = None
        self._barn = 0
        self.stage_marks = []

    def make(self):
        self.PS = self.st.enter_context(
            self.nc.psum_tensor("PS", [128, 4096], F32))
        for kb in (207, 204, 200, 196, 192):
            try:
                self._arena = self.st.enter_context(
                    self.nc.sbuf_tensor(f"arena{kb}", [128, kb * 1024],
                                        mybir.dt.uint8))
                break
            except Exception:
                continue
        self._asize = kb * 1024
        self._aoff = self.nc.lookup_mloc(self._arena).addr
        self.blk = self.st.enter_context(self.nc.Block())
        self.sem_h("bar")

    def PB(self, i):
        return self.PS[:, 512 * i:512 * (i + 1)]

    def sem_h(self, name):
        if name not in self._sems:
            self._sems[name] = self.st.enter_context(self.nc.semaphore(name))
        return self._sems[name]

    def I(self, instr, s, n=1):
        self.gcnt[s] = self.gcnt.get(s, 0) + n
        instr.then_inc(self.sem_h(s), n)
        return self.gcnt[s]

    def W(self, eng, s, v=None):
        g = self.gcnt.get(s, 0) if v is None else self.base.get(s, 0) + v
        eng.wait_ge(self.sem_h(s), g)

    def pdma(self, name, nslots=2):
        return PDma(self, name, nslots)

    def open_stage(self, tag):
        self._stage_bufs = {}
        self._cursor = 0
        self._tag = tag
        self.base = dict(self.gcnt)
        self.stage_marks.append((tag, self.nc.n_instructions()))

    def buf(self, name, shape, dt=F32):
        if name in self._stage_bufs:
            return self._stage_bufs[name]
        per = int(np.prod(shape[1:])) * mybir.dt.size(dt)
        addr = self._cursor
        self._cursor = (addr + per + 31) // 32 * 32
        self._maxcur = max(self._maxcur, self._cursor)
        assert self._cursor <= self._asize, \
            f"arena overflow {self._tag}/{name}: {self._cursor}>{self._asize}"
        h = self.nc.virtual_sbuf_tensor(f"{self._tag}_{name}", shape, dt)
        self.nc._tensor(h.name, h.shape, h.dtype, type="SB")
        mloc = self.nc.lookup_mloc(h)
        mloc.allocated = True
        mloc.addr = self._aoff + addr
        self._stage_bufs[name] = h
        return h

    def barrier(self):
        self._barn += 1
        blk, bar = self.blk, self._sems["bar"]
        for dec in (blk.sync, blk.tensor, blk.vector, blk.scalar, blk.gpsimd):
            def _inc(e):
                e.nop().then_inc(bar, 1)
            dec(_inc)
        tgt = 5 * self._barn
        for dec in (blk.sync, blk.tensor, blk.vector, blk.scalar, blk.gpsimd):
            def _wt(e):
                e.wait_ge(bar, tgt)
            dec(_wt)

    def close(self):
        self.st.close()


G4 = [[0, 1, 2, 3], [4, 5, 6, 7]]



def emit_A(env, l, xT, wdr, shdr, xn_o, ffn_o, DR):
    nc = env.nc
    NMAT = 2 if l == 0 else 3
    SZW = D * INNER

    ones_f = DR["ones_f"]
    ones_r = DR["ones_r"]
    iden_f = DR["iden_f"]
    arin = DR["arin"]
    arout = DR["arout"]
    hscr = DR["hscr"]
    t2scr = DR["t2scr"]
    xnf = DR["xnf"]
    sxd = env.buf("sxd", [128, 2 * NT])
    sxnf = env.buf("sxnf", [128, 2 * NT])
    smx = env.buf("smx", [128, NT])
    sxn = env.buf("sxn", [128, NCH * NT], BF16)
    ssq = env.buf("ssq", [128, 4 * NT])
    srow = env.buf("srow", [1, 6 * NT])
    sbc = env.buf("sbc", [128, 2 * NT])
    smt = env.buf("smt", [128, NT])
    sacc = env.buf("sacc", [128, NT]) if l == 0 else None
    stok = env.buf("stok", [128, 8])
    sone_f = env.buf("sone_f", [128, 1])
    sone_r = env.buf("sone_r", [1, 128])
    siden = env.buf("siden", [128, 128])
    swsh = env.buf("swsh", [128, 2 * 1024])
    scol = env.buf("scol", [128, 96])
    srsc = env.buf("srsc", [1, 64])
    swsc = env.buf("swsc", [128, 2 * NMAT])
    swst = env.buf("swst", [128, 2 * 512])
    swq = env.buf("swq", [128, 2 * 512], BF16)
    swqs = env.buf("swqs", [128, 2 * 512])
    sqa = env.buf("sqa", [128, 2 * 512])
    sups = env.buf("sups", [128, 512]) if l == 1 else None
    shs = env.buf("shs", [128, 8 * 512])
    sh2 = env.buf("sh2", [128, 2 * NT])
    st2r = env.buf("st2r", [128, 2 * NT], BF16)
    sti2 = env.buf("sti2", [128, 2 * NT], BF16)
    sgt = env.buf("sgt", [128, 8 * 512]) if l == 1 else None
    sfo = env.buf("sfo", [128, 8 * 512])

    psA = env.PS[:, 0:1024]
    psB = env.PS[:, 1024:2048]
    psC = env.PS[:, 2048:3072]
    psD = env.PS[:, 3072:3584]
    psE = env.PS[:, 3584:4096]
    PB8 = [psA[:, 0:512], psA[:, 512:1024], psB[:, 0:512], psB[:, 512:1024],
           psC[:, 0:512], psC[:, 512:1024], psD[:, :], psE[:, :]]

    RO = {"mean": 0, "a": 1 * NT, "b": 2 * NT, "tmp": 3 * NT, "tmp2": 4 * NT,
          "n2": 5 * NT, "rms": 0, "maxa": 1 * NT, "clp": 2 * NT, "g": 5 * NT,
          "d1": 0}

    def row(k, lo=0, hi=NT):
        return srow[0:1, RO[k] + lo: RO[k] + hi]

    BO = {"a": 0, "b": NT, "g": 0, "d": NT}

    px1 = env.pdma("px1")
    px2 = env.pdma("px2")
    pwl = env.pdma("pwl")
    psh = env.pdma("psh")
    phr = env.pdma("phr")
    phr2 = env.pdma("phr2")
    ptw = env.pdma("ptw")
    pstf = env.pdma("pstf", 4)
    pxr = env.pdma("pxr")

    blk = env.blk
    I, W, cnt = env.I, env.W, env.cnt

    @blk.sync
    def _(sync):
        I(sync.dma_start(out=sone_f[:], in_=ones_f[:, :]), "s_k", 16)
        I(sync.dma_start(out=sone_r[:], in_=ones_r[:, :]), "s_k", 16)
        I(sync.dma_start(out=siden[:], in_=iden_f[:, :]), "s_k", 16)
        for j in range(NCH):
            if j >= 2:
                W(sync, "s_ma", j - 1)
            px1.inc(sync.dma_start(out=sxd[:, (j % 2) * NT:(j % 2 + 1) * NT],
                                   in_=xT[128 * j:128 * (j + 1), :]), j)
        for j in range(NCH):
            if j < 2:
                W(sync, "s_ma", 15 + j)
            else:
                W(sync, "s_vx", j - 1)
            px2.inc(sync.dma_start(out=sxd[:, (j % 2) * NT:(j % 2 + 1) * NT],
                                   in_=xT[128 * j:128 * (j + 1), :]), j)


    wtiles = []
    if l == 0:
        for hb in range(16):
            for ic in range(16):
                wtiles.append((0, ic, hb))
        for ob in range(4):
            for ic in range(64):
                wtiles.append((1, ic, ob))
    else:
        for hb in range(16):
            for ic in range(16):
                wtiles.append((0, ic, hb))
            for ic in range(16):
                wtiles.append((1, ic, hb))
        for ob in range(4):
            for ic in range(64):
                wtiles.append((2, ic, ob))
    NWT = len(wtiles)
    engA = [n % 3 == 2 for n in range(NWT)]
    vpre = [0] * (NWT + 1)
    apre = [0] * (NWT + 1)
    for _n in range(NWT):
        vpre[_n + 1] = vpre[_n] + (0 if engA[_n] else 1)
        apre[_n + 1] = apre[_n] + (1 if engA[_n] else 0)

    @blk.sync
    def _(sync):
        for n, (m, ic, cb) in enumerate(wtiles):
            if n >= 2:
                k = n - 2
                if engA[k]:
                    W(sync, "s_wqa", apre[k + 1])
                else:
                    W(sync, "s_wq", vpre[k + 1])
            pwl.inc(sync.dma_start(
                out=swst[:, (n % 2) * 512:(n % 2 + 1) * 512],
                in_=wdr[m][128 * ic:128 * (ic + 1), 512 * cb:512 * (cb + 1)]), n)


    shard_tiles = []
    for m in range(NMAT):
        rsh, csh = shdr[m].shape
        shard_tiles += [(m, ri, ci) for ri in range(rsh // 128)
                        for ci in range(csh // 1024)]
    NSH = len(shard_tiles)
    tiles_per_mat = [sum(1 for t in shard_tiles if t[0] == mm) for mm in range(NMAT)]

    @blk.gpsimd
    def _(gpsimd):
        for j in range(NCH):
            W(gpsimd, "s_vx", j + 1)
            ssem = "s_stxa" if (j % 2) == 0 else "s_stxb"
            I(gpsimd.dma_start(out=xn_o[128 * j:128 * (j + 1), :],
                               in_=sxnf[:, (j % 2) * NT:(j % 2 + 1) * NT]), ssem, 16)
            I(gpsimd.dma_start(out=xnf[128 * j:128 * (j + 1), :],
                               in_=sxnf[:, (j % 2) * NT:(j % 2 + 1) * NT]), ssem, 16)
        pass


    @blk.scalar
    def _(scalar):
        for j in range(NCH):
            px1.wait(scalar, j)
            if j >= 2:
                W(scalar, "s_ma", j - 1)
            I(scalar.activation(ssq[:, (j % 2) * NT:(j % 2 + 1) * NT],
                                sxd[:, (j % 2) * NT:(j % 2 + 1) * NT], AF.Square),
              "s_sq", 1)

    @blk.tensor
    def _(tensor):
        W(tensor, "s_k", 48)
        for j in range(NCH):
            px1.wait(tensor, j)
            for hf in range(2):
                tensor.matmul(psA[0:1, hf * 512:(hf + 1) * 512], sone_f[:],
                              sxd[:, (j % 2) * NT + hf * 512:(j % 2) * NT + (hf + 1) * 512],
                              start=(j == 0), stop=(j == NCH - 1))
            W(tensor, "s_sq", j + 1)
            for hf in range(2):
                mm = tensor.matmul(psB[0:1, hf * 512:(hf + 1) * 512], sone_f[:],
                                   ssq[:, (j % 2) * NT + hf * 512:(j % 2) * NT + (hf + 1) * 512],
                                   start=(j == 0), stop=(j == NCH - 1))
            I(mm, "s_ma", 1)

    @blk.vector
    def _(vector):
        W(vector, "s_ma", NCH)
        vector.tensor_scalar(row("mean"), psA[0:1, :], 1.0 / D, 0.0, ALU.mult, ALU.add)
        vector.drain()
        vector.tensor_tensor(row("tmp"), row("mean"), row("mean"), ALU.mult)
        vector.drain()
        vector.scalar_tensor_tensor(row("tmp"), psB[0:1, :], 1.0 / D, row("tmp"),
                                    ALU.mult, ALU.subtract)
        vector.drain()
        I(vector.tensor_scalar(row("tmp"), row("tmp"), 1e-5, 0.0, ALU.add, ALU.add),
          "s_v", 1)

    @blk.scalar
    def _(scalar):
        W(scalar, "s_v", 1)
        I(scalar.activation(row("tmp2"), row("tmp"), AF.Sqrt), "s_sc", 1)

    @blk.vector
    def _(vector):
        W(vector, "s_sc", 1)
        vector.reciprocal(row("a"), row("tmp2"))
        vector.drain()
        vector.tensor_tensor(row("b"), row("mean"), row("a"), ALU.mult)
        vector.drain()
        I(vector.tensor_scalar(row("b"), row("b"), -1.0, 0.0, ALU.mult, ALU.add),
          "s_v", 1)

    @blk.tensor
    def _(tensor):
        W(tensor, "s_v", 2)
        for hf in range(2):
            tensor.matmul(psC[:, hf * 512:(hf + 1) * 512], sone_r[:],
                          row("a", hf * 512, (hf + 1) * 512), start=True, stop=True)
        for hf in range(2):
            mm = tensor.matmul(psB[:, hf * 512:(hf + 1) * 512], sone_r[:],
                               row("b", hf * 512, (hf + 1) * 512), start=True, stop=True)
        I(mm, "s_mm", 1)

    @blk.vector
    def _(vector):
        W(vector, "s_mm", 1)
        vector.tensor_copy(sbc[:, BO["a"]:BO["a"] + NT], psC[:, :])
        vector.drain()
        I(vector.tensor_copy(sbc[:, BO["b"]:BO["b"] + NT], psB[:, :]), "s_v", 1)

    MB0x = cnt["s_mb"]

    @blk.vector
    def _(vector):
        W(vector, "s_v", 3)
        for j in range(NCH):
            px2.wait(vector, j)
            if j >= 2:
                W(vector, "s_stxa" if (j % 2) == 0 else "s_stxb", 32 * ((j - 2) // 2 + 1))
                W(vector, "s_sq2", j - 1)
            vector.tensor_tensor(smt[:, :], sxd[:, (j % 2) * NT:(j % 2 + 1) * NT],
                                 sbc[:, BO["a"]:BO["a"] + NT], ALU.mult)
            vector.drain()
            I(vector.tensor_tensor(sxnf[:, (j % 2) * NT:(j % 2 + 1) * NT], smt[:, :],
                                   sbc[:, BO["b"]:BO["b"] + NT], ALU.add), "s_vx", 1)
            vector.drain()
            if j >= 1:
                W(vector, "s_sq2", j)
                if j == 1:
                    vector.tensor_copy(smx[:, :], ssq[:, 0:NT])
                else:
                    vector.tensor_tensor(smx[:, :], smx[:, :],
                                         ssq[:, ((j - 1) % 2) * NT:((j - 1) % 2 + 1) * NT],
                                         ALU.max)
                vector.drain()
                I(vector.nop(), "s_vm", 1)
        W(vector, "s_sq2", NCH)
        vector.tensor_tensor(smx[:, :], smx[:, :],
                             ssq[:, ((NCH - 1) % 2) * NT:((NCH - 1) % 2 + 1) * NT],
                             ALU.max)
        vector.drain()
        I(vector.nop(), "s_v", 1)

    @blk.scalar
    def _(scalar):
        for j in range(NCH):
            W(scalar, "s_vx", j + 1)
            if j >= 2:
                if l == 0:
                    W(scalar, "s_mb", MB0x + j - 1)
                W(scalar, "s_vm", j - 1)
            I(scalar.activation(ssq[:, (j % 2) * NT:(j % 2 + 1) * NT],
                                sxnf[:, (j % 2) * NT:(j % 2 + 1) * NT], AF.Square),
              "s_sq2", 1)

    if l == 0:
        @blk.tensor
        def _(tensor):
            for j in range(NCH):
                W(tensor, "s_sq2", j + 1)
                for hf in range(2):
                    mm = tensor.matmul(psA[0:1, hf * 512:(hf + 1) * 512], sone_f[:],
                                       ssq[:, (j % 2) * NT + hf * 512:(j % 2) * NT + (hf + 1) * 512],
                                       start=(j == 0), stop=(j == NCH - 1))
                I(mm, "s_mb", 1)

    @blk.tensor
    def _(tensor):
        W(tensor, "s_v", cnt["s_v"])
        for bb in range(8):
            if bb >= 1:
                W(tensor, "s_vt", bb)
            I(tensor.transpose(psE[:, 0:128], smx[:, bb * 128:(bb + 1) * 128],
                               siden[:]), "s_mt", 1)

    @blk.vector
    def _(vector):
        for bb in range(8):
            W(vector, "s_mt", bb + 1)
            vector.reduce_max(stok[:, bb:bb + 1], psE[:, 0:128], axis=AX.X)
            vector.drain()
            I(vector.nop(), "s_vt", 1)
        I(vector.nop(), "s_v", 1)

    @blk.tensor
    def _(tensor):
        W(tensor, "s_v", cnt["s_v"])
        for bb in range(8):
            if bb >= 1:
                W(tensor, "s_vt", 8 + bb)
            I(tensor.transpose(psD[0:1, 0:128], stok[:, bb:bb + 1], siden[:]), "s_mt", 1)

    @blk.vector
    def _(vector):
        for bb in range(8):
            W(vector, "s_mt", 8 + bb + 1)
            vector.tensor_copy(row("maxa", bb * 128, (bb + 1) * 128), psD[0:1, 0:128])
            vector.drain()
            I(vector.nop(), "s_vt", 1)
        I(vector.nop(), "s_v", 1)
    MAXA1 = cnt["s_v"]

    @blk.scalar
    def _(scalar):
        W(scalar, "s_v", MAXA1)
        I(scalar.activation(row("maxa"), row("maxa"), AF.Sqrt), "s_sc", 1)
    SQMAXA1 = cnt["s_sc"]

    if l == 0:
        @blk.vector
        def _(vector):
            W(vector, "s_mb", MB0x + NCH)
            W(vector, "s_sc", SQMAXA1)
            I(vector.tensor_copy(row("n2"), psA[0:1, :]), "s_v", 1)
            vector.drain()

        @blk.scalar
        def _(scalar):
            W(scalar, "s_v", cnt["s_v"])
            I(scalar.activation(row("tmp2"), row("n2"), AF.Sqrt), "s_sc", 1)

        @blk.vector
        def _(vector):
            W(vector, "s_sc", cnt["s_sc"])
            vector.tensor_scalar(row("tmp2"), row("tmp2"), 1e-12, 0.0, ALU.max, ALU.add)
            vector.drain()
            vector.reciprocal(row("rms"), row("tmp2"))
            vector.drain()
            vector.tensor_scalar(row("rms"), row("rms"), float(np.sqrt(2048.0)), 0.0,
                                 ALU.mult, ALU.add)
            vector.drain()
            vector.tensor_tensor(row("tmp"), row("rms"), row("maxa"), ALU.mult)
            vector.drain()
            vector.tensor_scalar(row("clp"), row("tmp"), 1e-5, 0.0, ALU.max, ALU.add)
            vector.drain()
            vector.reciprocal(row("tmp2"), row("clp"))
            vector.drain()
            vector.tensor_scalar(row("tmp2"), row("tmp2"), 127.0, 0.0, ALU.mult, ALU.add)
            vector.drain()
            I(vector.tensor_tensor(row("g"), row("rms"), row("tmp2"), ALU.mult),
              "s_v", 1)
            vector.drain()
    else:
        @blk.vector
        def _(vector):
            W(vector, "s_sc", SQMAXA1)
            vector.tensor_scalar(row("d1"), row("maxa"), 1.0 / 127.0, 1e-5,
                                 ALU.mult, ALU.add)
            vector.drain()
            I(vector.reciprocal(row("g"), row("d1")), "s_v", 1)
            vector.drain()

    GROW = cnt["s_v"]

    @blk.tensor
    def _(tensor):
        W(tensor, "s_v", GROW)
        for hf in range(2):
            mm = tensor.matmul(psC[:, hf * 512:(hf + 1) * 512], sone_r[:],
                               row("g", hf * 512, (hf + 1) * 512), start=True, stop=True)
        I(mm, "s_mm", 1)

    @blk.vector
    def _(vector):
        W(vector, "s_mm", 2)
        I(vector.tensor_copy(sbc[:, BO["g"]:BO["g"] + NT], psC[:, :]), "s_v", 1)
        vector.drain()
    GBC = cnt["s_v"]

    @blk.gpsimd
    def _(gpsimd):
        for j in range(NCH):
            ssem = "s_stxa" if (j % 2) == 0 else "s_stxb"
            W(gpsimd, ssem, 32 * (j // 2 + 1))
            if j >= 2:
                W(gpsimd, "s_vx", NCH + j - 1)
            pxr.inc(gpsimd.dma_start(out=sh2[:, (j % 2) * NT:(j % 2 + 1) * NT],
                                     in_=xnf[128 * j:128 * (j + 1), :]), j)

    @blk.vector
    def _(vector):
        W(vector, "s_v", GBC)
        for j in range(NCH):
            pxr.wait(vector, j)
            vector.tensor_tensor(smt[:, :], sh2[:, (j % 2) * NT:(j % 2 + 1) * NT],
                                 sbc[:, BO["g"]:BO["g"] + NT], ALU.mult)
            vector.drain()
            I(vector.tensor_scalar(sxn[:, j * NT:(j + 1) * NT], smt[:, :], RC, -RC,
                                   ALU.add, ALU.add), "s_vx", 1)
            vector.drain()
    TI_DONE = cnt["s_vx"]

    @blk.gpsimd
    def _(gpsimd):
        for n, (m, ri, ci) in enumerate(shard_tiles):
            if n >= 2:
                W(gpsimd, "s_pr", n - 1)
            psh.inc(gpsimd.dma_start(
                out=swsh[:, (n % 2) * 1024:(n % 2 + 1) * 1024],
                in_=shdr[m][128 * ri:128 * (ri + 1), 1024 * ci:1024 * (ci + 1)]), n)

    @blk.vector
    def _(vector):
        for n in range(NSH):
            psh.wait(vector, n)
            I(vector.reduce_sum(scol[:, n:n + 1],
                                swsh[:, (n % 2) * 1024:(n % 2 + 1) * 1024],
                                axis=AX.X, apply_absolute_value=True), "s_pr", 1)
            vector.drain()
        base = 0
        for m in range(NMAT):
            vector.reduce_sum(stok[:, m:m + 1], scol[:, base:base + tiles_per_mat[m]],
                              axis=AX.X)
            vector.drain()
            base += tiles_per_mat[m]
        I(vector.nop(), "s_v", 1)
    FOLD1 = cnt["s_v"]

    @blk.tensor
    def _(tensor):
        W(tensor, "s_v", FOLD1)
        for m in range(NMAT):
            if m >= 1:
                W(tensor, "s_vt", 16 + m)
            I(tensor.transpose(psE[0:1, 0:128], stok[:, m:m + 1], siden[:]), "s_mt", 1)

    @blk.vector
    def _(vector):
        for m in range(NMAT):
            W(vector, "s_mt", 16 + m + 1)
            vector.reduce_sum(srsc[0:1, m:m + 1], psE[0:1, 0:128], axis=AX.X)
            vector.drain()
            I(vector.nop(), "s_vt", 1)
        I(vector.nop(), "s_v", 1)
    PARTS = cnt["s_v"]

    @blk.gpsimd
    def _(gpsimd):
        W(gpsimd, "s_v", PARTS)
        I(gpsimd.dma_start(out=arin[0:1, 0:NMAT], in_=srsc[0:1, 0:NMAT]), "s_a", 16)
        W(gpsimd, "s_a", 16)
        I(gpsimd.collective_compute(
            "AllReduce", ALU.add, replica_groups=[list(range(NCORES))],
            ins=[arin.ap().opt()], outs=[arout.ap().opt()]), "s_cc", 1)
        W(gpsimd, "s_cc", 1)
        I(gpsimd.dma_start(out=srsc[0:1, 8:8 + NMAT], in_=arout[0:1, 0:NMAT]),
          "s_ar", 16)

    @blk.vector
    def _(vector):
        W(vector, "s_ar", 16)
        if l == 0:
            vector.tensor_scalar(srsc[0:1, 16:16 + NMAT], srsc[0:1, 8:8 + NMAT],
                                 1.0 / SZW, 1e-5, ALU.mult, ALU.max)
        else:
            vector.tensor_scalar(srsc[0:1, 16:16 + NMAT], srsc[0:1, 8:8 + NMAT],
                                 1.0 / SZW, 1e-5, ALU.mult, ALU.add)
        vector.drain()
        vector.reciprocal(srsc[0:1, 24:24 + NMAT], srsc[0:1, 16:16 + NMAT])
        vector.drain()
        if l == 0:
            vector.tensor_scalar(row("d1"), row("clp"), srsc[0:1, 16:17], 1.0 / 127.0,
                                 ALU.mult, ALU.mult)
            vector.drain()
        I(vector.nop(), "s_v", 1)
    SCROW = cnt["s_v"]

    @blk.tensor
    def _(tensor):
        W(tensor, "s_v", SCROW)
        nb = 0
        for m in range(NMAT):
            tensor.matmul(psD[:, m:m + 1], sone_r[:], srsc[0:1, 24 + m:25 + m],
                          start=(nb == 0), stop=False)
            nb += 1
            mm = tensor.matmul(psD[:, 8 + m:9 + m], sone_r[:], srsc[0:1, 16 + m:17 + m],
                               start=False, stop=(m == NMAT - 1))
            nb += 1
        I(mm, "s_mm", 1)

    @blk.vector
    def _(vector):
        W(vector, "s_mm", 3)
        vector.tensor_copy(swsc[:, 0:NMAT], psD[:, 0:NMAT])
        vector.drain()
        I(vector.tensor_copy(swsc[:, NMAT:2 * NMAT], psD[:, 8:8 + NMAT]), "s_v", 1)
        vector.drain()
    SCBC = cnt["s_v"]

    @blk.tensor
    def _(tensor):
        W(tensor, "s_v", SCBC)
        for hf in range(2):
            mm = tensor.matmul(psA[:, hf * 512:(hf + 1) * 512], sone_r[:],
                               row("d1", hf * 512, (hf + 1) * 512), start=True, stop=True)
        I(mm, "s_mm", 1)

    @blk.vector
    def _(vector):
        W(vector, "s_mm", 4)
        I(vector.tensor_copy(sbc[:, BO["d"]:BO["d"] + NT], psA[:, :]), "s_v", 1)
        vector.drain()
    DBC = cnt["s_v"]

    NB1 = 16 * 16 if l == 0 else 16 * 32
    EV0 = cnt["s_ev"]
    GL0 = cnt["s_gl"]

    def emit_quant(vector, n):
        m = wtiles[n][0]
        pwl.wait(vector, n)
        if n >= 2:
            W(vector, "s_up", n - 1)
        sl = (n % 2) * 512
        I(vector.tensor_scalar(swqs[:, 0:512], swst[:, sl:sl + 512],
                               swsc[:, m:m + 1], -1.0, ALU.mult, ALU.max),
          "s_wq", 1)
        vector.drain()
        vector.tensor_scalar(swqs[:, 0:512], swqs[:, 0:512], 1.0, RC,
                             ALU.min, ALU.add)
        vector.drain()
        I(vector.tensor_scalar(swq[:, sl:sl + 512], swqs[:, 0:512], -RC, 0.0,
                               ALU.add, ALU.add), "s_wu", 1)
        vector.drain()

    def emit_quantA(scalar, n):
        m = wtiles[n][0]
        pwl.wait(scalar, n)
        sl = (n % 2) * 512
        I(scalar.activation(sqa[:, 0:512], swst[:, sl:sl + 512], AF.Relu,
                            bias=env.sconst[:, 0:1], scale=swsc[:, m:m + 1]),
          "s_wqa", 1)
        scalar.drain()
        scalar.activation(sqa[:, 512:1024], sqa[:, 0:512], AF.Relu,
                          bias=env.sconst[:, 3:4], scale=-1.0)
        scalar.drain()
        scalar.activation(sqa[:, 0:512], sqa[:, 512:1024], AF.Relu,
                          bias=env.sconst[:, 1:2], scale=-1.0)
        scalar.drain()
        if n >= 2:
            W(scalar, "s_up", n - 1)
        I(scalar.activation(swq[:, sl:sl + 512], sqa[:, 0:512], AF.Copy,
                            bias=-RC, scale=1.0), "s_wua", 1)
        scalar.drain()

    if l == 0:
        def _stats_piece(vector, p):
            hb, b8 = p // 8, p % 8
            t4, hf = b8 % 4, b8 // 4
            W(vector, "s_hsq", p + 1)
            q4 = (p % 8) * 512
            if hb == 0 and t4 == 0:
                I(vector.tensor_copy(smt[:, hf * 512:(hf + 1) * 512],
                                     ssq[:, q4:q4 + 512]), "s_mh", 1)
            else:
                I(vector.tensor_tensor(smt[:, hf * 512:(hf + 1) * 512],
                                       smt[:, hf * 512:(hf + 1) * 512],
                                       ssq[:, q4:q4 + 512], ALU.max), "s_mh", 1)
            vector.drain()
            if hb == 0 and t4 == 0:
                I(vector.tensor_copy(sacc[:, hf * 512:(hf + 1) * 512],
                                     ssq[:, q4:q4 + 512]), "s_va", 1)
            else:
                I(vector.tensor_tensor(sacc[:, hf * 512:(hf + 1) * 512],
                                       sacc[:, hf * 512:(hf + 1) * 512],
                                       ssq[:, q4:q4 + 512], ALU.add), "s_va", 1)
            vector.drain()

        def emit_evac(vector, hb):
            W(vector, "s_up", (hb + 1) * 16)
            for b8 in range(8):
                t4, hf = b8 % 4, b8 // 4
                if hb >= 1:
                    W(vector, "s_gl", GL0 + (hb - 1) * 8 + b8 + 1)
                I(vector.tensor_tensor(sfo[:, b8 * 512:(b8 + 1) * 512], PB8[b8],
                                       sbc[:, BO["d"] + hf * 512:BO["d"] + (hf + 1) * 512],
                                       ALU.mult), "s_ev", 1)
                vector.drain()
            if hb >= 1:
                for b8 in range(8):
                    _stats_piece(vector, (hb - 1) * 8 + b8)

        @blk.vector
        def _(vector):
            for hb in range(16):
                if hb >= 1:
                    emit_evac(vector, hb - 1)
                for n in range(hb * 16, hb * 16 + 16):
                    if not engA[n]:
                        emit_quant(vector, n)
            emit_evac(vector, 15)
            for b8 in range(8):
                _stats_piece(vector, 15 * 8 + b8)

        @blk.scalar
        def _(scalar):
            W(scalar, "s_v", SCBC)
            W(scalar, "s_kc")
            for hb in range(16):
                for n in range(hb * 16, hb * 16 + 16):
                    if engA[n]:
                        emit_quantA(scalar, n)
                for b8 in range(8):
                    p = hb * 8 + b8
                    t4, hf = b8 % 4, b8 // 4
                    W(scalar, "s_ev", EV0 + p + 1)
                    scalar.activation(sfo[:, b8 * 512:(b8 + 1) * 512],
                                      sfo[:, b8 * 512:(b8 + 1) * 512], AF.Gelu)
                    scalar.drain()
                    sl8 = t4 * 2 + hf
                    if hb >= 1:
                        W(scalar, "s_sth", 16 * hb)
                    I(scalar.activation(shs[:, sl8 * 512:(sl8 + 1) * 512],
                                        sfo[:, b8 * 512:(b8 + 1) * 512], AF.Gelu),
                      "s_gl", 1)
                    scalar.drain()
                    if p >= 8:
                        W(scalar, "s_va", p - 7)
                        W(scalar, "s_mh", p - 7)
                    I(scalar.activation(ssq[:, (p % 8) * 512:(p % 8) * 512 + 512],
                                        shs[:, sl8 * 512:(sl8 + 1) * 512], AF.Square),
                      "s_hsq", 1)
                    scalar.drain()
    else:
        def _tree_piece(vector, p):
            hb, b8 = p // 8, p % 8
            t4, hf = b8 % 4, b8 // 4
            W(vector, "s_hsq", p + 1)
            q4 = (p % 8) * 512
            if hb == 0 and t4 == 0:
                I(vector.tensor_copy(smt[:, hf * 512:(hf + 1) * 512],
                                     ssq[:, q4:q4 + 512]), "s_mh", 1)
            else:
                I(vector.tensor_tensor(smt[:, hf * 512:(hf + 1) * 512],
                                       smt[:, hf * 512:(hf + 1) * 512],
                                       ssq[:, q4:q4 + 512], ALU.max), "s_mh", 1)
            vector.drain()

        def emit_evac_gate(vector, hb):
            W(vector, "s_up", hb * 32 + 16)
            for b8 in range(8):
                if hb >= 1:
                    W(vector, "s_gl", GL0 + (hb - 1) * 8 + b8 + 1)
                I(vector.scalar_tensor_tensor(
                    sfo[:, b8 * 512:(b8 + 1) * 512], PB8[b8],
                    swsc[:, NMAT + 0:NMAT + 1],
                    sbc[:, BO["d"] + (b8 // 4) * 512:BO["d"] + (b8 // 4 + 1) * 512],
                    ALU.mult, ALU.mult), "s_ev", 1)
                vector.drain()

        def emit_evac_up(vector, hb):
            W(vector, "s_up", hb * 32 + 32)
            for b8 in range(8):
                p = hb * 8 + b8
                t4, hf = b8 % 4, b8 // 4
                W(vector, "s_gl", GL0 + hb * 8 + b8 + 1)
                vector.scalar_tensor_tensor(
                    sups[:, :], PB8[b8], swsc[:, NMAT + 1:NMAT + 2],
                    sbc[:, BO["d"] + hf * 512:BO["d"] + (hf + 1) * 512],
                    ALU.mult, ALU.mult)
                vector.drain()
                I(vector.nop(), "s_ev", 1)
                if hb >= 1:
                    W(vector, "s_sth", 16 * hb)
                if p >= 8:
                    W(vector, "s_hsq", p - 7)
                sl8 = t4 * 2 + hf
                I(vector.tensor_tensor(shs[:, sl8 * 512:(sl8 + 1) * 512],
                                       sups[:, :], sgt[:, b8 * 512:(b8 + 1) * 512],
                                       ALU.mult), "s_pc", 1)
                vector.drain()
                if p >= 1:
                    _tree_piece(vector, p - 1)

        @blk.vector
        def _(vector):
            for hb in range(16):
                if hb >= 1:
                    emit_evac_up(vector, hb - 1)
                for n in range(hb * 32, hb * 32 + 16):
                    if not engA[n]:
                        emit_quant(vector, n)
                emit_evac_gate(vector, hb)
                for n in range(hb * 32 + 16, hb * 32 + 32):
                    if not engA[n]:
                        emit_quant(vector, n)
            emit_evac_up(vector, 15)
            _tree_piece(vector, 127)

        @blk.scalar
        def _(scalar):
            W(scalar, "s_v", SCBC)
            W(scalar, "s_kc")
            for hb in range(16):
                for n in range(hb * 32, (hb + 1) * 32):
                    if engA[n]:
                        emit_quantA(scalar, n)
                for b8 in range(8):
                    p = hb * 8 + b8
                    W(scalar, "s_ev", EV0 + hb * 16 + b8 + 1)
                    if hb >= 1:
                        W(scalar, "s_pc", (hb - 1) * 8 + b8 + 1)
                    I(scalar.activation(sgt[:, b8 * 512:(b8 + 1) * 512],
                                        sfo[:, b8 * 512:(b8 + 1) * 512], AF.Silu),
                      "s_gl", 1)
                    scalar.drain()
                    W(scalar, "s_pc", p + 1)
                    if p >= 8:
                        W(scalar, "s_mh", p - 7)
                    sl8 = (b8 % 4) * 2 + b8 // 4
                    I(scalar.activation(ssq[:, (p % 8) * 512:(p % 8) * 512 + 512],
                                        shs[:, sl8 * 512:(sl8 + 1) * 512], AF.Square),
                      "s_hsq", 1)
                    scalar.drain()

    @blk.tensor
    def _(tensor):
        W(tensor, "s_vx", TI_DONE)
        if l == 0:
            for hb in range(16):
                for ic in range(16):
                    n = hb * 16 + ic
                    if engA[n]:
                        W(tensor, "s_wua", apre[n + 1])
                    else:
                        W(tensor, "s_wu", vpre[n + 1])
                    sl = (n % 2) * 512
                    for b8 in range(8):
                        t4, hf = b8 % 4, b8 // 4
                        if ic == 0 and hb >= 1:
                            W(tensor, "s_ev", EV0 + (hb - 1) * 8 + b8 + 1)
                        mm = tensor.matmul(
                            PB8[b8], swq[:, sl + t4 * 128:sl + (t4 + 1) * 128],
                            sxn[:, ic * NT + hf * 512:ic * NT + (hf + 1) * 512],
                            start=(ic == 0), stop=(ic == 15))
                    I(mm, "s_up", 1)
        else:
            for hb in range(16):
                for g_or_u in range(2):
                    for ic in range(16):
                        n = hb * 32 + g_or_u * 16 + ic
                        if engA[n]:
                            W(tensor, "s_wua", apre[n + 1])
                        else:
                            W(tensor, "s_wu", vpre[n + 1])
                        sl = (n % 2) * 512
                        for b8 in range(8):
                            t4, hf = b8 % 4, b8 // 4
                            if ic == 0:
                                prev = hb * 16 if g_or_u == 1 else (hb - 1) * 16 + 8
                                if hb >= 1 or g_or_u == 1:
                                    W(tensor, "s_ev", EV0 + prev + b8 + 1)
                            mm = tensor.matmul(
                                PB8[b8], swq[:, sl + t4 * 128:sl + (t4 + 1) * 128],
                                sxn[:, ic * NT + hf * 512:ic * NT + (hf + 1) * 512],
                                start=(ic == 0), stop=(ic == 15))
                        I(mm, "s_up", 1)

    PC_SEM = "s_gl" if l == 0 else "s_pc"
    PC0 = GL0 if l == 0 else 0
    TPB = 16 if l == 0 else 32

    @blk.gpsimd
    def _(gpsimd):
        def h_store(hb):
            W(gpsimd, PC_SEM, PC0 + (hb + 1) * 8)
            dst = hscr.ap().rearrange("(t p) e -> p t e", p=128)[:, 4 * hb:4 * hb + 4, :]
            srcap = shs.ap().rearrange("p (t e) -> p t e", t=4)
            I(gpsimd.dma_start(out=dst, in_=srcap), "s_sth", 16)

        for hb in range(16):
            h_store(hb)

    MT0 = cnt["s_mt"]
    VT0 = cnt["s_vt"]
    TREE_DONE = cnt["s_mh"]

    if l == 0:
        @blk.tensor
        def _(tensor):
            W(tensor, "s_va", cnt["s_va"])
            W(tensor, "s_ev", cnt["s_ev"])
            for hf in range(2):
                mm = tensor.matmul(psA[0:1, hf * 512:(hf + 1) * 512], sone_f[:],
                                   sacc[:, hf * 512:(hf + 1) * 512],
                                   start=True, stop=True)
            I(mm, "s_mb", 1)
        MB1 = cnt["s_mb"]

    @blk.tensor
    def _(tensor):
        W(tensor, "s_mh", TREE_DONE)
        for bb in range(8):
            if bb >= 1:
                W(tensor, "s_vt", VT0 + bb)
            I(tensor.transpose(psE[:, 0:128], smt[:, bb * 128:(bb + 1) * 128],
                               siden[:]), "s_mt", 1)

    @blk.vector
    def _(vector):
        for bb in range(8):
            W(vector, "s_mt", MT0 + bb + 1)
            vector.reduce_max(stok[:, bb:bb + 1], psE[:, 0:128], axis=AX.X)
            vector.drain()
            I(vector.nop(), "s_vt", 1)
        I(vector.nop(), "s_v", 1)

    @blk.tensor
    def _(tensor):
        W(tensor, "s_v", cnt["s_v"])
        for bb in range(8):
            if bb >= 1:
                W(tensor, "s_vt", VT0 + 8 + bb)
            I(tensor.transpose(psD[0:1, 0:128], stok[:, bb:bb + 1], siden[:]), "s_mt", 1)

    @blk.vector
    def _(vector):
        for bb in range(8):
            W(vector, "s_mt", MT0 + 8 + bb + 1)
            vector.tensor_copy(row("maxa", bb * 128, (bb + 1) * 128), psD[0:1, 0:128])
            vector.drain()
            I(vector.nop(), "s_vt", 1)
        I(vector.nop(), "s_v", 1)
    MAXH = cnt["s_v"]

    @blk.scalar
    def _(scalar):
        W(scalar, "s_v", MAXH)
        I(scalar.activation(row("maxa"), row("maxa"), AF.Sqrt), "s_sc", 1)
    SQMAXH = cnt["s_sc"]

    if l == 0:
        @blk.vector
        def _(vector):
            W(vector, "s_mb", MB1)
            I(vector.tensor_copy(row("n2"), psA[0:1, :]), "s_v", 1)
            vector.drain()

        @blk.scalar
        def _(scalar):
            W(scalar, "s_v", cnt["s_v"])
            I(scalar.activation(row("tmp2"), row("n2"), AF.Sqrt), "s_sc", 1)
        SC2 = cnt["s_sc"]

        @blk.vector
        def _(vector):
            W(vector, "s_sc", SC2)
            W(vector, "s_v", MAXH)
            W(vector, "s_sc", SC2)
            vector.tensor_scalar(row("tmp2"), row("tmp2"), 1e-12, 0.0, ALU.max, ALU.add)
            vector.drain()
            vector.reciprocal(row("rms"), row("tmp2"))
            vector.drain()
            vector.tensor_scalar(row("rms"), row("rms"), float(np.sqrt(8192.0)), 0.0,
                                 ALU.mult, ALU.add)
            vector.drain()
            vector.tensor_tensor(row("tmp"), row("rms"), row("maxa"), ALU.mult)
            vector.drain()
            vector.tensor_scalar(row("clp"), row("tmp"), 1e-5, 0.0, ALU.max, ALU.add)
            vector.drain()
            vector.reciprocal(row("tmp2"), row("clp"))
            vector.drain()
            vector.tensor_scalar(row("tmp2"), row("tmp2"), 127.0, 0.0, ALU.mult, ALU.add)
            vector.drain()
            vector.tensor_tensor(row("g"), row("rms"), row("tmp2"), ALU.mult)
            vector.drain()
            I(vector.tensor_scalar(row("d1"), row("clp"), srsc[0:1, 17:18], 1.0 / 127.0,
                                   ALU.mult, ALU.mult), "s_v", 1)
            vector.drain()
    else:
        @blk.vector
        def _(vector):
            W(vector, "s_v", MAXH)
            W(vector, "s_sc", SQMAXH)
            vector.tensor_scalar(row("tmp"), row("maxa"), 1.0 / 127.0, 1e-5,
                                 ALU.mult, ALU.add)
            vector.drain()
            vector.reciprocal(row("g"), row("tmp"))
            vector.drain()
            I(vector.tensor_scalar(row("d1"), row("tmp"), srsc[0:1, 18:19], 0.0,
                                   ALU.mult, ALU.add), "s_v", 1)
            vector.drain()
    G2ROW = cnt["s_v"]

    @blk.tensor
    def _(tensor):
        W(tensor, "s_v", G2ROW)
        for hf in range(2):
            tensor.matmul(psC[:, hf * 512:(hf + 1) * 512], sone_r[:],
                          row("g", hf * 512, (hf + 1) * 512), start=True, stop=True)
        for hf in range(2):
            mm = tensor.matmul(psB[:, hf * 512:(hf + 1) * 512], sone_r[:],
                               row("d1", hf * 512, (hf + 1) * 512), start=True, stop=True)
        I(mm, "s_mm", 1)

    @blk.vector
    def _(vector):
        W(vector, "s_mm", cnt["s_mm"])
        vector.tensor_copy(sbc[:, BO["g"]:BO["g"] + NT], psC[:, :])
        vector.drain()
        I(vector.tensor_copy(sbc[:, BO["d"]:BO["d"] + NT], psB[:, :]), "s_v", 1)
        vector.drain()
    G2BC = cnt["s_v"]

    VX2_pre = cnt["s_vx"]
    U0 = cnt["s_up"]
    NOFF = NB1
    EV2 = cnt["s_ev"]

    @blk.gpsimd
    def _(gpsimd):
        for ic in range(HCH + 2):
            if ic < HCH:
                if ic >= 2:
                    W(gpsimd, "s_vx", VX2_pre + ic - 1)
                W(gpsimd, "s_sth", 16 * (ic // 4 + 1))
                phr.inc(gpsimd.dma_start(out=sh2[:, (ic % 2) * NT:(ic % 2 + 1) * NT],
                                         in_=hscr[128 * ic:128 * (ic + 1), :]), ic)
            if ic >= 2:
                jc = ic - 2
                W(gpsimd, "s_vx", VX2_pre + jc + 1)
                ptw.inc(gpsimd.dma_start(out=t2scr[128 * jc:128 * (jc + 1), :],
                                         in_=sti2[:, (jc % 2) * NT:(jc % 2 + 1) * NT]), jc)
        for ob in range(1, 4):
            for ic in range(HCH):
                m = (ob - 1) * HCH + ic
                ptw.wait(gpsimd, ic)
                if m >= 2:
                    W(gpsimd, "s_up", U0 + HCH + m - 1)
                phr2.inc(gpsimd.dma_start(out=st2r[:, (m % 2) * NT:(m % 2 + 1) * NT],
                                          in_=t2scr[128 * ic:128 * (ic + 1), :]), m)

    def emit_round2(vector, ic):
        phr.wait(vector, ic)
        vector.tensor_tensor(smt[:, :], sh2[:, (ic % 2) * NT:(ic % 2 + 1) * NT],
                             sbc[:, BO["g"]:BO["g"] + NT], ALU.mult)
        vector.drain()
        if ic >= 2:
            W(vector, "s_up", U0 + ic - 1)
            ptw.wait(vector, ic - 2)
        I(vector.tensor_scalar(sti2[:, (ic % 2) * NT:(ic % 2 + 1) * NT],
                               smt[:, :], RC, -RC, ALU.add, ALU.add), "s_vx", 1)
        vector.drain()

    def emit_evac2(vector, ob):
        W(vector, "s_up", U0 + (ob + 1) * HCH)
        for t4 in range(4):
            ot = ob * 4 + t4
            if ob >= 1:
                pstf.wait(vector, (ob - 1) * 4 + t4)
            for hf in range(2):
                b8 = hf * 4 + t4
                I(vector.tensor_tensor(
                    sfo[:, t4 * 1024 + hf * 512:t4 * 1024 + (hf + 1) * 512],
                    PB8[b8],
                    sbc[:, BO["d"] + hf * 512:BO["d"] + (hf + 1) * 512], ALU.mult),
                  "s_ev", 1)
                vector.drain()
            I(vector.nop(), "s_vf", 1)

    @blk.vector
    def _(vector):
        W(vector, "s_v", G2BC)
        for ob in range(4):
            if ob >= 1:
                emit_evac2(vector, ob - 1)
            for ic in range(HCH):
                if not engA[NOFF + ob * HCH + ic]:
                    emit_quant(vector, NOFF + ob * HCH + ic)
                if ob == 0:
                    emit_round2(vector, ic)
        emit_evac2(vector, 3)

    @blk.tensor
    def _(tensor):
        for ob in range(4):
            for ic in range(HCH):
                n = NOFF + ob * HCH + ic
                if engA[n]:
                    W(tensor, "s_wua", apre[n + 1])
                else:
                    W(tensor, "s_wu", vpre[n + 1])
                if ob == 0:
                    W(tensor, "s_vx", VX2_pre + ic + 1)
                    rhs_base = (ic % 2) * NT
                    rhs_t = sti2
                else:
                    m = (ob - 1) * HCH + ic
                    phr2.wait(tensor, m)
                    rhs_base = (m % 2) * NT
                    rhs_t = st2r
                sl = (n % 2) * 512
                for b8 in range(8):
                    t4, hf = b8 % 4, b8 // 4
                    if ic == 0 and ob >= 1:
                        W(tensor, "s_ev", EV2 + (ob - 1) * 8 + (b8 % 4) * 2 + b8 // 4 + 1)
                    mm = tensor.matmul(
                        PB8[b8], swq[:, sl + t4 * 128:sl + (t4 + 1) * 128],
                        rhs_t[:, rhs_base + hf * 512:rhs_base + (hf + 1) * 512],
                        start=(ic == 0), stop=(ic == HCH - 1))
                I(mm, "s_up", 1)

    @blk.scalar
    def _(scalar):
        def _store(ot):
            W(scalar, "s_vf", ot + 1)
            pstf.inc(scalar.dma_start(out=ffn_o[128 * ot:128 * (ot + 1), :],
                                      in_=sfo[:, (ot % 4) * 1024:(ot % 4) * 1024 + NT]), ot)

        W(scalar, "s_kc")
        for ob in range(4):
            for ic in range(HCH):
                n = NOFF + ob * HCH + ic
                if engA[n]:
                    emit_quantA(scalar, n)
            if ob >= 1:
                for t4 in range(4):
                    _store((ob - 1) * 4 + t4)
        for t4 in range(4):
            _store(12 + t4)
        for t4 in range(4):
            pstf.wait(scalar, 12 + t4)

    @blk.sync
    def _(sync):
        W(sync, "s_stxa")
        W(sync, "s_stxb")


def emit_B(env, l, xst, wq_, wk_, wv_, wo, bands, Tpart, DR):
    nc = env.nc
    SEQ = 4096
    SC = 1.0 / np.sqrt(128.0) if l == 0 else 1.0 / 64.0

    wsl = bsl = None
    ones_f = DR["ones_f"]
    ones_r = DR["ones_r"]
    ones_fb = DR["ones_fb"]
    ones_rb = DR["ones_rb"]
    iden_b = DR["iden_b"]
    iden_f = DR["iden_f"]

    sxn = env.buf("sxn", [128, 16 * 512], BF16)
    swb = env.buf("swb", [128, 16 * 1536], BF16)
    swo = env.buf("swo", [128, 4 * 2048], BF16)
    sq = env.buf("sq", [128, 2 * SEQ], BF16)
    sk = env.buf("sk", [128, 2 * SEQ], BF16)
    sv = env.buf("sv", [128, 2 * 32 * 130], BF16)
    sexp = env.buf("sexp", [128, 2 * 512], BF16)
    sctx = env.buf("sctx", [128, 32 * 128], BF16)
    sctxT = env.buf("sctxT", [128, 4 * SEQ], BF16)
    sfoB = env.buf("sfoB", [128, 8 * 512], BF16)
    ssc = sfoB
    sbnd = env.buf("sbnd", [128, 4 * 512], BF16) if l == 1 else None
    srw = env.buf("srw", [1, 64])
    sbias = env.buf("sbias", [128, 2])
    szq = env.buf("szq", [128, 2])
    sbq = env.buf("sbq", [128, 12]) if l == 0 else None
    sbvr = env.buf("sbvr", [1, 512], BF16) if l == 0 else None
    ssuf = env.buf("ssuf", [1, 9 * 129]) if l == 1 else None
    ssufb = env.buf("ssufb", [1, 8 * 129], BF16) if l == 1 else None
    sone_f = env.buf("sone_f", [128, 1])
    sone_r = env.buf("sone_r", [1, 128])
    sone_fb = env.buf("sone_fb", [128, 1], BF16)
    sone_rb = env.buf("sone_rb", [1, 128], BF16)
    siden = env.buf("siden", [128, 128], BF16)
    siden_f = env.buf("siden_f", [128, 128])

    PB = [env.PB(i) for i in range(8)]

    pxn = env.pdma("pxn", 1)
    pst = env.pdma("pst", 8)

    blk = env.blk
    I, W, cnt = env.I, env.W, env.cnt

    @blk.gpsimd
    def _(gpsimd):
        if l == 0:
            for j in range(NCH):
                I(gpsimd.dma_start(out=swb[:, j * 1536:(j + 1) * 1536],
                                   in_=wsl[128 * j:128 * (j + 1), :]), "s_w", 16)
        else:
            for j in range(NCH):
                I(gpsimd.dma_start(out=swb[:, j * 1536:j * 1536 + 512],
                                   in_=wq_[128 * j:128 * (j + 1), :]), "s_w", 16)
                I(gpsimd.dma_start(out=swb[:, j * 1536 + 512:j * 1536 + 1024],
                                   in_=wk_[128 * j:128 * (j + 1), :]), "s_w", 16)
                I(gpsimd.dma_start(out=swb[:, j * 1536 + 1024:j * 1536 + 1536],
                                   in_=wv_[128 * j:128 * (j + 1), :]), "s_w", 16)
        for j in range(4):
            I(gpsimd.dma_start(out=swo[:, j * 2048:(j + 1) * 2048],
                               in_=wo[128 * j:128 * (j + 1), :]), "s_w", 16)
        if l == 0:
            for t in range(12):
                I(gpsimd.dma_start(out=sbq[:, t:t + 1],
                                   in_=bsl[128 * t:128 * (t + 1), :]), "s_w", 16)
        gpsimd.memset(sv.ap().rearrange("p (t w) -> p t w", w=130)[:, :, 128:129], 1.0)
        gpsimd.drain()
        I(gpsimd.nop(), "s_bs", 1)

    @blk.sync
    def _(sync):
        I(sync.dma_start(out=sone_f[:], in_=ones_f[:, :]), "s_k", 16)
        I(sync.dma_start(out=sone_r[:], in_=ones_r[:, :]), "s_k", 16)
        I(sync.dma_start(out=sone_fb[:], in_=ones_fb[:, :]), "s_k", 16)
        I(sync.dma_start(out=sone_rb[:], in_=ones_rb[:, :]), "s_k", 16)
        I(sync.dma_start(out=siden[:], in_=iden_b[:, :]), "s_k", 16)
        I(sync.dma_start(out=siden_f[:], in_=iden_f[:, :]), "s_k", 16)
        if l == 1:
            I(sync.dma_start(out=sbnd[:, :], in_=bands[:, :]), "s_k", 16)
        K_ALL = cnt["s_k"]
        for hp in range(2):
            for tb in range(8):
                i = hp * 8 + tb
                if i >= 1:
                    W(sync, "s_mq", i * 8)
                src = xst.ap().rearrange("(k j p) t -> p k j t", k=4, p=128)[
                    :, tb // 2, :, 512 * (tb % 2):512 * (tb % 2) + 512]
                pxn.inc(sync.dma_start(out=sxn.ap().rearrange("p (j t) -> p j t", t=512),
                                       in_=src), i)

    def qkv_phase(hp):
        @blk.tensor
        def _(tensor):
            if hp == 0:
                W(tensor, "s_k", 96 if l == 0 else 112)
                W(tensor, "s_w", cnt["s_w"])
                W(tensor, "s_bs", 1)
                if l == 0:
                    W(tensor, "s_vb", cnt["s_vb"])
            else:
                W(tensor, "s_vt", cnt["s_vt"])
                W(tensor, "s_cv", cnt["s_cv"])
                for bk in range(8):
                    head_state["ctxT_free"][bk % 4] = None
                    head_state["av_free"][bk % 4] = None
            for tb in range(8):
                i = hp * 8 + tb
                pxn.wait(tensor, i)
                sb = 0
                if i >= 1:
                    W(tensor, "s_vq", i * 8)
                for h2 in range(2):
                    hh = hp * 2 + h2
                    for j in range(NCH):
                        mm = tensor.matmul(PB[h2][:, :],
                                           swb[:, j * 1536 + hh * 128:j * 1536 + (hh + 1) * 128],
                                           sxn[:, sb + j * 512:sb + (j + 1) * 512],
                                           start=(j == 0), stop=(j == NCH - 1))
                    I(mm, "s_mq", 1)
                    for j in range(NCH):
                        mm = tensor.matmul(PB[2 + h2][:, :],
                                           swb[:, j * 1536 + 512 + hh * 128:j * 1536 + 512 + (hh + 1) * 128],
                                           sxn[:, sb + j * 512:sb + (j + 1) * 512],
                                           start=(j == 0), stop=(j == NCH - 1))
                    I(mm, "s_mq", 1)
                for tt in range(4):
                    for j in range(NCH):
                        last = (j == NCH - 1) and (l == 1)
                        mm = tensor.matmul(PB[4 + tt][:, 0:256],
                                           sxn[:, sb + j * 512 + tt * 128:sb + j * 512 + (tt + 1) * 128],
                                           swb[:, j * 1536 + 1024 + hp * 256:j * 1536 + 1024 + (hp + 1) * 256],
                                           start=(j == 0), stop=last)
                    if l == 0:
                        mm = tensor.matmul(PB[4 + tt][:, 0:256], sone_rb[:],
                                           sbvr[0:1, hp * 256:(hp + 1) * 256],
                                           start=False, stop=True)
                    I(mm, "s_mq", 1)

        @blk.vector
        def _(vector):
            for tb in range(8):
                i = hp * 8 + tb
                base = i * 8
                for h2 in range(2):
                    W(vector, "s_mq", base + 2 * h2 + 1)
                    if l == 0:
                        vector.tensor_scalar(sq[:, h2 * SEQ + tb * 512:h2 * SEQ + (tb + 1) * 512],
                                             PB[h2][:, :], sbq[:, (hp * 2 + h2):(hp * 2 + h2) + 1],
                                             0.0, ALU.add, ALU.add)
                    else:
                        vector.tensor_copy(sq[:, h2 * SEQ + tb * 512:h2 * SEQ + (tb + 1) * 512],
                                           PB[h2][:, :])
                    vector.drain()
                    I(vector.nop(), "s_vq", 1)
                    W(vector, "s_mq", base + 2 * h2 + 2)
                    if l == 0:
                        vector.tensor_scalar(sk[:, h2 * SEQ + tb * 512:h2 * SEQ + (tb + 1) * 512],
                                             PB[2 + h2][:, :], sbq[:, 4 + hp * 2 + h2:5 + hp * 2 + h2],
                                             0.0, ALU.add, ALU.add)
                    else:
                        vector.tensor_copy(sk[:, h2 * SEQ + tb * 512:h2 * SEQ + (tb + 1) * 512],
                                           PB[2 + h2][:, :])
                    vector.drain()
                    I(vector.nop(), "s_vq", 1)
                for tt in range(4):
                    W(vector, "s_mq", base + 5 + tt)
                    for h2 in range(2):
                        vector.tensor_copy(
                            sv.ap().rearrange("p (s t w) -> p s t w", s=2, w=130)[:, h2, tb * 4 + tt, 0:128],
                            PB[4 + tt][:, h2 * 128:(h2 + 1) * 128])
                        vector.drain()
                    I(vector.nop(), "s_vq", 1)

    if l == 0:
        @blk.vector
        def _(vector):
            W(vector, "s_w", cnt["s_w"])
            I(vector.nop(), "s_vb", 1)

        @blk.tensor
        def _(tensor):
            W(tensor, "s_vb", 1)
            W(tensor, "s_k", 96)
            for t in range(4):
                mm = tensor.matmul(PB[t][0:1, 0:128],
                                   sbq[:, 8 + t:9 + t], siden_f[:],
                                   start=True, stop=True)
                I(mm, "s_mb", 1)

        @blk.vector
        def _(vector):
            W(vector, "s_mb", 4)
            for t in range(4):
                vector.tensor_copy(sbvr[0:1, t * 128:(t + 1) * 128], PB[t][0:1, 0:128])
                vector.drain()
            I(vector.nop(), "s_vb", 1)

    def bounds_head(hp, h2):
        hh = hp * 2 + h2
        MB_b = cnt["s_mb"]
        VB0 = cnt["s_vb"]

        @blk.scalar
        def _(scalar):
            W(scalar, "s_vq", 64 * (hp + 1))
            for side in range(2):
                for pc in range(8):
                    i = side * 8 + pc
                    W(scalar, "s_mb", MB_b + i)
                    srcbuf = sq if side == 0 else sk
                    I(scalar.activation(ssc[:, 3584:4096], srcbuf[:, h2 * SEQ + pc * 512:h2 * SEQ + (pc + 1) * 512],
                                        AF.Square), "s_sb", 1)

        SB0 = cnt["s_sb"] - 16
        MB0 = cnt["s_mb"]

        @blk.tensor
        def _(tensor):
            for bk in range(2):
                if head_state["ctxT_free"][bk] is not None:
                    W(tensor, "s_vt", head_state["ctxT_free"][bk])
                    head_state["ctxT_free"][bk] = None
            for i in range(16):
                W(tensor, "s_sb", SB0 + i + 1)
                if i >= 1:
                    W(tensor, "s_vb", VB0 + i)
                I(tensor.matmul(PB[0][0:1, 0:512], sone_fb[:], ssc[:, 3584:4096],
                                start=True, stop=True), "s_mb", 1)

        @blk.vector
        def _(vector):
            for i in range(16):
                W(vector, "s_mb", MB0 + i + 1)
                vector.reduce_max(srw[0:1, 8 + i % 2:9 + i % 2] if False else srw[0:1, 8:9],
                                  PB[0][0:1, 0:512], axis=AX.X)
                vector.drain()
                side = i // 8
                if i % 8 == 0:
                    vector.tensor_copy(srw[0:1, side:side + 1], srw[0:1, 8:9])
                else:
                    vector.tensor_tensor(srw[0:1, side:side + 1], srw[0:1, side:side + 1],
                                         srw[0:1, 8:9], ALU.max)
                vector.drain()
                I(vector.nop(), "s_vb", 1)
            I(vector.tensor_tensor(srw[0:1, 2:3], srw[0:1, 0:1], srw[0:1, 1:2],
                                   ALU.mult), "s_vb", 1)
            vector.drain()

        @blk.scalar
        def _(scalar):
            W(scalar, "s_vb", cnt["s_vb"])
            I(scalar.activation(srw[0:1, 2:3], srw[0:1, 2:3], AF.Sqrt), "s_sb", 1)
            scalar.drain()
            if l == 1:
                I(scalar.activation(srw[0:1, 4:5], srw[0:1, 2:3], AF.Exp,
                                    scale=-SC), "s_sb", 1)
                scalar.drain()

        @blk.vector
        def _(vector):
            W(vector, "s_sb", cnt["s_sb"])
            I(vector.tensor_scalar(srw[0:1, 3:4], srw[0:1, 2:3], -SC, 0.0,
                                   ALU.mult, ALU.add), "s_vb", 1)
            vector.drain()

        @blk.tensor
        def _(tensor):
            W(tensor, "s_vb", cnt["s_vb"])
            I(tensor.matmul(PB[2 + h2][:, 0:1], sone_r[:], srw[0:1, 3:4],
                            start=True, stop=True), "s_mb", 1)

        @blk.vector
        def _(vector):
            W(vector, "s_mb", cnt["s_mb"])
            I(vector.tensor_copy(sbias[:, h2:h2 + 1], PB[2 + h2][:, 0:1]), "s_vb", 1)
            vector.drain()

    def suffix_head(hp, h2):
        VB_b = cnt["s_vb"]

        @blk.tensor
        def _(tensor):
            for b in range(8):
                if b >= 1:
                    W(tensor, "s_vb", VB_b + b)
                for t in range(4):
                    mm = tensor.matmul(
                        PB[1][0:1, 0:129], sone_fb[:],
                        sv.ap().rearrange("p (s t w) -> p s t w", s=2, w=130)[:, h2, b * 4 + t, 0:129],
                        start=(t == 0), stop=(t == 3))
                I(mm, "s_mb", 1)

        MB0 = cnt["s_mb"] - 8

        @blk.vector
        def _(vector):
            for b in range(8):
                W(vector, "s_mb", MB0 + b + 1)
                vector.tensor_copy(ssuf[0:1, b * 129:(b + 1) * 129], PB[1][0:1, 0:129])
                vector.drain()
                I(vector.nop(), "s_vb", 1)
            vector.tensor_copy(ssuf[0:1, 8 * 129:9 * 129], ssuf[0:1, 7 * 129:8 * 129])
            vector.drain()
            I(vector.tensor_scalar(ssufb[0:1, 6 * 129:7 * 129], ssuf[0:1, 8 * 129:9 * 129],
                                   srw[0:1, 4:5], 0.0, ALU.mult, ALU.add), "s_vb", 1)
            vector.drain()
            for Q in range(5, -1, -1):
                vector.tensor_tensor(ssuf[0:1, 8 * 129:9 * 129], ssuf[0:1, 8 * 129:9 * 129],
                                     ssuf[0:1, (Q + 1) * 129:(Q + 2) * 129], ALU.add)
                vector.drain()
                I(vector.tensor_scalar(ssufb[0:1, Q * 129:(Q + 1) * 129],
                                       ssuf[0:1, 8 * 129:9 * 129],
                                       srw[0:1, 4:5], 0.0, ALU.mult, ALU.add), "s_vb", 1)
                vector.drain()

    head_state = {"av_free": [None] * 4, "bank_free": [None] * 4, "ek": 0,
                  "ctxT_free": [None] * 4}

    def attn_head(hp, h2):
        hh = hp * 2 + h2
        KMs = [32] * 8 if l == 0 else [4 * (Q + 1) for Q in range(8)]
        pre = [0]
        for Q in range(8):
            pre.append(pre[-1] + KMs[Q])
        KMtot = pre[-1]
        ek0 = head_state["ek"]
        ak0 = cnt["s_ak"]
        cv0 = cnt["s_cv"]
        mk0 = cnt["s_mk"]
        vk0 = cnt["s_vk"]
        svbase = h2 * 32 * 130

        def av_inc(Q, kc):
            sfx = Q if l == 1 else 0
            return ak0 + pre[Q] + kc + 1 + sfx

        def emit_av(tensor, Q, kc, KM):
            W(tensor, "s_ek", ek0 + pre[Q] + kc + 1)
            for qt in range(4):
                if kc == 0 and head_state["av_free"][qt] is not None:
                    W(tensor, "s_cv", head_state["av_free"][qt])
                last = (kc == KM - 1) and (l == 0 or Q == 7)
                mm = tensor.matmul(
                    PB[4 + qt][:, 0:129],
                    sexp[:, (kc % 2) * 512 + qt * 128:(kc % 2) * 512 + (qt + 1) * 128],
                    sv[:, (svbase + kc * 130):(svbase + kc * 130 + 129)],
                    start=(kc == 0), stop=last)
            I(mm, "s_ak", 1)

        @blk.tensor
        def _(tensor):
            W(tensor, "s_vb", cnt["s_vb"])
            for Q in range(8):
                KM = KMs[Q]
                for kc in range(KM):
                    bk = kc % 4
                    g = pre[Q] + kc
                    if head_state["bank_free"][bk] is not None:
                        W(tensor, "s_ek", head_state["bank_free"][bk])
                    if head_state["ctxT_free"][bk] is not None:
                        W(tensor, "s_vt", head_state["ctxT_free"][bk])
                        head_state["ctxT_free"][bk] = None
                    head_state["bank_free"][bk] = ek0 + g + 1
                    I(tensor.matmul(PB[bk][:, :],
                                    sk[:, h2 * SEQ + kc * 128:h2 * SEQ + (kc + 1) * 128],
                                    sq[:, h2 * SEQ + Q * 512:h2 * SEQ + (Q + 1) * 512],
                                    start=True, stop=True), "s_mk", 1)
                    if kc >= 1:
                        emit_av(tensor, Q, kc - 1, KM)
                emit_av(tensor, KM and Q, KM - 1, KM)
                if l == 1 and Q < 7:
                    for qt in range(4):
                        mm = tensor.matmul(PB[4 + qt][:, 0:129], sone_rb[:],
                                           ssufb[0:1, Q * 129:(Q + 1) * 129],
                                           start=False, stop=True)
                    I(mm, "s_ak", 1)
                for qt in range(4):
                    head_state["av_free"][qt] = cv0 + Q * 4 + qt + 1

        @blk.scalar
        def _(scalar):
            for Q in range(8):
                KM = KMs[Q]
                for kc in range(KM):
                    g = pre[Q] + kc
                    if g >= 2:
                        gq = g - 2
                        Q2 = 0
                        while pre[Q2 + 1] <= gq:
                            Q2 += 1
                        W(scalar, "s_ak", av_inc(Q2, gq - pre[Q2]))
                    if l == 1 and kc >= 4 * Q:
                        W(scalar, "s_vk", vk0 + 4 * Q + (kc - 4 * Q) + 1)
                    else:
                        W(scalar, "s_mk", mk0 + g + 1)
                    I(scalar.activation(sexp[:, (kc % 2) * 512:(kc % 2 + 1) * 512],
                                        PB[kc % 4][:, :], AF.Exp,
                                        bias=sbias[:, h2:h2 + 1], scale=SC),
                      "s_ek", 1)
        head_state["ek"] = ek0 + KMtot

        def _evac_block(vector, Q):
            KM = KMs[Q]
            for qt in range(4):
                W(vector, "s_ak", av_inc(Q, KM - 1) + (1 if (l == 1 and Q < 7) else 0))
                vector.tensor_copy(szq[:, 0:1], PB[4 + qt][:, 128:129])
                vector.drain()
                vector.reciprocal(szq[:, 1:2], szq[:, 0:1])
                vector.drain()
                I(vector.tensor_scalar(sctx[:, (Q * 4 + qt) * 128:(Q * 4 + qt + 1) * 128],
                                       PB[4 + qt][:, 0:128], szq[:, 1:2], 0.0,
                                       ALU.mult, ALU.add), "s_cv", 1)
                vector.drain()

        @blk.vector
        def _(vector):
            for Q in range(8):
                if Q >= 1:
                    _evac_block(vector, Q - 1)
                if l == 1:
                    for j in range(4):
                        kc = 4 * Q + j
                        g = pre[Q] + kc
                        W(vector, "s_mk", mk0 + g + 1)
                        I(vector.tensor_tensor(PB[kc % 4][:, :], PB[kc % 4][:, :],
                                               sbnd[:, j * 512:(j + 1) * 512],
                                               ALU.mult), "s_vk", 1)
                        vector.drain()
            _evac_block(vector, 7)

        MT0 = cnt["s_mt"]
        VT0 = cnt["s_vt"]

        @blk.tensor
        def _(tensor):
            for i32 in range(32):
                bk = i32 % 4
                W(tensor, "s_cv", cv0 + i32 + 1)
                if head_state["bank_free"][bk] is not None:
                    W(tensor, "s_ek", head_state["bank_free"][bk])
                    head_state["bank_free"][bk] = None
                if i32 >= 4:
                    W(tensor, "s_vt", VT0 + i32 - 3)
                I(tensor.matmul(PB[bk][:, 0:128], sctx[:, i32 * 128:(i32 + 1) * 128],
                                siden[:], start=True, stop=True), "s_mt", 1)

        @blk.vector
        def _(vector):
            for i32 in range(32):
                bk = i32 % 4
                W(vector, "s_mt", MT0 + i32 + 1)
                vector.tensor_copy(sctxT[:, hh * SEQ + i32 * 128:hh * SEQ + (i32 + 1) * 128],
                                   PB[bk][:, 0:128])
                vector.drain()
                I(vector.nop(), "s_vt", 1)
        for bk in range(4):
            head_state["ctxT_free"][bk] = VT0 + 29 + bk

    def proj_phase():
        MP0 = cnt["s_mp"]
        VP0 = cnt["s_vp"]

        @blk.tensor
        def _(tensor):
            g = 0
            for ot in range(16):
                for tb in range(8):
                    bk = g % 8
                    if g >= 8:
                        W(tensor, "s_vp", VP0 + g - 7)
                    elif bk < 4:
                        if head_state["ctxT_free"][bk] is not None:
                            W(tensor, "s_vt", head_state["ctxT_free"][bk])
                            head_state["ctxT_free"][bk] = None
                    else:
                        if head_state["av_free"][bk - 4] is not None:
                            W(tensor, "s_cv", head_state["av_free"][bk - 4])
                            head_state["av_free"][bk - 4] = None
                    for fc in range(4):
                        mm = tensor.matmul(PB[bk][:, :],
                                           swo[:, fc * 2048 + ot * 128:fc * 2048 + (ot + 1) * 128],
                                           sctxT[:, fc * SEQ + tb * 512:fc * SEQ + (tb + 1) * 512],
                                           start=(fc == 0), stop=(fc == 3))
                    I(mm, "s_mp", 1)
                    g += 1

        @blk.vector
        def _(vector):
            for g in range(128):
                bk = g % 8
                W(vector, "s_mp", MP0 + g + 1)
                if g >= 8:
                    pst.wait(vector, g - 8)
                vector.tensor_copy(sfoB[:, bk * 512:(bk + 1) * 512], PB[bk][:, :])
                vector.drain()
                I(vector.nop(), "s_vp", 1)

        @blk.scalar
        def _(scalar):
            for g in range(128):
                ot, tb = g // 8, g % 8
                bk = g % 8
                W(scalar, "s_vp", VP0 + g + 1)
                ro = 2048 * (tb // 2) + 128 * ot
                co = 512 * (tb % 2)
                pst.inc(scalar.dma_start(out=Tpart[ro:ro + 128, co:co + 512],
                                         in_=sfoB[:, bk * 512:(bk + 1) * 512]), g)
            for i in range(8):
                pst.wait(scalar, 120 + i)

    qkv_phase(0)
    for h2 in range(2):
        bounds_head(0, h2)
        if l == 1:
            suffix_head(0, h2)
        attn_head(0, h2)
    qkv_phase(1)
    for h2 in range(2):
        bounds_head(1, h2)
        if l == 1:
            suffix_head(1, h2)
        attn_head(1, h2)
    proj_phase()


def emit_C1(env, xT, ffn, psl, xo):
    sx = env.buf("sx", [128, 2 * NT])
    sf = env.buf("sf", [128, 2 * NT])
    sp = env.buf("sp", [128, 2 * NT], BF16)
    so = env.buf("so", [128, 2 * NT])
    px = env.pdma("px")
    pf = env.pdma("pf")
    pp2 = env.pdma("psc", 4)
    po = env.pdma("po")
    blk = env.blk
    I, W = env.I, env.W

    @blk.sync
    def _(sync):
        for j in range(NCH):
            if j >= 2:
                W(sync, "s_v", j - 1)
            sl = (j % 2)
            px.inc(sync.dma_start(out=sx[:, sl * NT:(sl + 1) * NT],
                                  in_=xT[128 * j:128 * (j + 1), :]), j)
            pf.inc(sync.dma_start(out=sf[:, sl * NT:(sl + 1) * NT],
                                  in_=ffn[128 * j:128 * (j + 1), :]), j)
            pp2.inc(sync.dma_start(out=sp[:, sl * NT:(sl + 1) * NT],
                                   in_=psl[128 * j:128 * (j + 1), :]), j)

    @blk.vector
    def _(vector):
        for j in range(NCH):
            sl = (j % 2)
            px.wait(vector, j)
            pf.wait(vector, j)
            pp2.wait(vector, j)
            if j >= 2:
                po.wait(vector, j - 2)
            vector.tensor_tensor(so[:, sl * NT:(sl + 1) * NT],
                                 sp[:, sl * NT:(sl + 1) * NT],
                                 sx[:, sl * NT:(sl + 1) * NT], ALU.add)
            vector.drain()
            I(vector.tensor_tensor(so[:, sl * NT:(sl + 1) * NT],
                                   so[:, sl * NT:(sl + 1) * NT],
                                   sf[:, sl * NT:(sl + 1) * NT], ALU.add), "s_v", 1)
            vector.drain()

    @blk.scalar
    def _(scalar):
        for j in range(NCH):
            W(scalar, "s_v", j + 1)
            po.inc(scalar.dma_start(out=xo[128 * j:128 * (j + 1), :],
                                    in_=so[:, (j % 2) * NT:(j % 2 + 1) * NT]), j)
        po.wait(scalar, NCH - 2)
        po.wait(scalar, NCH - 1)


def emit_B0(env, xn, wqkv, bqkv, Tp, DR):
    nc = env.nc
    SC = 1.0 / np.sqrt(128.0)

    ones_rb = DR["ones_rb"]

    sxnb = env.buf("sxnb", [128, NCH * NT], BF16)
    sqkv = env.buf("sqkv", [128, 8 * 6144], BF16)
    swt = env.buf("swt", [128, 2 * 512], BF16)
    sbrf = env.buf("sbrf", [1, 3 * D])
    sbrb = env.buf("sbrb", [1, 3 * D], BF16)
    sone_rb = env.buf("sone_rb", [1, 128], BF16)
    sP = env.buf("sP", [128, 2048], BF16)
    sS = env.buf("sS", [128, 256])
    sE = env.buf("sE", [128, 256])
    sm = env.buf("sm", [128, 1])
    sZ = env.buf("sZ", [128, 32])
    sacc = env.buf("sacc", [128, 2048])
    sco = env.buf("sco", [128, 2 * 2048], BF16)

    PB = [env.PB(i) for i in range(8)]

    pwl = env.pdma("pwg")
    pco = env.pdma("px")

    blk = env.blk
    I, W, cnt = env.I, env.W, env.cnt

    @blk.sync
    def _(sync):
        I(sync.dma_start(out=sbrf[:, :], in_=bqkv[:, :]), "s_k", 16)
        I(sync.dma_start(out=sone_rb[:], in_=ones_rb[:, :]), "s_k", 16)
        for j in range(NCH):
            I(sync.dma_start(out=sxnb[:, j * NT:(j + 1) * NT],
                             in_=xn[128 * j:128 * (j + 1), :]), "s_xa", 16)

    @blk.vector
    def _(vector):
        W(vector, "s_k", 32)
        vector.tensor_copy(sbrb[:, :], sbrf[:, :])
        vector.drain()
        I(vector.nop(), "s_vx", 1)

    def pe_done(n):
        cb, fc = n // 16, n % 16
        return cb * 136 + (fc + 1) * 8

    @blk.gpsimd
    def _(gpsimd):
        for n in range(12 * NCH):
            if n >= 2:
                W(gpsimd, "s_mq", pe_done(n - 2))
            cb, fc = n // 16, n % 16
            pwl.inc(gpsimd.dma_start(
                out=swt[:, (n % 2) * 512:(n % 2 + 1) * 512],
                in_=wqkv[128 * fc:128 * (fc + 1), 512 * cb:512 * (cb + 1)]), n)

    @blk.tensor
    def _(tensor):
        W(tensor, "s_vx", 1)
        W(tensor, "s_xa", 16 * NCH)
        for cb in range(12):
            if cb >= 1:
                W(tensor, "s_vq", cb * 8)
            for fc in range(NCH):
                n = cb * NCH + fc
                pwl.wait(tensor, n)
                sl = (n % 2) * 512
                for tt in range(8):
                    mm = tensor.matmul(PB[tt][:, :],
                                       sxnb[:, fc * NT + tt * 128:fc * NT + (tt + 1) * 128],
                                       swt[:, sl:sl + 512],
                                       start=(fc == 0), stop=False)
                I(mm, "s_mq", 8)
            for tt in range(8):
                mm = tensor.matmul(PB[tt][:, :], sone_rb[:],
                                   sbrb[0:1, 512 * cb:512 * (cb + 1)],
                                   start=False, stop=True)
            I(mm, "s_mq", 8)

    @blk.vector
    def _(vector):
        for cb in range(12):
            W(vector, "s_mq", (cb + 1) * 136)
            for tt in range(8):
                dst = sqkv[:, tt * 6144 + cb * 512:tt * 6144 + (cb + 1) * 512]
                if cb < 4:
                    vector.tensor_scalar(dst, PB[tt][:, :], SC, 0.0, ALU.mult, ALU.add)
                else:
                    vector.tensor_copy(dst, PB[tt][:, :])
                vector.drain()
            I(vector.nop(), "s_vq", 8)

    VQ_ALL = 12 * 8

    @blk.vector
    def _(vector):
        W(vector, "s_vq", VQ_ALL)
        for tt in range(8):
            qb, kb, vb = tt * 6144, tt * 6144 + 2048, tt * 6144 + 4096
            if tt >= 1:
                W(vector, "s_ex", tt)
            for o in range(16):
                lim = (16 - o) * 128
                vector.tensor_tensor(sP[:, 0:lim], sqkv[:, qb:qb + lim],
                                     sqkv[:, kb + o * 128:kb + 2048], ALU.mult)
                if o > 0:
                    vector.tensor_tensor(sP[:, lim:2048], sqkv[:, qb + lim:qb + 2048],
                                         sqkv[:, kb:kb + o * 128], ALU.mult)
                vector.drain()
                vector.reduce_sum(sS[:, o * 16:(o + 1) * 16],
                                  sP.ap().rearrange("p (h e) -> p h e", e=128),
                                  axis=AX.X)
                vector.drain()
            I(vector.reduce_max(sm[:, :], sS[:, :], axis=AX.X, negate=True), "s_s0", 1)
            vector.drain()
            W(vector, "s_ex", tt + 1)
            if tt >= 1:
                W(vector, "s_co", 16 * tt)
            vector.reduce_sum(sZ[:, 0:16],
                              sE.ap().rearrange("p (o h) -> p h o", h=16),
                              axis=AX.X)
            vector.drain()
            vector.reciprocal(sZ[:, 16:32], sZ[:, 0:16])
            vector.drain()
            I(vector.nop(), "s_zv", 1)
            for t in range(16):
                vsl = sqkv[:, vb + t * 128:vb + (t + 1) * 128]
                for h in range(16):
                    col = ((t - h) % 16) * 16 + h
                    if t == 0:
                        vector.tensor_scalar(sacc[:, h * 128:(h + 1) * 128], vsl,
                                             sE[:, col:col + 1], 0.0, ALU.mult, ALU.add)
                    else:
                        vector.scalar_tensor_tensor(sacc[:, h * 128:(h + 1) * 128], vsl,
                                                    sE[:, col:col + 1],
                                                    sacc[:, h * 128:(h + 1) * 128],
                                                    ALU.mult, ALU.add)
                vector.drain()
            if tt >= 2:
                pco.wait(vector, tt - 2)
            for h in range(16):
                I(vector.tensor_scalar(sco[:, (tt % 2) * 2048 + h * 128:(tt % 2) * 2048 + (h + 1) * 128],
                                       sacc[:, h * 128:(h + 1) * 128],
                                       sZ[:, 16 + h:17 + h], 0.0, ALU.mult, ALU.add),
                  "s_co", 1)
            vector.drain()

    @blk.scalar
    def _(scalar):
        for tt in range(8):
            W(scalar, "s_s0", tt + 1)
            if tt >= 1:
                W(scalar, "s_co", 16 * tt)
            I(scalar.activation(sE[:, :], sS[:, :], AF.Exp, bias=sm[:, :]), "s_ex", 1)
            scalar.drain()

    @blk.sync
    def _(sync):
        TpV = Tp.ap().rearrange("(g t) d -> t g d", g=4)
        for tt in range(8):
            W(sync, "s_co", 16 * (tt + 1))
            pco.inc(sync.dma_start(
                out=TpV[128 * tt:128 * (tt + 1)],
                in_=sco.ap().rearrange("p (s g d) -> p s g d", s=2, g=4)[:, tt % 2]),
                tt)
        pco.wait(sync, 6)
        pco.wait(sync, 7)


def emit_SC(env, Tp, Gc, Tq, ctxR, Yt, DR):
    blk = env.blk
    I, W, cnt = env.I, env.W, env.cnt
    PB = [env.PB(i) for i in range(8)]
    sld = env.buf("sld", [128, 4 * 128], BF16)
    sct = env.buf("sct", [128, 4 * 128], BF16)
    sperm = env.buf("sperm", [128, 128], BF16)
    psc = env.pdma("psc", 4)
    pq = [env.pdma("pstf", 4), env.pdma("px")]

    @blk.gpsimd
    def _(g):
        I(g.collective_compute("AllGather", ALU.bypass, replica_groups=G4,
                               ins=[Tp.ap().opt()], outs=[Gc.ap().opt()]),
          "s_cc", 1)
        W(g, "s_cc")
        for k in range(4):
            for gi in range(4):
                I(g.dma_start(
                    out=Tq[4096 * gi + 1024 * k:4096 * gi + 1024 * (k + 1), :],
                    in_=Gc[4096 * k + 1024 * gi:4096 * k + 1024 * (gi + 1), :]),
                  "s_pm", 16)
        W(g, "s_pm")
        I(g.collective_compute("ReduceScatter", ALU.max, replica_groups=G4,
                               ins=[Tq.ap().opt()], outs=[ctxR.ap().opt()]),
          "s_cc", 1)

    YV0 = Yt.ap().rearrange("(u dd) (h a j) -> dd u h a j", u=16, h=4, a=32)
    sctV0 = sct.ap().rearrange("p (sl u j) -> p sl u j", sl=4, j=8)

    def _ystore(eng, m):
        W(eng, "s_vt", m + 1)
        pq[m % 2].inc(eng.dma_start(out=YV0[:, :, m % 4, m // 4, :],
                                    in_=sctV0[:, m % 4]), m // 2)

    @blk.sync
    def _(sync):
        I(sync.dma_start(out=sperm[:, :], in_=DR["perm_b"][:, :]), "s_k", 16)
        W(sync, "s_cc")
        for n in range(128):
            a, h = n // 4, n % 4
            if n >= 4:
                W(sync, "s_mt", n - 3)
            psc.inc(sync.dma_start(
                out=sld[:, (n % 4) * 128:(n % 4 + 1) * 128],
                in_=ctxR[128 * a:128 * (a + 1), 128 * h:128 * (h + 1)]), n)
            m = n - 4
            if m >= 0 and m % 2 == 1:
                _ystore(sync, m)
        for m in (125, 127):
            _ystore(sync, m)
        pq[1].wait(sync, 62)
        pq[1].wait(sync, 63)

    @blk.tensor
    def _(t):
        W(t, "s_k", 16)
        for n in range(128):
            psc.wait(t, n)
            if n >= 4:
                W(t, "s_vt", n - 3)
            I(t.matmul(PB[n % 4][:, 0:128],
                       sld[:, (n % 4) * 128:(n % 4 + 1) * 128],
                       sperm[:, :], start=True, stop=True), "s_mt", 1)

    @blk.vector
    def _(v):
        for n in range(128):
            W(v, "s_mt", n + 1)
            if n >= 4:
                pq[(n - 4) % 2].wait(v, (n - 4) // 2)
            v.tensor_copy(sct[:, (n % 4) * 128:(n % 4 + 1) * 128],
                          PB[n % 4][:, 0:128])
            v.drain()
            I(v.nop(), "s_vt", 1)

    @blk.scalar
    def _(s):
        for m in range(0, 128, 2):
            _ystore(s, m)
        for k in (60, 61, 62, 63):
            pq[0].wait(s, k)


def emit_C0(env, Yt, wproj, bias, xT, ffn, xo, DR):
    nc = env.nc

    sy = env.buf("sy", [128, NCH * NT], BF16)
    swt = env.buf("swt", [128, 4 * 128], BF16)
    sb = env.buf("sb", [128, 16])
    sx = env.buf("sx", [128, 2 * NT])
    sf = env.buf("sf", [128, 2 * NT])
    so = env.buf("so", [128, 2 * NT])
    PB = [env.PB(i) for i in range(8)]

    pwl = env.pdma("pwl4", 4)
    px = env.pdma("px")
    pf = env.pdma("pf")
    po = env.pdma("po")

    blk = env.blk
    I, W, cnt = env.I, env.W, env.cnt

    @blk.sync
    def _(sync):
        for j in range(NCH):
            I(sync.dma_start(out=sb[:, j:j + 1], in_=bias[128 * j:128 * (j + 1), :]),
              "s_k", 16)
        for j in range(NCH):
            I(sync.dma_start(out=sy[:, j * NT:(j + 1) * NT],
                             in_=Yt[128 * j:128 * (j + 1), :]), "s_ya", 16)
        for ot in range(16):
            if ot >= 2:
                W(sync, "s_vp", 2 * (ot - 1))
            sl = (ot % 2)
            px.inc(sync.dma_start(out=sx[:, sl * NT:(sl + 1) * NT],
                                  in_=xT[128 * ot:128 * (ot + 1), :]), ot)
            pf.inc(sync.dma_start(out=sf[:, sl * NT:(sl + 1) * NT],
                                  in_=ffn[128 * ot:128 * (ot + 1), :]), ot)

    @blk.gpsimd
    def _(gpsimd):
        for n in range(16 * 16):
            ot, fc = n // 16, n % 16
            if n >= 4:
                W(gpsimd, "s_mp", n - 3)
            pwl.inc(gpsimd.dma_start(
                out=swt[:, (n % 4) * 128:(n % 4 + 1) * 128],
                in_=wproj[128 * fc:128 * (fc + 1), 128 * ot:128 * (ot + 1)]), n)

    @blk.tensor
    def _(tensor):
        W(tensor, "s_ya", 16 * NCH)
        for ot in range(16):
            b0 = (ot % 4) * 2
            if ot >= 4:
                W(tensor, "s_vp", 2 * (ot - 3))
            for fc in range(NCH):
                n = ot * 16 + fc
                pwl.wait(tensor, n)
                sl = (n % 4) * 128
                for hf in range(2):
                    mm = tensor.matmul(PB[b0 + hf][:, :], swt[:, sl:sl + 128],
                                       sy[:, fc * NT + hf * 512:fc * NT + (hf + 1) * 512],
                                       start=(fc == 0), stop=(fc == NCH - 1))
                I(mm, "s_mp", 1)

    @blk.vector
    def _(vector):
        W(vector, "s_k", 16 * NCH)
        for ot in range(16):
            b0 = (ot % 4) * 2
            W(vector, "s_mp", (ot + 1) * NCH)
            px.wait(vector, ot)
            pf.wait(vector, ot)
            if ot >= 2:
                po.wait(vector, ot - 2)
            sl = (ot % 2)
            for hf in range(2):
                rng = slice(sl * NT + hf * 512, sl * NT + (hf + 1) * 512)
                vector.scalar_tensor_tensor(so[:, rng], PB[b0 + hf][:, :],
                                            sb[:, ot:ot + 1], sx[:, rng],
                                            ALU.add, ALU.add)
                vector.drain()
                vector.tensor_tensor(so[:, rng], so[:, rng], sf[:, rng], ALU.add)
                vector.drain()
            I(vector.nop(), "s_vp", 2)

    @blk.scalar
    def _(scalar):
        for ot in range(16):
            W(scalar, "s_vp", 2 * (ot + 1))
            po.inc(scalar.dma_start(out=xo[128 * ot:128 * (ot + 1), :],
                                    in_=so[:, (ot % 2) * NT:(ot % 2 + 1) * NT]), ot)
        po.wait(scalar, 14)
        po.wait(scalar, 15)


def build_fused(debug=False):
    env = Env(debug=debug)
    nc = env.nc

    def din_(name, shape, dt=F32):
        return nc.dram_tensor(name, shape, dt, kind="ExternalInput")

    xT0 = din_("xT0", [D, NT])
    w1t = din_("w1t", [D, INNER])
    w2t = din_("w2t", [INNER, D])
    gatet = din_("gatet", [D, INNER])
    upt = din_("upt", [D, INNER])
    downt = din_("downt", [INNER, D])
    sh_w1 = din_("sh_w1", [D // 8, INNER])
    sh_w2 = din_("sh_w2", [INNER // 8, D])
    sh_g = din_("sh_g", [D // 8, INNER])
    sh_u = din_("sh_u", [D // 8, INNER])
    sh_d = din_("sh_d", [INNER // 8, D])
    wqkv = din_("wqkv", [D, 3 * D])
    bqkv = din_("bqkv", [1, 3 * D])
    wproj = din_("wproj", [D, D])
    bproj = din_("bproj", [D, 1])
    wq_sl = din_("wq_sl", [D, 512])
    wk_sl = din_("wk_sl", [D, 512])
    wv_sl = din_("wv_sl", [D, 512])
    wo_sl = din_("wo_sl", [512, D])
    bands = din_("bands", [128, 4 * 512], BF16)
    DR = {
        "ones_f": din_("ones_f", [128, 1]),
        "ones_r": din_("ones_r", [1, 128]),
        "ones_fb": din_("ones_fb", [128, 1], BF16),
        "ones_rb": din_("ones_rb", [1, 128], BF16),
        "iden_f": din_("iden_f", [128, 128]),
        "iden_b": din_("iden_b", [128, 128], BF16),
        "perm_b": din_("perm_b", [128, 128], BF16),
    }
    xo = nc.dram_tensor("xo", [D, NT], F32, kind="ExternalOutput")

    DR["arin"] = nc.dram_tensor("arin", [1, 3], F32)
    DR["arout"] = nc.dram_tensor("arout", [1, 3], F32)
    DR["hscr"] = nc.dram_tensor("hscr", [INNER, NT], F32)
    DR["t2scr"] = nc.dram_tensor("t2scr", [INNER, NT], BF16)
    DR["xnf"] = nc.dram_tensor("xnf", [D, NT], F32)
    xn0 = nc.dram_tensor("xn0", [D, NT], BF16)
    ffn0 = nc.dram_tensor("ffn0", [D, NT], F32)
    Tp = nc.dram_tensor("Tp", [S, 512], BF16)
    Gc = nc.dram_tensor("Gc", [4 * S, 512], BF16)
    Tq = nc.dram_tensor("Tq", [4 * S, 512], BF16)
    ctxR = nc.dram_tensor("ctxR", [S, 512], BF16)
    Yt = nc.dram_tensor("Yt", [D, NT], BF16)
    x1 = nc.dram_tensor("x1", [D, NT], F32)
    xn1 = nc.dram_tensor("xn1", [D, NT], BF16)
    ffn1 = nc.dram_tensor("ffn1", [D, NT], F32)
    xst = nc.dram_tensor("xst", [4 * D, NT], BF16)
    Tpart = nc.dram_tensor("Tpart", [4 * D, NT], BF16)
    psl = nc.dram_tensor("psl", [D, NT], BF16)

    env.make()
    blk, I, W = env.blk, env.I, env.W
    sconst = env.st.enter_context(nc.sbuf_tensor("sconst", [128, 4], F32))
    env.sconst = sconst
    nc.const_aps.register(F32, -RC, sconst[:, 2:3])

    @blk.vector
    def _(v):
        v.memset(sconst[:, 0:1], 1.0)
        v.memset(sconst[:, 1:2], 1.0 + RC)
        v.memset(sconst[:, 2:3], -RC)
        v.memset(sconst[:, 3:4], 2.0)
        v.drain()
        I(v.nop(), "s_kc", 1)

    env.open_stage("a0")
    emit_A(env, 0, xT0, [w1t, w2t], [sh_w1, sh_w2], xn0, ffn0, DR)
    env.barrier()
    env.open_stage("b0")
    emit_B0(env, xn0, wqkv, bqkv, Tp, DR)
    env.barrier()
    env.open_stage("sc")
    emit_SC(env, Tp, Gc, Tq, ctxR, Yt, DR)
    env.barrier()
    env.open_stage("c0")
    emit_C0(env, Yt, wproj, bproj, xT0, ffn0, x1, DR)
    env.barrier()
    env.open_stage("a1")
    emit_A(env, 1, x1, [gatet, upt, downt], [sh_g, sh_u, sh_d], xn1, ffn1, DR)
    env.barrier()
    env.open_stage("gx")

    @blk.gpsimd
    def _(g):
        I(g.collective_compute("AllGather", ALU.bypass, replica_groups=G4,
                               ins=[xn1.ap().opt()], outs=[xst.ap().opt()]),
          "s_cc", 1)
        W(g, "s_cc")

    env.barrier()
    env.open_stage("b1")
    emit_B(env, 1, xst, wq_sl, wk_sl, wv_sl, wo_sl, bands, Tpart, DR)
    env.barrier()
    env.open_stage("pr")

    @blk.gpsimd
    def _(g):
        I(g.collective_compute("ReduceScatter", ALU.add, replica_groups=G4,
                               ins=[Tpart.ap().opt()], outs=[psl.ap().opt()]),
          "s_cc", 1)
        W(g, "s_cc")

    env.barrier()
    env.open_stage("c1")
    emit_C1(env, x1, ffn1, psl, xo)

    env.close()
    return env


from concourse.bass_utils import run_bass_kernel_spmd

_CACHE = {}


def kernel(**inputs):
    inputs = {k: np.asarray(v) for k, v in inputs.items()}
    sh, percore = host_prep(inputs)
    if "env" not in _CACHE:
        _CACHE["env"] = build_fused()
    nc = _CACHE["env"].nc
    bq = np.ascontiguousarray(
        np.asarray(inputs["L0_attn_bqkv"], np.float32).reshape(1, -1))
    wproj = np.ascontiguousarray(np.asarray(inputs["L0_attn_Wproj"], np.float32))
    ims = []
    for c in range(NCORES):
        pc = percore[c]
        ims.append({
            "xT0": pc["x0"].astype(np.float32),
            "w1t": sh["l0_w1t"], "w2t": sh["l0_w2t"],
            "gatet": sh["l1_gatet"], "upt": sh["l1_upt"],
            "downt": sh["l1_downt"],
            "sh_w1": pc["sh_l0w1"], "sh_w2": pc["sh_l0w2"],
            "sh_g": pc["sh_l1g"], "sh_u": pc["sh_l1u"], "sh_d": pc["sh_l1d"],
            "wqkv": sh["l0_wqkv"], "bqkv": bq,
            "wproj": wproj, "bproj": sh["l0_bproj"],
            "wq_sl": pc["l1_wq_sl"], "wk_sl": pc["l1_wk_sl"],
            "wv_sl": pc["l1_wv_sl"], "wo_sl": pc["l1_wo_sl"],
            "bands": sh["bands"],
            "ones_f": sh["ones_f"], "ones_r": sh["ones_r"],
            "ones_fb": np.ones((128, 1), bf16), "ones_rb": sh["ones_rb"],
            "iden_f": np.eye(128, dtype=np.float32), "iden_b": sh["iden_b"],
            "perm_b": sh["perm_b"],
        })
    r = run_bass_kernel_spmd(nc, ims, core_ids=list(range(NCORES)))
    out = np.empty((B, S, D), np.float32)
    for c in range(NCORES):
        s_, r_ = c // 4, c % 4
        out[s_, 1024 * r_:1024 * (r_ + 1), :] = np.asarray(r.results[c]["xo"]).T
    return out
